# version 09
# baseline (speedup 1.0000x reference)
import numpy as np
import ml_dtypes
from contextlib import ExitStack

import concourse.bass as bass
import concourse.mybir as mybir
from concourse.bass_utils import run_bass_kernel_spmd

BF16 = mybir.dt.bfloat16
F32 = mybir.dt.float32
I32 = mybir.dt.int32
AF = mybir.ActivationFunctionType
ALU = mybir.AluOpType
AX = mybir.AxisListType
bfloat16 = ml_dtypes.bfloat16

NCORES = 8
B = 2
S = 4096
D = 4096
HD = 128
TPC = 1024
NCH = D // 128
NKT = (20, 32)
NT = NKT[0] + NKT[1]
EXP_SHIFT = -30.0
PAD_BIAS = -1.0e9
LN_EPS = 1e-5


def core_chunks(j):
    q = j % 4
    return q, 7 - q


def core_tokens(j):
    c0, c1 = core_chunks(j)
    return np.concatenate([np.arange(512 * c0, 512 * (c0 + 1)),
                           np.arange(512 * c1, 512 * (c1 + 1))])


def colchunk(Weff):
    I, O = Weff.shape
    return np.ascontiguousarray(
        Weff.reshape(I // 128, 128, O // 128, 128).transpose(2, 1, 0, 3)
        .reshape(O // 128, 128, I))


def rope_tables_l0(n):
    inv = 1.0 / (10000.0 ** (np.arange(0, HD, 2, dtype=np.float32) / HD))
    fr = np.arange(n, dtype=np.float32)[:, None] * inv[None, :]
    emb = np.concatenate([fr, fr], -1)
    return np.cos(emb).astype(np.float32), np.sin(emb).astype(np.float32)


def rope_tables_l1(n):
    TH, FA, HI, LO, OM = 500000.0, 8.0, 4.0, 1.0, 8192
    inv = 1.0 / TH ** (np.arange(0, HD, 2, dtype=np.float64) / HD)
    wl = 2 * np.pi / inv
    sm = (OM / wl - LO) / (HI - LO)
    sc = np.where(wl > OM / LO, inv / FA, inv)
    mid = (wl <= OM / LO) & (wl >= OM / HI)
    sc = np.where(mid, (1 - sm) * inv / FA + sm * inv, sc).astype(np.float32)
    fr = np.arange(n, dtype=np.float32)[:, None] * sc[None, :]
    emb = np.concatenate([fr, fr], -1)
    return np.cos(emb).astype(np.float32), np.sin(emb).astype(np.float32)


def rope_fm(cos, sin, toks):
    c = np.ascontiguousarray(cos[toks, :].T)
    s = sin[toks, :].T.copy()
    s[0:64, :] *= -1.0
    return c, np.ascontiguousarray(s)


def diag_patterns():
    pats = np.zeros((4, 128, 512), np.float32)
    r = np.arange(128)[:, None]
    cc = np.arange(128)[None, :]
    tri = (r <= cc).astype(np.float32)
    for v in range(4):
        pats[v, :, 128 * v:128 * (v + 1)] = tri
        pats[v, :, 128 * (v + 1):] = 1.0
    return np.ascontiguousarray(pats.astype(bfloat16))


def pack_kv(core_j, full_fm):
    c0, c1 = core_chunks(core_j)
    Hkv = full_fm.shape[0]
    out = np.zeros((Hkv, 128, 128 * NT), bfloat16)
    bias = np.full(NT, PAD_BIAS, np.float32)
    for qb, ch in ((0, c0), (1, c1)):
        base = 0 if qb == 0 else NKT[0]
        nfull = 4 * ch
        out[:, :, 128 * base:128 * (base + nfull)] = full_fm[:, :, :128 * nfull]
        bias[base:base + nfull] = EXP_SHIFT
        dbase = base + NKT[qb] - 4
        out[:, :, 128 * dbase:128 * (dbase + 4)] = \
            full_fm[:, :, 128 * nfull:128 * (nfull + 4)]
        bias[dbase:dbase + 4] = EXP_SHIFT
    return out, bias



class Ctx:
    def __init__(self):
        self.nc = bass.Bass("TRN2", target_bir_lowering=False)
        self.es = ExitStack()
        self.cnt = {}

    def sbuf(self, name, shape, dt):
        return self.es.enter_context(self.nc.sbuf_tensor(name, shape, dt))

    def psum(self, name, shape, dt):
        return self.es.enter_context(self.nc.psum_tensor(name, shape, dt))

    def sem(self, name):
        h = self.es.enter_context(self.nc.semaphore(name))
        self.cnt[id(h)] = 0
        return h

    def inc(self, inst, semh, n=1):
        self.cnt[id(semh)] += n
        inst.then_inc(semh, n)
        return self.cnt[id(semh)]

    def val(self, semh):
        return self.cnt[id(semh)]


def bfview(t, cols):
    return t[:].bitcast(BF16)[:, 1:2 * cols:2]



def build_qkv(layer, only=None):
    nkv = 32 if layer == 0 else 8
    c = Ctx()
    nc = c.nc
    x_in = nc.dram_tensor("x", [NCH, 128, TPC], F32 if layer == 0 else BF16,
                          kind="ExternalInput").ap()
    wq_d = nc.dram_tensor("wq", [32, 128, D], F32, kind="ExternalInput").ap()
    wk_d = nc.dram_tensor("wk", [nkv, 128, D], F32, kind="ExternalInput").ap()
    wv_d = nc.dram_tensor("wv", [nkv, 128, D], F32, kind="ExternalInput").ap()
    cos_d = nc.dram_tensor("cos", [128, TPC], F32, kind="ExternalInput").ap()
    sin_d = nc.dram_tensor("sinS", [128, TPC], F32, kind="ExternalInput").ap()
    id_d = nc.dram_tensor("ident", [128, 128], BF16, kind="ExternalInput").ap()
    q_o = nc.dram_tensor("q", [32, 128, TPC], BF16, kind="ExternalOutput").ap()
    k_o = nc.dram_tensor("k", [nkv, 128, TPC], BF16, kind="ExternalOutput").ap()
    v_o = nc.dram_tensor("v", [nkv, 128, TPC], BF16, kind="ExternalOutput").ap()
    xs_o = None
    if layer == 0:
        xs_o = nc.dram_tensor("xs", [NCH, 128, TPC], BF16, kind="ExternalOutput").ap()

    projs = [("v", wv_d, nkv), ("k", wk_d, nkv), ("q", wq_d, 32)]
    if only is not None:
        projs = [p for p in projs if p[0] in only]

    with c.es:
        xs = c.sbuf("xs_s", [128, NCH * TPC], BF16)
        xl = c.sbuf("xl", [128, 2 * TPC], F32)
        wbuf = [c.sbuf("w0", [128, D], F32), c.sbuf("w1", [128, D], F32)]
        cos_s = c.sbuf("cos_s", [128, TPC], F32)
        sin_s = c.sbuf("sin_s", [128, TPC], F32)
        id_s = c.sbuf("id_s", [128, 128], BF16)
        tmp = c.sbuf("tmp", [128, 512], F32)
        u = c.sbuf("u", [128, 512], F32)
        qk_out = [c.sbuf("qo%d" % i, [128, TPC], BF16) for i in range(2)]
        vfm = [c.sbuf("vf%d" % i, [128, TPC], BF16) for i in range(2)]
        vtm = [c.sbuf("vt%d" % i, [128, TPC], BF16) for i in range(2)]
        ps = [c.psum("ps%d" % i, [128, 512], F32) for i in range(4)]
        ps_tr = [c.psum("ps_tr%d" % i, [128, 1024], BF16) for i in range(2)]

        ld = c.sem("ld")
        ldx = c.sem("ldx")
        wld = c.sem("wld")
        wfree = c.sem("wfree")
        mmd = c.sem("mmd")
        pev = c.sem("pev")
        xca = c.sem("xca")
        trd = c.sem("trd")
        tre = c.sem("tre")
        sto = c.sem("sto")
        svx = c.sem("svx")

        blk = c.es.enter_context(nc.Block())

        n_xs_st = NCH if layer == 0 else 0
        store_base = {}
        _si = n_xs_st
        for _pn, _wd, _noc in projs:
            store_base[_pn] = _si
            _si += _noc

        @blk.sync
        def _(sy):
            sy.dma_start(out=cos_s[:], in_=cos_d).then_inc(ld, 16)
            sy.dma_start(out=sin_s[:], in_=sin_d).then_inc(ld, 16)
            sy.dma_start(out=id_s[:], in_=id_d).then_inc(ld, 16)
            for ic in range(NCH):
                if layer == 0:
                    if ic >= 2:
                        sy.wait_ge(xca, ic - 1)
                    sy.dma_start(out=xl[:, (ic % 2) * TPC:(ic % 2 + 1) * TPC],
                                 in_=x_in[ic]).then_inc(ldx, 16)
                else:
                    sy.dma_start(out=xs[:, ic * TPC:(ic + 1) * TPC],
                                 in_=x_in[ic]).then_inc(ldx, 16)
            si = 0
            for _pn, wd, noc in projs:
                for oc in range(noc):
                    if si >= 2:
                        sy.wait_ge(pev, 2 * si - 2)
                    sy.dma_start(out=wbuf[si % 2][:], in_=wd[oc]).then_inc(wld, 16)
                    si += 1

        if layer == 0:
            @blk.vector
            def _(v):
                for ic in range(NCH):
                    v.wait_ge(ldx, 16 * (ic + 1))
                    c.inc(v.tensor_copy(out=xs[:, ic * TPC:(ic + 1) * TPC],
                                        in_=xl[:, (ic % 2) * TPC:(ic % 2 + 1) * TPC]),
                          xca)

        @blk.tensor
        def _(t):
            if layer == 0:
                t.wait_ge(xca, NCH)
            else:
                t.wait_ge(ldx, 16 * NCH)
            si = 0
            g = 0

            def v_transpose(t, oc):
                t.wait_ge(pev, 2 * oc + 2)
                if oc >= 2:
                    t.wait_ge(tre, oc - 1)
                tr = None
                for tt in range(8):
                    tr = t.transpose(out=ps_tr[oc % 2][:, 128 * tt:128 * (tt + 1)],
                                     in_=vfm[oc % 2][:, 128 * tt:128 * (tt + 1)],
                                     identity=id_s[:])
                c.inc(tr, trd)

            for _pn, wd, noc in projs:
                for oc in range(noc):
                    t.wait_ge(wld, 16 * (si + 1))
                    wv_ = bfview(wbuf[si % 2], D)
                    for tb in range(2):
                        if g >= 4:
                            t.wait_ge(pev, g - 3)
                        bank = ps[g % 4]
                        mm = None
                        for ic in range(NCH):
                            mm = t.matmul(out=bank[:],
                                          lhsT=wv_[:, 128 * ic:128 * (ic + 1)],
                                          rhs=xs[:, ic * TPC + 512 * tb:
                                                 ic * TPC + 512 * tb + 512],
                                          start=(ic == 0), stop=(ic == NCH - 1))
                        c.inc(mm, mmd)
                        g += 1
                    si += 1
                    if _pn == "v" and oc >= 2:
                        v_transpose(t, oc - 2)
                if _pn == "v":
                    v_transpose(t, nkv - 2)
                    v_transpose(t, nkv - 1)

        @blk.vector
        def _(v):
            v.wait_ge(ld, 48)
            g = 0

            def vtm_evac(v, oc):
                v.wait_ge(trd, oc + 1)
                if oc >= 2:
                    v.wait_ge(sto, 16 * (store_base["v"] + oc - 1))
                c.inc(v.tensor_copy(out=vtm[oc % 2][:], in_=ps_tr[oc % 2][:]), tre)

            if any(p[0] == "v" for p in projs):
                for oc in range(nkv):
                    for tb in range(2):
                        v.wait_ge(mmd, g + 1)
                        bank = ps[g % 4]
                        cols = slice(512 * tb, 512 * tb + 512)
                        if oc >= 2 and tb == 0:
                            v.wait_ge(trd, oc - 1)
                        c.inc(v.tensor_copy(out=vfm[oc % 2][:, cols],
                                            in_=bank[:]), pev)
                        g += 1
                    if oc >= 2:
                        vtm_evac(v, oc - 2)
                vtm_evac(v, nkv - 2)
                vtm_evac(v, nkv - 1)
            for pn, noc in [(p[0], p[2]) for p in projs if p[0] != "v"]:
                for oc in range(noc):
                    for tb in range(2):
                        v.wait_ge(mmd, g + 1)
                        bank = ps[g % 4]
                        cols = slice(512 * tb, 512 * tb + 512)
                        if tb == 0 and oc >= 2:
                            v.wait_ge(sto, 16 * (store_base[pn] + oc - 1))
                        i1 = c.inc(v.tensor_tensor(
                            out=tmp[0:64, :], in0=bank[64:128, :],
                            in1=sin_s[0:64, cols], op=ALU.mult), svx)
                        i2 = c.inc(v.tensor_tensor(
                            out=tmp[64:128, :], in0=bank[0:64, :],
                            in1=sin_s[64:128, cols], op=ALU.mult), svx)
                        i3 = c.inc(v.tensor_tensor(
                            out=u[:], in0=bank[:], in1=cos_s[:, cols],
                            op=ALU.mult), svx)
                        v.wait_ge(svx, i3)
                        c.inc(v.tensor_tensor(
                            out=qk_out[oc % 2][:, cols], in0=u[:],
                            in1=tmp[:], op=ALU.add), pev)
                        g += 1

        @blk.gpsimd
        def _(gp):
            nst = 0
            if layer == 0:
                for ic in range(NCH):
                    gp.wait_ge(xca, ic + 1)
                    gp.dma_start(out=xs_o[ic],
                                 in_=xs[:, ic * TPC:(ic + 1) * TPC]).then_inc(sto, 16)
                    nst += 1
            base_g = 0
            for pn, _wd, noc in projs:
                if pn == "v":
                    for oc in range(nkv):
                        gp.wait_ge(tre, oc + 1)
                        gp.dma_start(out=v_o[oc], in_=vtm[oc % 2][:]).then_inc(sto, 16)
                        nst += 1
                else:
                    out_d = k_o if pn == "k" else q_o
                    for oc in range(noc):
                        gp.wait_ge(pev, base_g + 2 * oc + 2)
                        gp.dma_start(out=out_d[oc], in_=qk_out[oc % 2][:]).then_inc(sto, 16)
                        nst += 1
                base_g += 2 * noc
            gp.wait_ge(sto, 16 * nst)

    return nc



def build_attn(layer):
    nkv = 32 if layer == 0 else 8
    rep = 32 // nkv
    c = Ctx()
    nc = c.nc
    q_in = nc.dram_tensor("q", [32, 128, TPC], BF16, kind="ExternalInput").ap()
    k_in = nc.dram_tensor("kp", [nkv, 128, 128 * NT], BF16, kind="ExternalInput").ap()
    v_in = nc.dram_tensor("vp", [nkv, 128, 128 * NT], BF16, kind="ExternalInput").ap()
    bias_in = nc.dram_tensor("ebias", [128, NT], F32, kind="ExternalInput").ap()
    dpat_in = nc.dram_tensor("dpat", [4, 128, 512], BF16, kind="ExternalInput").ap()
    ones_in = nc.dram_tensor("onesb", [128, 128], BF16, kind="ExternalInput").ap()
    a_o = nc.dram_tensor("attn", [32, 128, TPC], BF16, kind="ExternalOutput").ap()

    def kt_seq():
        gkt = 0
        for h in range(32):
            for qb in range(2):
                for kt in range(NKT[qb]):
                    yield h, qb, kt, gkt, 2 * h + qb
                    gkt += 1

    with c.es:
        q_s = [c.sbuf("q%d" % i, [128, TPC], BF16) for i in range(2)]
        k_s = [c.sbuf("k%d" % i, [128, 128 * NT], BF16) for i in range(2)]
        v_s = [c.sbuf("v%d" % i, [128, 128 * NT], BF16) for i in range(2)]
        pt = [c.sbuf("pt%d" % i, [128, 512], BF16) for i in range(4)]
        bias_s = c.sbuf("bias_s", [128, NT], F32)
        dpat_s = c.sbuf("dpat_s", [128, 4 * 512], BF16)
        ones_s = c.sbuf("ones_s", [128, 128], BF16)
        rcp_s = c.sbuf("rcp_s", [128, 512], F32)
        ao_s = [c.sbuf("ao%d" % i, [128, TPC], BF16) for i in range(2)]
        ps_s = [c.psum("pss%d" % i, [128, 512], F32) for i in range(2)]
        ps_o = [c.psum("pso%d" % i, [128, 512], F32) for i in range(2)]
        ps_z = [c.psum("psz%d" % i, [128, 512], F32) for i in range(2)]

        ld = c.sem("ld")
        qld = c.sem("qld")
        kvld = c.sem("kvld")
        smm = c.sem("smm")
        sexp = c.sem("sexp")
        sdg = c.sem("sdg")
        spv = c.sem("spv")
        sev = c.sem("sev")
        sdone = c.sem("sdone")
        svx = c.sem("svx")

        blk = c.es.enter_context(nc.Block())

        @blk.sync
        def _(sy):
            sy.dma_start(out=bias_s[:], in_=bias_in).then_inc(ld, 16)
            for a_ in range(4):
                sy.dma_start(out=dpat_s[:, 512 * a_:512 * (a_ + 1)],
                             in_=dpat_in[a_]).then_inc(ld, 16)
            sy.dma_start(out=ones_s[:], in_=ones_in).then_inc(ld, 16)
            for h in range(32):
                if h >= 2:
                    sy.wait_ge(sev, 2 * (h - 1))
                sy.dma_start(out=q_s[h % 2][:], in_=q_in[h]).then_inc(qld, 16)
                if h % rep == 0:
                    kv = h // rep
                    if kv >= 2:
                        sy.wait_ge(sev, 2 * rep * (kv - 1))
                    sy.dma_start(out=k_s[kv % 2][:], in_=k_in[kv]).then_inc(kvld, 16)
                    sy.dma_start(out=v_s[kv % 2][:], in_=v_in[kv]).then_inc(kvld, 16)

        @blk.tensor
        def _(t):
            t.wait_ge(ld, 96)
            for h in range(32):
                t.wait_ge(qld, 16 * (h + 1))
                t.wait_ge(kvld, 32 * (h // rep + 1))
                kbuf = k_s[(h // rep) % 2]
                vbuf = v_s[(h // rep) % 2]
                for qb in range(2):
                    nkt = NKT[qb]
                    base = 0 if qb == 0 else NKT[0]
                    g2 = 2 * h + qb
                    gkt0 = 52 * h + base
                    qcols = slice(512 * qb, 512 * qb + 512)
                    if g2 >= 2:
                        t.wait_ge(sev, g2 - 1)

                    def scores(kt):
                        gkt = gkt0 + kt
                        if gkt >= 2:
                            t.wait_ge(sexp, gkt - 1)
                        c.inc(t.matmul(
                            out=ps_s[gkt % 2][:],
                            lhsT=kbuf[:, 128 * (base + kt):128 * (base + kt + 1)],
                            rhs=q_s[h % 2][:, qcols], start=True, stop=True), smm)

                    def pv(kt):
                        gkt = gkt0 + kt
                        if kt >= nkt - 4:
                            t.wait_ge(sdg, 4 * g2 + (kt - (nkt - 4)) + 1)
                        else:
                            t.wait_ge(sexp, gkt + 1)
                        ptile = pt[(base + kt) % 4]
                        t.matmul(out=ps_o[g2 % 2][:],
                                 lhsT=vbuf[:, 128 * (base + kt):128 * (base + kt + 1)],
                                 rhs=ptile[:], start=(kt == 0), stop=(kt == nkt - 1))
                        c.inc(t.matmul(out=ps_z[g2 % 2][:], lhsT=ones_s[:],
                                       rhs=ptile[:], start=(kt == 0),
                                       stop=(kt == nkt - 1)), spv)

                    scores(0)
                    for kt in range(1, nkt):
                        scores(kt)
                        pv(kt - 1)
                    pv(nkt - 1)

        @blk.scalar
        def _(sc):
            sc.wait_ge(ld, 96)
            for h, qb, kt, gkt, g2 in kt_seq():
                base = 0 if qb == 0 else NKT[0]
                sc.wait_ge(smm, gkt + 1)
                if gkt >= 4:
                    sc.wait_ge(spv, gkt - 3)
                c.inc(sc.activation(out=pt[(base + kt) % 4][:],
                                    in_=ps_s[gkt % 2][:], func=AF.Exp,
                                    bias=bias_s[:, base + kt:base + kt + 1],
                                    scale=float(HD) ** -0.5), sexp)

        @blk.vector
        def _(v):
            v.wait_ge(ld, 96)
            for h in range(32):
                if h >= 2:
                    v.wait_ge(sdone, 16 * (h - 1))
                for qb in range(2):
                    nkt = NKT[qb]
                    base = 0 if qb == 0 else NKT[0]
                    g2 = 2 * h + qb
                    gkt0 = 52 * h + base
                    for kt in range(nkt - 4, nkt):
                        vv = kt - (nkt - 4)
                        v.wait_ge(sexp, gkt0 + kt + 1)
                        ptile = pt[(base + kt) % 4]
                        c.inc(v.tensor_tensor(
                            out=ptile[:], in0=ptile[:],
                            in1=dpat_s[:, 512 * vv:512 * (vv + 1)],
                            op=ALU.mult), sdg)
                    v.wait_ge(spv, gkt0 + nkt)
                    i1 = c.inc(v.reciprocal(out=rcp_s[:], in_=ps_z[g2 % 2][:]), svx)
                    v.wait_ge(svx, i1)
                    c.inc(v.tensor_tensor(
                        out=ao_s[h % 2][:, 512 * qb:512 * qb + 512],
                        in0=ps_o[g2 % 2][:], in1=rcp_s[:], op=ALU.mult), sev)

        @blk.gpsimd
        def _(gp):
            for h in range(32):
                gp.wait_ge(sev, 2 * h + 2)
                gp.dma_start(out=a_o[h], in_=ao_s[h % 2][:]).then_inc(sdone, 16)
            gp.wait_ge(sdone, 16 * 32)

    return nc



def build_proj_ln(mode, final=False):
    c = Ctx()
    nc = c.nc
    wdt = I32 if mode == "qffn" else F32
    x_in = nc.dram_tensor("xin", [NCH, 128, TPC], BF16, kind="ExternalInput").ap()
    xr_in = nc.dram_tensor("xres", [NCH, 128, TPC], BF16, kind="ExternalInput").ap()
    w_d = nc.dram_tensor("w", [32, 128, D], wdt, kind="ExternalInput").ap()
    ob_d = nc.dram_tensor("onesb", [128, 1], BF16, kind="ExternalInput").ap()
    or_d = nc.dram_tensor("onesr", [1, 128], F32, kind="ExternalInput").ap()
    eps_d = nc.dram_tensor("epsv", [1, 1], F32, kind="ExternalInput").ap()
    if mode == "qffn":
        sc_d = nc.dram_tensor("scales", [128, 32], F32, kind="ExternalInput").ap()
        bi_d = nc.dram_tensor("fbias", [128, 32], F32, kind="ExternalInput").ap()
    out_dt = F32 if final else BF16
    x_o = nc.dram_tensor("xout", [NCH, 128, TPC], out_dt, kind="ExternalOutput").ap()

    with c.es:
        xs = c.sbuf("xs_s", [128, NCH * TPC], BF16)
        xr = c.sbuf("xr_s", [128, NCH * TPC], BF16)
        wbuf = [c.sbuf("w%d" % i, [128, D], wdt) for i in range(2)]
        wb16 = [c.sbuf("wb%d" % i, [128, D], BF16) for i in range(2)] \
            if mode == "qffn" else None
        ones_b = c.sbuf("ones_b", [128, 1], BF16)
        ones_r = c.sbuf("ones_r", [1, 128], F32)
        eps_s = c.sbuf("eps_s", [1, 1], F32)
        sq = [c.sbuf("sq%d" % i, [128, 512], BF16) for i in range(2)]
        u = c.sbuf("u", [128, 512], F32)
        uo = [c.sbuf("uo%d" % i, [128, TPC], out_dt) for i in range(2)]
        r1 = c.sbuf("r1", [1, 512], F32)
        r2 = c.sbuf("r2", [1, 512], F32)
        r3 = c.sbuf("r3", [1, 512], F32)
        if mode == "qffn":
            sc_s = c.sbuf("sc_s", [128, 32], F32)
            bi_s = c.sbuf("bi_s", [128, 32], F32)
        ps = [c.psum("ps%d" % i, [128, 512], F32) for i in range(4)]
        ps_mu = c.psum("ps_mu", [128, 512], F32)
        ps_m2 = c.psum("ps_m2", [128, 512], F32)

        ld = c.sem("ld")
        wld = c.sem("wld")
        wfree = c.sem("wfree")
        wcst = c.sem("wcst")
        mmd = c.sem("mmd")
        pev = c.sem("pev")
        sqd = c.sem("sqd")
        sm2 = c.sem("sm2")
        stat = c.sem("stat")
        rowv = c.sem("rowv")
        ssx = c.sem("ssx")
        rowd = c.sem("rowd")
        bcd = c.sem("bcd")
        appd = c.sem("appd")
        sto = c.sem("sto")
        svx = c.sem("svx")

        blk = c.es.enter_context(nc.Block())
        n_misc = 3 + (2 if mode == "qffn" else 0)

        @blk.sync
        def _(sy):
            sy.dma_start(out=ones_b[:], in_=ob_d).then_inc(ld, 16)
            sy.dma_start(out=ones_r[:], in_=or_d).then_inc(ld, 16)
            sy.dma_start(out=eps_s[:], in_=eps_d).then_inc(ld, 16)
            if mode == "qffn":
                sy.dma_start(out=sc_s[:], in_=sc_d).then_inc(ld, 16)
                sy.dma_start(out=bi_s[:], in_=bi_d).then_inc(ld, 16)
            for ic in range(NCH):
                sy.dma_start(out=xs[:, ic * TPC:(ic + 1) * TPC],
                             in_=x_in[ic]).then_inc(ld, 16)
            for ic in range(NCH):
                sy.dma_start(out=xr[:, ic * TPC:(ic + 1) * TPC],
                             in_=xr_in[ic]).then_inc(ld, 16)
            for oc in range(32):
                if oc >= 2:
                    if mode == "qffn":
                        sy.wait_ge(wcst, oc - 1)
                    else:
                        sy.wait_ge(pev, 2 * oc - 2)
                sy.dma_start(out=wbuf[oc % 2][:], in_=w_d[oc]).then_inc(wld, 16)

        ld_all = 16 * (n_misc + 2 * NCH)


        @blk.tensor
        def _(t):
            t.wait_ge(ld, ld_all)
            for oc in range(32):
                if mode == "qffn":
                    t.wait_ge(wcst, oc + 1)
                    wv_ = wb16[oc % 2][:]
                else:
                    t.wait_ge(wld, 16 * (oc + 1))
                    wv_ = bfview(wbuf[oc % 2], D)
                for tb in range(2):
                    g = 2 * oc + tb
                    if g >= 4:
                        t.wait_ge(pev, g - 3)
                    bank = ps[g % 4]
                    mm = None
                    for ic in range(NCH):
                        mm = t.matmul(out=bank[:],
                                      lhsT=wv_[:, 128 * ic:128 * (ic + 1)],
                                      rhs=xs[:, ic * TPC + 512 * tb:
                                             ic * TPC + 512 * tb + 512],
                                      start=(ic == 0), stop=(ic == NCH - 1))
                    c.inc(mm, mmd)
            t.wait_ge(pev, 64)
            for tb in range(2):
                if tb == 1:
                    t.wait_ge(rowd, 2)
                for ic in range(NCH):
                    t.matmul(out=ps_mu[0:1, :], lhsT=ones_b[:],
                             rhs=xr[:, ic * TPC + 512 * tb:ic * TPC + 512 * tb + 512],
                             start=(ic == 0), stop=(ic == NCH - 1))
                for ic in range(NCH):
                    k = NCH * tb + ic
                    t.wait_ge(sqd, k + 1)
                    c.inc(t.matmul(out=ps_m2[0:1, :], lhsT=ones_b[:],
                                   rhs=sq[ic % 2][:],
                                   start=(ic == 0), stop=(ic == NCH - 1)), sm2)
                t.wait_ge(rowd, 2 * tb + 2)
                t.matmul(out=ps[2 * tb][:], lhsT=ones_r[:], rhs=r1[:],
                         start=True, stop=True)
                c.inc(t.matmul(out=ps[2 * tb + 1][:], lhsT=ones_r[:], rhs=r3[:],
                               start=True, stop=True), bcd)

        @blk.scalar
        def _(sc):
            sc.wait_ge(ld, ld_all)
            sc.wait_ge(pev, 64)
            for tb in range(2):
                for ic in range(NCH):
                    k = NCH * tb + ic
                    if k >= 2:
                        sc.wait_ge(sm2, k - 1)
                    c.inc(sc.activation(out=sq[ic % 2][:],
                                        in_=xr[:, ic * TPC + 512 * tb:
                                               ic * TPC + 512 * tb + 512],
                                        func=AF.Square), sqd)
                sc.wait_ge(rowv, tb + 1)
                c.inc(sc.activation(out=r3[:], in_=r2[:], func=AF.Sqrt,
                                    bias=eps_s[0:1, :], scale=1.0), ssx)

        @blk.vector
        def _(v):
            if mode == "qffn":
                for oc in range(2):
                    v.wait_ge(wld, 16 * (oc + 1))
                    c.inc(v.tensor_copy(out=wb16[oc % 2][:], in_=wbuf[oc % 2][:]),
                          wcst)
            for oc in range(32):
                for tb in range(2):
                    g = 2 * oc + tb
                    v.wait_ge(mmd, g + 1)
                    bank = ps[g % 4]
                    cols = slice(oc * TPC + 512 * tb, oc * TPC + 512 * tb + 512)
                    if mode == "qffn":
                        i1 = c.inc(v.tensor_scalar(
                            out=u[:], in0=bank[:], scalar1=sc_s[:, oc:oc + 1],
                            scalar2=bi_s[:, oc:oc + 1], op0=ALU.mult,
                            op1=ALU.add), svx)
                        v.wait_ge(svx, i1)
                        c.inc(v.tensor_tensor(out=xr[:, cols], in0=u[:],
                                              in1=xr[:, cols], op=ALU.add), pev)
                    else:
                        c.inc(v.tensor_tensor(out=xr[:, cols], in0=bank[:],
                                              in1=xr[:, cols], op=ALU.add), pev)
                if mode == "qffn" and oc + 2 < 32:
                    v.wait_ge(wld, 16 * (oc + 3))
                    c.inc(v.tensor_copy(out=wb16[oc % 2][:], in_=wbuf[oc % 2][:]),
                          wcst)
            for tb in range(2):
                v.wait_ge(sm2, NCH * (tb + 1))
                i1 = c.inc(v.tensor_scalar(out=r1[:], in0=ps_mu[0:1, :],
                                           scalar1=-1.0 / D, scalar2=None,
                                           op0=ALU.mult), svx)
                i2 = c.inc(v.tensor_scalar(out=r2[:], in0=ps_m2[0:1, :],
                                           scalar1=1.0 / D, scalar2=None,
                                           op0=ALU.mult), svx)
                v.wait_ge(svx, i2)
                i3 = c.inc(v.tensor_tensor(out=r3[:], in0=r1[:], in1=r1[:],
                                           op=ALU.mult), svx)
                v.wait_ge(svx, i3)
                i4 = c.inc(v.scalar_tensor_tensor(out=r2[:], in0=r3[:],
                                                  scalar=-1.0, in1=r2[:],
                                                  op0=ALU.mult, op1=ALU.add), svx)
                v.wait_ge(svx, i4)
                c.inc(v.tensor_copy(out=r2[:], in_=r2[:]), rowv)
                v.wait_ge(ssx, tb + 1)
                i5 = c.inc(v.reciprocal(out=r3[:], in_=r3[:]), svx)
                v.wait_ge(svx, i5)
                c.inc(v.tensor_copy(out=r1[:], in_=r1[:]), rowd)
                c.inc(v.tensor_copy(out=r3[:], in_=r3[:]), rowd)
            v.wait_ge(bcd, 2)
            for ic in range(NCH):
                if ic >= 2:
                    v.wait_ge(sto, 16 * (ic - 1))
                for tb in range(2):
                    cols = slice(ic * TPC + 512 * tb, ic * TPC + 512 * tb + 512)
                    j1 = c.inc(v.tensor_tensor(out=u[:], in0=xr[:, cols],
                                               in1=ps[2 * tb][:], op=ALU.add), svx)
                    v.wait_ge(svx, j1)
                    last = v.tensor_tensor(out=uo[ic % 2][:, 512 * tb:512 * tb + 512],
                                           in0=u[:], in1=ps[2 * tb + 1][:],
                                           op=ALU.mult)
                    if tb == 1:
                        c.inc(last, appd)

        @blk.gpsimd
        def _(gp):
            for ic in range(NCH):
                gp.wait_ge(appd, ic + 1)
                gp.dma_start(out=x_o[ic], in_=uo[ic % 2][:]).then_inc(sto, 16)
            gp.wait_ge(sto, 16 * NCH)

    return nc



def build_ffn1(debug=False, lora=True, loraonly=False):
    c = Ctx()
    nc = c.nc
    x_in = nc.dram_tensor("xin", [NCH, 128, TPC], BF16, kind="ExternalInput").ap()
    w_d = nc.dram_tensor("w", [32, 128, D], F32, kind="ExternalInput").ap()
    fb_d = nc.dram_tensor("fbias", [128, 32], F32, kind="ExternalInput").ap()
    wr_d = nc.dram_tensor("wr", [128, 256], F32, kind="ExternalInput").ap()
    a_d = nc.dram_tensor("la", [128, 4096], F32, kind="ExternalInput").ap()
    b_d = nc.dram_tensor("lb", [128, 4096], F32, kind="ExternalInput").ap()
    d2_d = nc.dram_tensor("d2", [8, 128], F32, kind="ExternalInput").ap()
    idf_d = nc.dram_tensor("idf", [128, 128], F32, kind="ExternalInput").ap()
    ob_d = nc.dram_tensor("onesb", [128, 1], BF16, kind="ExternalInput").ap()
    or_d = nc.dram_tensor("onesr", [1, 128], F32, kind="ExternalInput").ap()
    eps_d = nc.dram_tensor("epsv", [1, 1], F32, kind="ExternalInput").ap()
    x_o = nc.dram_tensor("xout", [NCH, 128, TPC], F32, kind="ExternalOutput").ap()
    dbg_o = {}
    if debug:
        for n, shp in [("d_e", [128, 64]), ("d_wts", [128, 64]), ("d_wt", [8, TPC]),
                       ("d_wb", [128, TPC]), ("d_g", [128, TPC]), ("d_lo", [128, 512]),
                       ("d_bb", [128, 2048])]:
            dbg_o[n] = nc.dram_tensor(n, shp, F32, kind="ExternalOutput").ap()

    with c.es:
        xs = c.sbuf("xs_s", [128, NCH * TPC], BF16)
        xr = c.sbuf("xr_s", [128, NCH * TPC], BF16)
        wbuf = [c.sbuf("w%d" % i, [128, D], F32) for i in range(2)]
        fb_s = c.sbuf("fb_s", [128, 32], F32)
        wr_s = c.sbuf("wr_s", [128, 256], F32)
        a_bf = c.sbuf("a_bf", [128, 4096], BF16)
        b_bf = c.sbuf("b_bf", [128, 4096], BF16)
        d2_s = c.sbuf("d2_s", [8, 128], F32)
        idf_s = c.sbuf("idf_s", [128, 128], F32)
        ones_b = c.sbuf("ones_b", [128, 1], BF16)
        ones_r = c.sbuf("ones_r", [1, 128], F32)
        eps_s = c.sbuf("eps_s", [1, 1], F32)
        lgt_s = c.sbuf("lgt_s", [128, 8 * 8], F32)
        e_s = c.sbuf("e_s", [128, 8 * 8], F32)
        m1 = c.sbuf("m1", [128, 1], F32)
        mneg = c.sbuf("mneg", [128, 1], F32)
        m2 = c.sbuf("m2", [128, 1], F32)
        msk = c.sbuf("msk", [128, 8], F32)
        rcpr = c.sbuf("rcpr", [128, 1], F32)
        wts = c.sbuf("wts", [128, 8 * 8], F32)
        wt_s = c.sbuf("wt_s", [8, TPC], F32)
        wb_s = c.sbuf("wb_s", [128, TPC], F32)
        g_s = [c.sbuf("g%d" % i, [128, 512], BF16) for i in range(2)]
        sq = [c.sbuf("sq%d" % i, [128, 512], BF16) for i in range(2)]
        u = c.sbuf("u", [128, 512], F32)
        uo = [c.sbuf("uo%d" % i, [128, TPC], F32) for i in range(2)]
        r1 = c.sbuf("r1", [1, 512], F32)
        r2 = c.sbuf("r2", [1, 512], F32)
        r3 = c.sbuf("r3", [1, 512], F32)
        dbg_lo = None
        ps = [c.psum("ps%d" % i, [128, 512], F32) for i in range(4)]
        ps_mu = c.psum("ps_mu", [128, 512], F32)
        ps_m2 = c.psum("ps_m2", [128, 512], F32)
        ps_a = c.psum("ps_a", [128, 512], F32)
        ps_b = c.psum("ps_b", [128, 512], F32)

        ld = c.sem("ld")
        wld = c.sem("wld")
        wfree = c.sem("wfree")
        ldab = c.sem("ldab")
        abct = c.sem("abct")
        rt = c.sem("rt")
        rv = c.sem("rv")
        mmd = c.sem("mmd")
        pev = c.sem("pev")
        sqd = c.sem("sqd")
        sm2 = c.sem("sm2")
        stat = c.sem("stat")
        rowv = c.sem("rowv")
        ssx = c.sem("ssx")
        rowd = c.sem("rowd")
        bcd = c.sem("bcd")
        appd = c.sem("appd")
        sto = c.sem("sto")
        svx = c.sem("svx")

        blk = c.es.enter_context(nc.Block())

        @blk.sync
        def _(sy):
            for src, dst in [(fb_d, fb_s), (wr_d, wr_s), (d2_d, d2_s),
                             (idf_d, idf_s), (ob_d, ones_b), (or_d, ones_r),
                             (eps_d, eps_s)]:
                sy.dma_start(out=dst[:], in_=src).then_inc(ld, 16)
            sy.dma_start(out=wbuf[0][:], in_=a_d).then_inc(ldab, 16)
            sy.dma_start(out=wbuf[1][:], in_=b_d).then_inc(ldab, 16)
            for ic in range(NCH):
                sy.dma_start(out=xs[:, ic * TPC:(ic + 1) * TPC],
                             in_=x_in[ic]).then_inc(ld, 16)
            for ic in range(NCH):
                sy.dma_start(out=xr[:, ic * TPC:(ic + 1) * TPC],
                             in_=x_in[ic]).then_inc(ld, 16)
            sy.wait_ge(abct, 2)
            for oc in range(32):
                if oc >= 2:
                    sy.wait_ge(pev, 2 * oc - 2)
                sy.dma_start(out=wbuf[oc % 2][:], in_=w_d[oc]).then_inc(wld, 16)

        ld_all = 16 * (7 + 2 * NCH)

        @blk.vector
        def _(v):
            v.wait_ge(ldab, 32)
            c.inc(v.tensor_copy(out=a_bf[:], in_=wbuf[0][:]), abct)
            c.inc(v.tensor_copy(out=b_bf[:], in_=wbuf[1][:]), abct)

        @blk.tensor
        def _(t):
            t.wait_ge(ld, ld_all)
            for tt in range(8):
                bank = [ps_a, ps_b][tt % 2]
                if tt >= 2:
                    t.wait_ge(rv, tt - 1)
                mm = None
                for ic in range(NCH):
                    mm = t.matmul(out=bank[:, 0:8],
                                  lhsT=xs[:, ic * TPC + 128 * tt:ic * TPC + 128 * (tt + 1)],
                                  rhs=bfview(wr_s, 256)[:, 8 * ic:8 * (ic + 1)],
                                  start=(ic == 0), stop=(ic == NCH - 1))
                c.inc(mm, rt)
            for tt in range(8):
                t.wait_ge(rv, 8 + tt + 1)
                bank = [ps_a, ps_b][tt % 2]
                if tt >= 2:
                    t.wait_ge(rv, 16 + tt - 1)
                c.inc(t.transpose(out=bank[0:8, 0:128],
                                  in_=wts[:, 8 * tt:8 * (tt + 1)],
                                  identity=idf_s[:]), rt)
            for tb in range(2):
                t.wait_ge(rv, 24)
                bank = [ps_a, ps_b][tb]
                c.inc(t.matmul(out=bank[:], lhsT=d2_s[:],
                               rhs=wt_s[0:8, 512 * tb:512 * (tb + 1)],
                               start=True, stop=True), rt)
            t.wait_ge(abct, 2)
            for tb in range(2):
                t.wait_ge(rv, 24 + tb + 1)
                bank = [ps_a, ps_b][tb]
                mm = None
                for ic in range(NCH):
                    mm = t.matmul(out=bank[:],
                                  lhsT=a_bf[:, 128 * ic:128 * (ic + 1)],
                                  rhs=xs[:, ic * TPC + 512 * tb:ic * TPC + 512 * tb + 512],
                                  start=(ic == 0), stop=(ic == NCH - 1))
                c.inc(mm, rt)
            for oc in range(32):
                t.wait_ge(wld, 16 * (oc + 1))
                wv_ = bfview(wbuf[oc % 2], D)
                for tb in range(2):
                    g = 2 * oc + tb
                    if g >= 4:
                        t.wait_ge(pev, g - 3)
                    if oc == 0:
                        t.wait_ge(rv, 26 + tb + 1)
                    bank = ps[g % 4]
                    if loraonly:
                        c.inc(t.matmul(out=bank[:],
                                       lhsT=b_bf[:, 128 * oc:128 * (oc + 1)],
                                       rhs=g_s[tb][:], start=True, stop=True), mmd)
                    elif lora:
                        for ic in range(NCH):
                            t.matmul(out=bank[:],
                                     lhsT=wv_[:, 128 * ic:128 * (ic + 1)],
                                     rhs=xs[:, ic * TPC + 512 * tb:
                                            ic * TPC + 512 * tb + 512],
                                     start=(ic == 0), stop=False)
                        c.inc(t.matmul(out=bank[:],
                                       lhsT=b_bf[:, 128 * oc:128 * (oc + 1)],
                                       rhs=g_s[tb][:], start=False, stop=True), mmd)
                    else:
                        mm = None
                        for ic in range(NCH):
                            mm = t.matmul(out=bank[:],
                                          lhsT=wv_[:, 128 * ic:128 * (ic + 1)],
                                          rhs=xs[:, ic * TPC + 512 * tb:
                                                 ic * TPC + 512 * tb + 512],
                                          start=(ic == 0), stop=(ic == NCH - 1))
                        c.inc(mm, mmd)
            t.wait_ge(pev, 64)
            for tb in range(2):
                if tb == 1:
                    t.wait_ge(rowd, 2)
                for ic in range(NCH):
                    t.matmul(out=ps_mu[0:1, :], lhsT=ones_b[:],
                             rhs=xr[:, ic * TPC + 512 * tb:ic * TPC + 512 * tb + 512],
                             start=(ic == 0), stop=(ic == NCH - 1))
                for ic in range(NCH):
                    k = NCH * tb + ic
                    t.wait_ge(sqd, k + 1)
                    c.inc(t.matmul(out=ps_m2[0:1, :], lhsT=ones_b[:],
                                   rhs=sq[ic % 2][:],
                                   start=(ic == 0), stop=(ic == NCH - 1)), sm2)
                t.wait_ge(rowd, 2 * tb + 2)
                t.matmul(out=ps[2 * tb][:], lhsT=ones_r[:], rhs=r1[:],
                         start=True, stop=True)
                c.inc(t.matmul(out=ps[2 * tb + 1][:], lhsT=ones_r[:], rhs=r3[:],
                               start=True, stop=True), bcd)

        @blk.vector
        def _(v):
            v.wait_ge(ld, ld_all)
            for tt in range(8):
                v.wait_ge(rt, tt + 1)
                bank = [ps_a, ps_b][tt % 2]
                ecols = slice(8 * tt, 8 * (tt + 1))
                i1 = c.inc(v.tensor_reduce(out=m1[:], in_=bank[:, 0:8],
                                           axis=AX.X, op=ALU.max), svx)
                v.wait_ge(svx, i1)
                i2 = c.inc(v.tensor_scalar(out=mneg[:], in0=m1[:], scalar1=-1.0,
                                           scalar2=None, op0=ALU.mult), svx)
                v.wait_ge(svx, i2)
                i3 = c.inc(v.tensor_scalar(out=lgt_s[:, ecols], in0=bank[:, 0:8],
                                           scalar1=mneg[:, :], scalar2=None,
                                           op0=ALU.add), svx)
                v.wait_ge(svx, i3)
                c.inc(v.tensor_copy(out=lgt_s[:, ecols], in_=lgt_s[:, ecols]), rv)

        @blk.scalar
        def _(sc):
            for tt in range(8):
                sc.wait_ge(rv, tt + 1)
                ecols = slice(8 * tt, 8 * (tt + 1))
                c.inc(sc.activation(out=e_s[:, ecols], in_=lgt_s[:, ecols],
                                    func=AF.Exp), ssx)

        @blk.vector
        def _(v):
            for tt in range(8):
                v.wait_ge(ssx, tt + 1)
                ecols = slice(8 * tt, 8 * (tt + 1))
                i1 = c.inc(v.tensor_reduce(out=m1[:], in_=e_s[:, ecols],
                                           axis=AX.X, op=ALU.max), svx)
                v.wait_ge(svx, i1)
                i2 = c.inc(v.tensor_scalar(out=msk[:], in0=e_s[:, ecols],
                                           scalar1=m1[:, :], scalar2=None,
                                           op0=ALU.is_ge), svx)
                v.wait_ge(svx, i2)
                i3 = c.inc(v.scalar_tensor_tensor(out=wts[:, ecols], in0=msk[:],
                                                  scalar=-1000.0, in1=e_s[:, ecols],
                                                  op0=ALU.mult, op1=ALU.add), svx)
                v.wait_ge(svx, i3)
                i4 = c.inc(v.tensor_reduce(out=m2[:], in_=wts[:, ecols],
                                           axis=AX.X, op=ALU.max), svx)
                v.wait_ge(svx, i4)
                i5 = c.inc(v.tensor_tensor(out=rcpr[:], in0=m1[:], in1=m2[:],
                                           op=ALU.add), svx)
                v.wait_ge(svx, i5)
                i6 = c.inc(v.reciprocal(out=rcpr[:], in_=rcpr[:]), svx)
                i7 = c.inc(v.tensor_scalar(out=msk[:], in0=e_s[:, ecols],
                                           scalar1=m2[:, :], scalar2=None,
                                           op0=ALU.is_ge), svx)
                v.wait_ge(svx, i7)
                i8 = c.inc(v.tensor_tensor(out=msk[:], in0=msk[:],
                                           in1=e_s[:, ecols], op=ALU.mult), svx)
                v.wait_ge(svx, max(i6, i8))
                c.inc(v.tensor_scalar(out=wts[:, ecols], in0=msk[:],
                                      scalar1=rcpr[:, :], scalar2=None,
                                      op0=ALU.mult), rv)
            for tt in range(8):
                v.wait_ge(rt, 8 + tt + 1)
                bank = [ps_a, ps_b][tt % 2]
                c.inc(v.tensor_copy(out=wt_s[0:8, 128 * tt:128 * (tt + 1)],
                                    in_=bank[0:8, 0:128]), rv)
            for tb in range(2):
                v.wait_ge(rt, 16 + tb + 1)
                bank = [ps_a, ps_b][tb]
                c.inc(v.tensor_copy(out=wb_s[:, 512 * tb:512 * (tb + 1)],
                                    in_=bank[:]), rv)
            for tb in range(2):
                v.wait_ge(rt, 18 + tb + 1)
                v.wait_ge(rv, 26)
                bank = [ps_a, ps_b][tb]
                c.inc(v.tensor_tensor(out=g_s[tb][:], in0=bank[:],
                                      in1=wb_s[:, 512 * tb:512 * (tb + 1)],
                                      op=ALU.mult), rv)

        @blk.scalar
        def _(sc):
            sc.wait_ge(pev, 64)
            for tb in range(2):
                for ic in range(NCH):
                    k = NCH * tb + ic
                    if k >= 2:
                        sc.wait_ge(sm2, k - 1)
                    c.inc(sc.activation(out=sq[ic % 2][:],
                                        in_=xr[:, ic * TPC + 512 * tb:
                                               ic * TPC + 512 * tb + 512],
                                        func=AF.Square), sqd)
                sc.wait_ge(rowv, tb + 1)
                c.inc(sc.activation(out=r3[:], in_=r2[:], func=AF.Sqrt,
                                    bias=eps_s[0:1, :], scale=1.0), ssx)

        @blk.vector
        def _(v):
            for oc in range(32):
                for tb in range(2):
                    g = 2 * oc + tb
                    v.wait_ge(mmd, g + 1)
                    bank = ps[g % 4]
                    cols = slice(oc * TPC + 512 * tb, oc * TPC + 512 * tb + 512)
                    if debug and oc == 0 and tb == 0:
                        i0 = c.inc(v.tensor_copy(out=uo[1][:, 0:512], in_=bank[:]), svx)
                        v.wait_ge(svx, i0)
                    i1 = c.inc(v.tensor_scalar(
                        out=u[:], in0=bank[:], scalar1=fb_s[:, oc:oc + 1],
                        scalar2=None, op0=ALU.add), svx)
                    v.wait_ge(svx, i1)
                    c.inc(v.tensor_tensor(out=xr[:, cols], in0=u[:],
                                          in1=xr[:, cols], op=ALU.add), pev)
            for tb in range(2):
                v.wait_ge(sm2, NCH * (tb + 1))
                i1 = c.inc(v.tensor_scalar(out=r1[:], in0=ps_mu[0:1, :],
                                           scalar1=-1.0 / D, scalar2=None,
                                           op0=ALU.mult), svx)
                i2 = c.inc(v.tensor_scalar(out=r2[:], in0=ps_m2[0:1, :],
                                           scalar1=1.0 / D, scalar2=None,
                                           op0=ALU.mult), svx)
                v.wait_ge(svx, i2)
                i3 = c.inc(v.tensor_tensor(out=r3[:], in0=r1[:], in1=r1[:],
                                           op=ALU.mult), svx)
                v.wait_ge(svx, i3)
                i4 = c.inc(v.scalar_tensor_tensor(out=r2[:], in0=r3[:],
                                                  scalar=-1.0, in1=r2[:],
                                                  op0=ALU.mult, op1=ALU.add), svx)
                v.wait_ge(svx, i4)
                c.inc(v.tensor_copy(out=r2[:], in_=r2[:]), rowv)
                v.wait_ge(ssx, 8 + tb + 1)
                i5 = c.inc(v.reciprocal(out=r3[:], in_=r3[:]), svx)
                v.wait_ge(svx, i5)
                c.inc(v.tensor_copy(out=r1[:], in_=r1[:]), rowd)
                c.inc(v.tensor_copy(out=r3[:], in_=r3[:]), rowd)
            v.wait_ge(bcd, 2)
            for ic in range(NCH):
                if ic >= 2:
                    v.wait_ge(sto, 16 * (ic - 1))
                for tb in range(2):
                    cols = slice(ic * TPC + 512 * tb, ic * TPC + 512 * tb + 512)
                    j1 = c.inc(v.tensor_tensor(out=u[:], in0=xr[:, cols],
                                               in1=ps[2 * tb][:], op=ALU.add), svx)
                    v.wait_ge(svx, j1)
                    last = v.tensor_tensor(out=uo[ic % 2][:, 512 * tb:512 * tb + 512],
                                           in0=u[:], in1=ps[2 * tb + 1][:],
                                           op=ALU.mult)
                    if tb == 1:
                        c.inc(last, appd)

        @blk.gpsimd
        def _(gp):
            nst = 0
            if debug:
                gp.wait_ge(pev, 1)
                gp.dma_start(out=dbg_o["d_lo"], in_=uo[1][:, 0:512]).then_inc(sto, 16)
                nst += 1
            for ic in range(NCH):
                gp.wait_ge(appd, ic + 1)
                gp.dma_start(out=x_o[ic], in_=uo[ic % 2][:]).then_inc(sto, 16)
            nst += NCH
            if debug:
                gp.wait_ge(rv, 28)
                gp.dma_start(out=dbg_o["d_e"], in_=e_s[:]).then_inc(sto, 16)
                gp.dma_start(out=dbg_o["d_wts"], in_=wts[:]).then_inc(sto, 16)
                gp.dma_start(out=dbg_o["d_wt"], in_=wt_s[:]).then_inc(sto, 16)
                gp.dma_start(out=dbg_o["d_wb"], in_=wb_s[:]).then_inc(sto, 16)
                gp.dma_start(out=dbg_o["d_g"][:, 0:512], in_=g_s[0][:]).then_inc(sto, 16)
                gp.dma_start(out=dbg_o["d_g"][:, 512:1024], in_=g_s[1][:]).then_inc(sto, 16)
                pass
                gp.dma_start(out=dbg_o["d_bb"], in_=b_bf[:].bitcast(F32)[:, 0:2048]).then_inc(sto, 16)
                nst += 7
            gp.wait_ge(sto, 16 * nst)

    return nc



_PROGS = {}
_DBG = {}


def _prog(key, fn, *a):
    if key not in _PROGS:
        _PROGS[key] = fn(*a)
    return _PROGS[key]


def _stack(arrs):
    return np.ascontiguousarray(arrs.reshape(NCH, 128, TPC))


def _tm_from_chunks(out_chunks):
    return out_chunks.reshape(D, TPC)


def kernel_unfused(L0_attn_x, L0_attn_Wq, L0_attn_Wk, L0_attn_Wv, L0_attn_Wo,
           L0_ffn_quantized_weights, L0_ffn_scales, L0_ffn_bias,
           L1_attn_wq, L1_attn_wk, L1_attn_wv, L1_attn_wo,
           L1_ffn_W_base, L1_ffn_b_base, L1_ffn_W_router,
           L1_ffn_lora_A, L1_ffn_lora_B):
    cores = list(range(NCORES))
    toks = [core_tokens(j) for j in cores]

    ident_bf = np.ascontiguousarray(np.eye(128, dtype=np.float32).astype(bfloat16))
    ident_f = np.ascontiguousarray(np.eye(128, dtype=np.float32))
    ones_bf = np.ones((128, 128), bfloat16)
    ones_col = np.ones((128, 1), bfloat16)
    ones_row = np.ones((1, 128), np.float32)
    epsv = np.full((1, 1), LN_EPS, np.float32)
    dpat = diag_patterns()
    cos0, sin0 = rope_tables_l0(S)
    cos1, sin1 = rope_tables_l1(S)
    rope0 = [rope_fm(cos0, sin0, toks[j]) for j in cores]
    rope1 = [rope_fm(cos1, sin1, toks[j]) for j in cores]
    ebias = {}
    for j in cores:
        _, bias = pack_kv(j, np.zeros((1, 128, S), bfloat16))
        ebias[j] = np.ascontiguousarray(
            np.broadcast_to(bias[None, :], (128, NT)).astype(np.float32))

    wq0 = colchunk(np.ascontiguousarray(L0_attn_Wq.T))
    wk0 = colchunk(np.ascontiguousarray(L0_attn_Wk.T))
    wv0 = colchunk(np.ascontiguousarray(L0_attn_Wv.T))
    wo0 = colchunk(np.ascontiguousarray(L0_attn_Wo.T))
    qw0 = colchunk(np.ascontiguousarray(L0_ffn_quantized_weights.T))
    sc0 = np.ascontiguousarray(L0_ffn_scales.reshape(32, 128).T)
    bi0 = np.ascontiguousarray(L0_ffn_bias.reshape(32, 128).T)
    wq1 = colchunk(np.ascontiguousarray(L1_attn_wq))
    wk1 = colchunk(np.ascontiguousarray(L1_attn_wk))
    wv1 = colchunk(np.ascontiguousarray(L1_attn_wv))
    wo1 = colchunk(np.ascontiguousarray(L1_attn_wo))
    wb1 = colchunk(np.ascontiguousarray(L1_ffn_W_base.T))
    fb1 = np.ascontiguousarray(L1_ffn_b_base.reshape(32, 128).T)
    wr1 = np.ascontiguousarray(
        L1_ffn_W_router.reshape(8, 32, 128).transpose(2, 1, 0).reshape(128, 256))
    la1 = np.ascontiguousarray(
        L1_ffn_lora_A.reshape(128, D).T.reshape(32, 128, 128)
        .transpose(1, 0, 2).reshape(128, 4096))
    lb1 = np.ascontiguousarray(
        L1_ffn_lora_B.transpose(0, 2, 1).reshape(128, D))
    d2 = np.zeros((8, 128), np.float32)
    for e in range(8):
        d2[e, 16 * e:16 * (e + 1)] = 2.0

    x_fm = [np.ascontiguousarray(L0_attn_x[j // 4].T[:, toks[j]]) for j in cores]

    def launch(nc, maps):
        res = run_bass_kernel_spmd(nc, maps, core_ids=cores)
        return res.results

    def gather_kv(q_r, k_r, v_r, nkv):
        kfull = [np.zeros((nkv, 128, S), bfloat16) for _ in range(B)]
        vfull = [np.zeros((nkv, 128, S), bfloat16) for _ in range(B)]
        for j in cores:
            b = j // 4
            karr = k_r[j]["k"]
            varr = v_r[j]["v"]
            va = varr.reshape(nkv, 128, 8, 128)
            c0, c1 = core_chunks(j)
            for li, ch in ((0, c0), (1, c1)):
                gsl = slice(512 * ch, 512 * (ch + 1))
                lsl = slice(512 * li, 512 * (li + 1))
                kfull[b][:, :, gsl] = karr[:, :, lsl]
            for tt in range(8):
                ch = c0 if tt < 4 else c1
                ktg = 4 * ch + (tt % 4)
                vfull[b][:, :, 128 * ktg:128 * (ktg + 1)] = va[:, :, tt, :]
        kp, vp = {}, {}
        for j in cores:
            b = j // 4
            kp[j], _ = pack_kv(j, kfull[b])
            vp[j], _ = pack_kv(j, vfull[b])
        return kp, vp

    _DBG.clear()
    nc_a = _prog("qkv0", build_qkv, 0)
    maps = [{"x": _stack(x_fm[j]), "wq": wq0, "wk": wk0, "wv": wv0,
             "cos": rope0[j][0], "sinS": rope0[j][1], "ident": ident_bf}
            for j in cores]
    r_a = launch(nc_a, maps)
    xs_cur = [r_a[j]["xs"] for j in cores]
    kp, vp = gather_kv(None, r_a, r_a, 32)

    nc_b = _prog("attn0", build_attn, 0)
    maps = [{"q": r_a[j]["q"], "kp": kp[j], "vp": vp[j], "ebias": ebias[j],
             "dpat": dpat, "onesb": ones_bf} for j in cores]
    r_b = launch(nc_b, maps)

    nc_c = _prog("oln", build_proj_ln, "plain", False)
    maps = [{"xin": r_b[j]["attn"], "xres": xs_cur[j], "w": wo0,
             "onesb": ones_col, "onesr": ones_row, "epsv": epsv}
            for j in cores]
    r_c = launch(nc_c, maps)
    xs_cur = [r_c[j]["xout"] for j in cores]
    _DBG["x1"] = list(xs_cur)

    nc_d = _prog("qffn", build_proj_ln, "qffn", False)
    maps = [{"xin": xs_cur[j], "xres": xs_cur[j], "w": qw0, "scales": sc0,
             "fbias": bi0, "onesb": ones_col, "onesr": ones_row, "epsv": epsv}
            for j in cores]
    r_d = launch(nc_d, maps)
    xs_cur = [r_d[j]["xout"] for j in cores]
    _DBG["x2"] = list(xs_cur)

    nc_e = _prog("qkv1", build_qkv, 1)
    maps = [{"x": xs_cur[j], "wq": wq1, "wk": wk1, "wv": wv1,
             "cos": rope1[j][0], "sinS": rope1[j][1], "ident": ident_bf}
            for j in cores]
    r_e = launch(nc_e, maps)
    kp, vp = gather_kv(None, r_e, r_e, 8)

    nc_f = _prog("attn1", build_attn, 1)
    maps = [{"q": r_e[j]["q"], "kp": kp[j], "vp": vp[j], "ebias": ebias[j],
             "dpat": dpat, "onesb": ones_bf} for j in cores]
    r_f = launch(nc_f, maps)

    maps = [{"xin": r_f[j]["attn"], "xres": xs_cur[j], "w": wo1,
             "onesb": ones_col, "onesr": ones_row, "epsv": epsv}
            for j in cores]
    r_g = launch(nc_c, maps)
    xs_cur = [r_g[j]["xout"] for j in cores]
    _DBG["x3"] = list(xs_cur)

    nc_h = _prog("ffn1", build_ffn1)
    maps = [{"xin": xs_cur[j], "w": wb1, "fbias": fb1, "wr": wr1, "la": la1,
             "lb": lb1, "d2": d2, "idf": ident_f, "onesb": ones_col,
             "onesr": ones_row, "epsv": epsv} for j in cores]
    r_h = launch(nc_h, maps)

    out = np.empty((B, S, D), np.float32)
    for j in cores:
        b = j // 4
        xo = _tm_from_chunks(r_h[j]["xout"])
        out[b, toks[j], :] = xo.T
    return out



NKTM = (16, 28)
CHUNK_RANK = [min(cg, 7 - cg) for cg in range(8)]
CHUNK_HALF = [0 if cg == CHUNK_RANK[cg] else 1 for cg in range(8)]


def fused_ebias(core_j):
    c0, c1 = core_chunks(core_j)
    b = np.full(52, PAD_BIAS, np.float32)
    b[0:4 * c0] = EXP_SHIFT
    b[16:20] = EXP_SHIFT
    b[20:20 + 4 * (c1 - 4) + 16] = EXP_SHIFT
    b[48:52] = EXP_SHIFT
    b[20:48] = np.where(np.arange(28) < 4 * c1, EXP_SHIFT, PAD_BIAS)
    return np.ascontiguousarray(np.broadcast_to(b[None, :], (128, 52)).astype(np.float32))


def build_fused():
    c = Ctx()
    nc = c.nc
    ap = lambda t: t.ap()

    x_in = nc.dram_tensor("x", [NCH, 128, TPC], F32, kind="ExternalInput").ap()
    w_l0 = {n: nc.dram_tensor(n, [noc, 128, D], F32, kind="ExternalInput").ap()
            for n, noc in [("wq0", 32), ("wk0", 32), ("wv0", 32), ("wo0", 32)]}
    qw_d = nc.dram_tensor("qw", [32, 128, D], I32, kind="ExternalInput").ap()
    sc_d = nc.dram_tensor("scales", [128, 32], F32, kind="ExternalInput").ap()
    bi_d = nc.dram_tensor("qbias", [128, 32], F32, kind="ExternalInput").ap()
    w_l1 = {n: nc.dram_tensor(n, [noc, 128, D], F32, kind="ExternalInput").ap()
            for n, noc in [("wq1", 32), ("wk1", 8), ("wv1", 8), ("wo1", 32)]}
    wb_d = nc.dram_tensor("wbase", [32, 128, D], F32, kind="ExternalInput").ap()
    fb_d = nc.dram_tensor("fbias", [128, 32], F32, kind="ExternalInput").ap()
    wr_d = nc.dram_tensor("wr", [128, 256], F32, kind="ExternalInput").ap()
    a_d = nc.dram_tensor("la", [128, 4096], F32, kind="ExternalInput").ap()
    b_d = nc.dram_tensor("lb", [128, 4096], F32, kind="ExternalInput").ap()
    d2_d = nc.dram_tensor("d2", [8, 128], F32, kind="ExternalInput").ap()
    rope_d = {n: nc.dram_tensor(n, [128, TPC], F32, kind="ExternalInput").ap()
              for n in ["cos0", "sin0", "cos1", "sin1"]}
    id_d = nc.dram_tensor("ident", [128, 128], BF16, kind="ExternalInput").ap()
    idf_d = nc.dram_tensor("idf", [128, 128], F32, kind="ExternalInput").ap()
    ob_d = nc.dram_tensor("onesb", [128, 1], BF16, kind="ExternalInput").ap()
    obm_d = nc.dram_tensor("onesbm", [128, 128], BF16, kind="ExternalInput").ap()
    or_d = nc.dram_tensor("onesr", [1, 128], F32, kind="ExternalInput").ap()
    eps_d = nc.dram_tensor("epsv", [1, 1], F32, kind="ExternalInput").ap()
    dpat_d = nc.dram_tensor("dpat", [4, 128, 512], BF16, kind="ExternalInput").ap()
    eb_d = nc.dram_tensor("ebias", [128, 52], F32, kind="ExternalInput").ap()
    x_o = nc.dram_tensor("xout", [NCH, 128, TPC], F32, kind="ExternalOutput").ap()

    q_h = {L: nc.dram_tensor("qh%d" % L, [32, 128, TPC], BF16).ap() for L in (0, 1)}
    k_h = {L: nc.dram_tensor("kh%d" % L, [(32 if L == 0 else 8) * 128, TPC], BF16).ap()
           for L in (0, 1)}
    v_h = {L: nc.dram_tensor("vh%d" % L, [(32 if L == 0 else 8) * 128, TPC], BF16).ap()
           for L in (0, 1)}
    kg_h = {L: nc.dram_tensor("kg%d" % L, [4 * (32 if L == 0 else 8) * 128, TPC], BF16)
            for L in (0, 1)}
    vg_h = {L: nc.dram_tensor("vg%d" % L, [4 * (32 if L == 0 else 8) * 128, TPC], BF16)
            for L in (0, 1)}
    at_h = {L: nc.dram_tensor("at%d" % L, [32, 128, TPC], BF16).ap() for L in (0, 1)}
    xs0_h = nc.dram_tensor("xs0", [NCH, 128, TPC], BF16).ap()
    xh = {i: nc.dram_tensor("xh%d" % i, [NCH, 128, TPC], BF16).ap() for i in (1, 2, 3)}

    with c.es:
        S_XS = c.sbuf("S_XS", [128, 32768], BF16)
        S_XR = c.sbuf("S_XR", [128, 32768], BF16)
        S_W = c.sbuf("S_W", [128, 8192], F32)
        AB = c.sbuf("AB", [128, 8192], BF16)
        cos_s = c.sbuf("cos_s", [128, TPC], F32)
        sin_s = c.sbuf("sin_s", [128, TPC], F32)
        id_s = c.sbuf("id_s", [128, 128], BF16)
        idf_s = c.sbuf("idf_s", [128, 128], F32)
        ones_b = c.sbuf("ones_b", [128, 1], BF16)
        ones_bm = c.sbuf("ones_bm", [128, 128], BF16)
        ones_r = c.sbuf("ones_r", [1, 128], F32)
        eps_s = c.sbuf("eps_s", [1, 1], F32)
        eb_s = c.sbuf("eb_s", [128, 52], F32)
        vsc_s = c.sbuf("vsc_s", [128, 32], F32)
        vbi_s = c.sbuf("vbi_s", [128, 32], F32)
        fb_s = c.sbuf("fb_s", [128, 32], F32)
        wr_s = c.sbuf("wr_s", [128, 256], F32)
        u_s = c.sbuf("u_s", [128, 512], F32)
        r1 = c.sbuf("r1", [1, 512], F32)
        r2 = c.sbuf("r2", [1, 512], F32)
        r3 = c.sbuf("r3", [1, 512], F32)
        lgt_s = c.sbuf("lgt_s", [128, 64], F32)
        e_s = c.sbuf("e_s", [128, 64], F32)
        wts_s = c.sbuf("wts_s", [128, 64], F32)
        m1_s = c.sbuf("m1_s", [128, 1], F32)
        mneg_s = c.sbuf("mneg_s", [128, 1], F32)
        m2_s = c.sbuf("m2_s", [128, 1], F32)
        msk_s = c.sbuf("msk_s", [128, 8], F32)
        rcpr_s = c.sbuf("rcpr_s", [128, 1], F32)
        wt_s = c.sbuf("wt_s", [8, TPC], F32)
        wb_s = c.sbuf("wb_s", [128, TPC], F32)
        g_s = [c.sbuf("g%d" % i, [128, 512], BF16) for i in range(2)]
        ps = [c.psum("ps%d" % i, [128, 512], F32) for i in range(4)]
        ps_mu = c.psum("ps_mu", [128, 512], F32)
        ps_m2 = c.psum("ps_m2", [128, 512], F32)
        ps_a = c.psum("ps_a", [128, 512], F32)
        ps_b = c.psum("ps_b", [128, 512], F32)
        ps_trA = ps_a[:].bitcast(BF16)
        ps_trB = ps_b[:].bitcast(BF16)
        d2_s2 = c.sbuf("d2_s2", [8, 128], F32)

        pb = c.sem("pb")
        gsvx = c.sem("gsvx")
        blk = c.es.enter_context(nc.Block())
        phase_n = [0]

        def barrier_wait(eng):
            eng.wait_ge(pb, 5 * phase_n[0])

        def barrier_done(incs):
            phase_n[0] += 1

        XSf = S_XS[:].bitcast(F32)
        XRf = S_XR[:].bitcast(F32)
        ABf = AB[:].bitcast(F32)
        uo = [XSf[:, 1024 * i:1024 * (i + 1)] for i in range(2)]
        sq = [S_XS[:, 4096 + 512 * i:4096 + 512 * (i + 1)] for i in range(2)]
        uo_w = [S_W[:, 2048 + 1024 * i:2048 + 1024 * (i + 1)] for i in range(2)]
        sq_w = [S_W[:].bitcast(BF16)[:, 512 * i:512 * (i + 1)] for i in range(2)]
        dpat_s = S_XS[:, 24576:26624]
        rcp_s = XSf[:, 13312:13824]


        def emit_const_loads():
            p = "cl"
            ld = c.sem(p + "ld")
            ms = c.sem(p + "ms")

            @blk.gpsimd
            def _(gp):
                gp.memset(S_W[:], 0.0)
                gp.memset(S_XS[:], 0.0)
                c.inc(gp.tensor_copy(out=S_XR[:, 0:1], in_=S_XS[:, 0:1]), ms)
                gp.memset(S_XR[:], 0.0)
                c.inc(gp.tensor_copy(out=S_W[:, 0:1], in_=S_W[:, 0:1]), ms)

            @blk.sync
            def _(sy):
                sy.wait_ge(ms, 2)
                for src, dst in [(id_d, id_s), (idf_d, idf_s), (ob_d, ones_b),
                                 (obm_d, ones_bm), (or_d, ones_r), (eps_d, eps_s),
                                 (eb_d, eb_s), (sc_d, vsc_s), (bi_d, vbi_s),
                                 (fb_d, fb_s), (wr_d, wr_s), (d2_d, d2_s2)]:
                    sy.dma_start(out=dst[:], in_=src).then_inc(ld, 16)
                sy.wait_ge(ld, 16 * 12)
                sy.sem_inc(pb, 5)
            phase_n[0] += 1

        def emit_qkv(L):
            p = "qk%d" % L
            nkv = 32 if L == 0 else 8
            ldx = c.sem(p + "ldx")
            ldxs = [c.sem(p + "ldxa"), c.sem(p + "ldxb")]
            wld = c.sem(p + "wld")
            mmd = c.sem(p + "mmd")
            pev = c.sem(p + "pev")
            xca = c.sem(p + "xca")
            trd = c.sem(p + "trd")
            tre = c.sem(p + "tre")
            sto = c.sem(p + "sto")
            svx = gsvx
            ccs = c.sem(p + "ccs")
            projs = [("v", w_l0["wv0"] if L == 0 else w_l1["wv1"], nkv),
                     ("k", w_l0["wk0"] if L == 0 else w_l1["wk1"], nkv),
                     ("q", w_l0["wq0"] if L == 0 else w_l1["wq1"], 32)]
            xs = S_XR
            xl = S_W[:, 4096:6144]
            qk_out = [S_XS[:, 1024 * i:1024 * (i + 1)] for i in range(2)]
            vfm = [S_XS[:, 2048 + 1024 * i:2048 + 1024 * (i + 1)] for i in range(2)]
            vtm = [S_XS[:, 4096 + 1024 * i:4096 + 1024 * (i + 1)] for i in range(2)]
            tmp = XSf[:, 3072:3584]
            u = XSf[:, 3584:4096]
            wbuf = [S_W[:, 0:4096], S_W[:, 4096:8192]]
            ps_tr = [ps_trA, ps_trB]
            n_xs_st = 0
            store_base = {}
            _si = n_xs_st
            for _pn, _wd, _noc in projs:
                store_base[_pn] = _si
                _si += _noc
            x_src = x_in if L == 0 else xh[2]
            cosd = rope_d["cos0"] if L == 0 else rope_d["cos1"]
            sind = rope_d["sin0"] if L == 0 else rope_d["sin1"]

            @blk.sync
            def _(sy):
                barrier_wait(sy)
                sy.dma_start(out=cos_s[:], in_=cosd).then_inc(ldx, 16)
                sy.dma_start(out=sin_s[:], in_=sind).then_inc(ldx, 16)
                if L == 0:
                    for ic in range(NCH):
                        if ic >= 2:
                            sy.wait_ge(xca, ic - 1)
                        sy.dma_start(out=xl[:, (ic % 2) * TPC:(ic % 2 + 1) * TPC],
                                     in_=x_src[ic]).then_inc(ldxs[ic % 2], 16)
                si = 0
                for _pn, wd, noc in projs:
                    for oc in range(noc):
                        if si == 1 and L == 0:
                            sy.wait_ge(xca, NCH)
                        if si >= 2:
                            sy.wait_ge(mmd, 2 * si - 2)
                        sy.dma_start(out=wbuf[si % 2][:],
                                     in_=wd[oc]).then_inc(wld, 16)
                        si += 1
                sy.sem_inc(pb, 1)

            if L == 0:
                @blk.vector
                def _(v):
                    barrier_wait(v)
                    for ic in range(NCH):
                        v.wait_ge(ldxs[ic % 2], 16 * (ic // 2 + 1))
                        c.inc(v.tensor_copy(out=xs[:, ic * TPC:(ic + 1) * TPC],
                                            in_=xl[:, (ic % 2) * TPC:(ic % 2 + 1) * TPC]),
                              xca)

            @blk.tensor
            def _(t):
                barrier_wait(t)
                if L == 0:
                    t.wait_ge(xca, NCH)
                t.wait_ge(ldx, 32)
                si = 0
                g = 0

                def v_transpose(oc):
                    t.wait_ge(pev, 2 * oc + 2)
                    if oc >= 2:
                        t.wait_ge(tre, oc - 1)
                    tr = None
                    for tt in range(8):
                        tr = t.transpose(out=ps_tr[oc % 2][:, 128 * tt:128 * (tt + 1)],
                                         in_=vfm[oc % 2][:, 128 * tt:128 * (tt + 1)],
                                         identity=id_s[:])
                    c.inc(tr, trd)

                for _pn, wd, noc in projs:
                    for oc in range(noc):
                        t.wait_ge(wld, 16 * (si + 1))
                        wv_ = wbuf[si % 2][:].bitcast(BF16)[:, 1:2 * D:2]
                        for tb in range(2):
                            if g >= 4:
                                t.wait_ge(pev, g - 3)
                            bank = ps[g % 4]
                            mm = None
                            for ic in range(NCH):
                                mm = t.matmul(out=bank[:],
                                              lhsT=wv_[:, 128 * ic:128 * (ic + 1)],
                                              rhs=xs[:, ic * TPC + 512 * tb:
                                                     ic * TPC + 512 * tb + 512],
                                              start=(ic == 0), stop=(ic == NCH - 1))
                            c.inc(mm, mmd)
                            g += 1
                        si += 1
                        if _pn == "v" and oc >= 2:
                            v_transpose(oc - 2)
                    if _pn == "v":
                        v_transpose(nkv - 2)
                        v_transpose(nkv - 1)
                c.inc(t.nop() if False else mm, pb) if False else None
                t.sem_inc(pb, 1)

            @blk.vector
            def _(v):
                v.wait_ge(ldx, 32)
                g = 0

                def vtm_evac(oc):
                    v.wait_ge(trd, oc + 1)
                    if oc >= 2:
                        v.wait_ge(sto, 16 * (store_base["v"] + oc - 1))
                    c.inc(v.tensor_copy(out=vtm[oc % 2][:],
                                        in_=ps_tr[oc % 2][:, 0:1024]), tre)

                for oc in range(nkv):
                    for tb in range(2):
                        v.wait_ge(mmd, g + 1)
                        bank = ps[g % 4]
                        cols = slice(512 * tb, 512 * tb + 512)
                        if oc >= 2 and tb == 0:
                            v.wait_ge(trd, oc - 1)
                        c.inc(v.tensor_copy(out=vfm[oc % 2][:, cols], in_=bank[:]), pev)
                        g += 1
                    if oc >= 2:
                        vtm_evac(oc - 2)
                vtm_evac(nkv - 2)
                vtm_evac(nkv - 1)
                for pn, noc in (("k", nkv), ("q", 32)):
                    for oc in range(noc):
                        for tb in range(2):
                            v.wait_ge(mmd, g + 1)
                            bank = ps[g % 4]
                            cols = slice(512 * tb, 512 * tb + 512)
                            if tb == 0 and oc >= 2:
                                v.wait_ge(sto, 16 * (store_base[pn] + oc - 1))
                            i1 = c.inc(v.tensor_tensor(
                                out=tmp[0:64, :], in0=bank[64:128, :],
                                in1=sin_s[0:64, cols], op=ALU.mult), svx)
                            i2 = c.inc(v.tensor_tensor(
                                out=tmp[64:128, :], in0=bank[0:64, :],
                                in1=sin_s[64:128, cols], op=ALU.mult), svx)
                            i3 = c.inc(v.tensor_tensor(
                                out=u[:], in0=bank[:], in1=cos_s[:, cols],
                                op=ALU.mult), svx)
                            v.wait_ge(svx, i3)
                            last = v.tensor_tensor(out=qk_out[oc % 2][:, cols],
                                                   in0=u[:], in1=tmp[:], op=ALU.add)
                            c.inc(last, pev)
                            g += 1
                last.then_inc(pb, 1) if False else None
                v.sem_inc(pb, 1)

            @blk.scalar
            def _(s):
                barrier_wait(s)
                s.sem_inc(pb, 1)

            @blk.gpsimd
            def _(gp):
                barrier_wait(gp)
                nst = 0
                base_g = 0
                for pn, _wd, noc in projs:
                    if pn == "v":
                        for oc in range(nkv):
                            gp.wait_ge(tre, oc + 1)
                            gp.dma_start(out=v_h[L][128 * oc:128 * (oc + 1), :],
                                         in_=vtm[oc % 2][:]).then_inc(sto, 16)
                            nst += 1
                        gp.wait_ge(sto, 16 * nst)
                        gp.collective_compute(
                            "AllGather", ALU.bypass,
                            replica_groups=[[0, 1, 2, 3], [4, 5, 6, 7]],
                            ins=[v_h[L].opt()],
                            outs=[vg_h[L].ap().opt()]).then_inc(ccs)
                    else:
                        for oc in range(noc):
                            gp.wait_ge(pev, base_g + 2 * oc + 2)
                            out_d = (k_h[L][128 * oc:128 * (oc + 1), :] if pn == "k"
                                     else q_h[L][oc])
                            gp.dma_start(out=out_d, in_=qk_out[oc % 2][:]).then_inc(sto, 16)
                            nst += 1
                        if pn == "k":
                            gp.wait_ge(sto, 16 * nst)
                            gp.collective_compute(
                                "AllGather", ALU.bypass,
                                replica_groups=[[0, 1, 2, 3], [4, 5, 6, 7]],
                                ins=[k_h[L].opt()],
                                outs=[kg_h[L].ap().opt()]).then_inc(ccs)
                    base_g += 2 * noc
                gp.wait_ge(sto, 16 * nst)
                gp.wait_ge(ccs, 2)
                gp.sem_inc(pb, 1)
            phase_n[0] += 1

        def emit_attn(L):
            p = "at%d" % L
            nkv = 32 if L == 0 else 8
            rep = 32 // nkv
            qld = c.sem(p + "qld")
            kvld = c.sem(p + "kvld")
            smm = c.sem(p + "smm")
            sexp = c.sem(p + "sexp")
            sdg = c.sem(p + "sdg")
            spv = c.sem(p + "spv")
            sev = c.sem(p + "sev")
            sdone = c.sem(p + "sdone")
            svx = gsvx
            q_s = [S_XS[:, 1024 * i:1024 * (i + 1)] for i in range(2)]
            k_s = [S_XS[:, 2048 + 4608 * i:2048 + 4608 * (i + 1)] for i in range(2)]
            v_s = [S_XS[:, 11264 + 4608 * i:11264 + 4608 * (i + 1)] for i in range(2)]
            pt = [S_XS[:, 20480 + 512 * i:20480 + 512 * (i + 1)] for i in range(4)] + \
                 [S_XS[:, 27648 + 512 * i:27648 + 512 * (i + 1)] for i in range(2)]
            ao_s = [S_XS[:, 22528 + 1024 * i:22528 + 1024 * (i + 1)] for i in range(2)]
            ps_s = [ps[0], ps[1], ps_a, ps_b]
            ps_o = [ps[2], ps[3]]
            ps_z = [ps_mu, ps_m2]
            kgr = kg_h[L].ap()
            vgr = vg_h[L].ap()

            def slot_col(qb, kt):
                if kt < NKTM[qb]:
                    return 128 * kt
                return (3584 + 128 * (kt - NKTM[qb])) if qb == 0 else \
                       (4096 + 128 * (kt - NKTM[qb]))

            def bias_slot(qb, kt):
                return kt if qb == 0 else 20 + kt

            NTOT = (NKTM[0] + 4) + (NKTM[1] + 4)

            dpl = c.sem(p + "dpl")

            @blk.sync
            def _(sy):
                barrier_wait(sy)
                for a_ in range(4):
                    sy.dma_start(out=dpat_s[:, 512 * a_:512 * (a_ + 1)],
                                 in_=dpat_d[a_]).then_inc(dpl, 16)
                for h in range(32):
                    if h >= 2:
                        sy.wait_ge(sev, 2 * (h - 1))
                    sy.dma_start(out=q_s[h % 2][:], in_=q_h[L][h]).then_inc(qld, 16)
                    if h % rep == 0:
                        kv = h // rep
                        if kv >= 2:
                            sy.wait_ge(sev, 2 * rep * (kv - 1))
                        for cg in range(7):
                            rk, hf = CHUNK_RANK[cg], CHUNK_HALF[cg]
                            row = (rk * nkv + kv) * 128
                            sy.dma_start(
                                out=k_s[kv % 2][:, 512 * cg:512 * (cg + 1)],
                                in_=kgr[row:row + 128, 512 * hf:512 * (hf + 1)]
                            ).then_inc(kvld, 16)
                            sy.dma_start(
                                out=v_s[kv % 2][:, 512 * cg:512 * (cg + 1)],
                                in_=vgr[row:row + 128, 512 * hf:512 * (hf + 1)]
                            ).then_inc(kvld, 16)
                        for qb in range(2):
                            lc = slice(512 * qb, 512 * (qb + 1))
                            dc = slice(3584, 4096) if qb == 0 else slice(4096, 4608)
                            sy.dma_start(out=k_s[kv % 2][:, dc],
                                         in_=k_h[L][128 * kv:128 * (kv + 1), lc]
                                         ).then_inc(kvld, 16)
                            sy.dma_start(out=v_s[kv % 2][:, dc],
                                         in_=v_h[L][128 * kv:128 * (kv + 1), lc]
                                         ).then_inc(kvld, 16)
                sy.sem_inc(pb, 1)

            n_kv_dma = 18

            @blk.tensor
            def _(t):
                barrier_wait(t)
                for h in range(32):
                    t.wait_ge(qld, 16 * (h + 1))
                    t.wait_ge(kvld, 16 * n_kv_dma * (h // rep + 1))
                    kbuf = k_s[(h // rep) % 2]
                    vbuf = v_s[(h // rep) % 2]
                    for qb in range(2):
                        nkt = NKTM[qb] + 4
                        base = 0 if qb == 0 else NKTM[0] + 4
                        g2 = 2 * h + qb
                        gkt0 = NTOT * h + base
                        qcols = slice(512 * qb, 512 * qb + 512)
                        if g2 >= 2:
                            t.wait_ge(sev, g2 - 1)

                        def scores(kt):
                            gkt = gkt0 + kt
                            if gkt >= 4:
                                t.wait_ge(sexp, gkt - 3)
                            cc = slot_col(qb, kt)
                            c.inc(t.matmul(out=ps_s[gkt % 4][:],
                                           lhsT=kbuf[:, cc:cc + 128],
                                           rhs=q_s[h % 2][:, qcols],
                                           start=True, stop=True), smm)

                        def pv(kt):
                            gkt = gkt0 + kt
                            if kt >= nkt - 4:
                                t.wait_ge(sdg, 4 * g2 + (kt - (nkt - 4)) + 1)
                            else:
                                t.wait_ge(sexp, gkt + 1)
                            ptile = pt[gkt % 6]
                            cc = slot_col(qb, kt)
                            t.matmul(out=ps_o[g2 % 2][:], lhsT=vbuf[:, cc:cc + 128],
                                     rhs=ptile[:], start=(kt == 0), stop=(kt == nkt - 1))

                        def psum_mm(kt):
                            gkt = gkt0 + kt
                            ptile = pt[gkt % 6]
                            c.inc(t.matmul(out=ps_z[g2 % 2][:], lhsT=ones_bm[:],
                                           rhs=ptile[:], start=(kt == 0),
                                           stop=(kt == nkt - 1)), spv)

                        for kt_ in range(4):
                            scores(kt_)
                        for kt2 in range(0, nkt, 2):
                            if kt2 + 4 < nkt:
                                scores(kt2 + 4)
                            if kt2 + 5 < nkt:
                                scores(kt2 + 5)
                            pv(kt2)
                            pv(kt2 + 1)
                            psum_mm(kt2)
                            psum_mm(kt2 + 1)
                t.sem_inc(pb, 1)

            @blk.scalar
            def _(sc):
                barrier_wait(sc)
                for h in range(32):
                    for qb in range(2):
                        nkt = NKTM[qb] + 4
                        base = 0 if qb == 0 else NKTM[0] + 4
                        gkt0 = NTOT * h + base
                        for kt in range(nkt):
                            gkt = gkt0 + kt
                            sc.wait_ge(smm, gkt + 1)
                            if gkt >= 6:
                                sc.wait_ge(spv, gkt - 5)
                            bs = bias_slot(qb, kt)
                            c.inc(sc.activation(out=pt[gkt % 6][:],
                                                in_=ps_s[gkt % 4][:], func=AF.Exp,
                                                bias=eb_s[:, bs:bs + 1],
                                                scale=float(HD) ** -0.5), sexp)
                sc.sem_inc(pb, 1)

            @blk.vector
            def _(v):
                barrier_wait(v)
                v.wait_ge(dpl, 64)
                for h in range(32):
                    if h >= 2:
                        v.wait_ge(sdone, 16 * (h - 1))
                    for qb in range(2):
                        nkt = NKTM[qb] + 4
                        base = 0 if qb == 0 else NKTM[0] + 4
                        g2 = 2 * h + qb
                        gkt0 = NTOT * h + base
                        for kt in range(nkt - 4, nkt):
                            vv = kt - (nkt - 4)
                            v.wait_ge(sexp, gkt0 + kt + 1)
                            ptile = pt[(gkt0 + kt) % 6]
                            c.inc(v.tensor_tensor(out=ptile[:], in0=ptile[:],
                                                  in1=dpat_s[:, 512 * vv:512 * (vv + 1)],
                                                  op=ALU.mult), sdg)
                        v.wait_ge(spv, gkt0 + nkt)
                        i1 = c.inc(v.reciprocal(out=rcp_s[:], in_=ps_z[g2 % 2][:]), svx)
                        v.wait_ge(svx, i1)
                        c.inc(v.tensor_tensor(
                            out=ao_s[h % 2][:, 512 * qb:512 * qb + 512],
                            in0=ps_o[g2 % 2][:], in1=rcp_s[:], op=ALU.mult), sev)
                v.sem_inc(pb, 1)

            @blk.gpsimd
            def _(gp):
                barrier_wait(gp)
                for h in range(32):
                    gp.wait_ge(sev, 2 * h + 2)
                    gp.dma_start(out=at_h[L][h], in_=ao_s[h % 2][:]).then_inc(sdone, 16)
                gp.wait_ge(sdone, 16 * 32)
                gp.sem_inc(pb, 1)
            phase_n[0] += 1

        def emit_projln(name, mode, x_src, res_src, w_src, out_dst, final=False):
            p = name
            ld = c.sem(p + "ld")
            wld = c.sem(p + "wld")
            wcst = c.sem(p + "wcst")
            mmd = c.sem(p + "mmd")
            pev = c.sem(p + "pev")
            sqd = c.sem(p + "sqd")
            sm2 = c.sem(p + "sm2")
            rowv = c.sem(p + "rowv")
            ssx = c.sem(p + "ssx")
            rowd = c.sem(p + "rowd")
            bcd = c.sem(p + "bcd")
            appd = c.sem(p + "appd")
            sto = c.sem(p + "sto")
            svx = gsvx
            plain = (mode != "qffn")
            xs = S_XS if plain else S_XR
            xr = S_XR if plain else S_XS
            strm = S_XR
            sqv = sq if plain else sq_w
            uov = uo if plain else uo_w
            wbuf = [S_W[:, 0:4096], S_W[:, 4096:8192]]
            wb16 = [AB[:, 4096 * i:4096 * (i + 1)] for i in range(2)]
            out_dt = F32 if final else BF16

            @blk.sync
            def _(sy):
                barrier_wait(sy)
                if plain:
                    for ic in range(NCH):
                        sy.dma_start(out=xs[:, ic * TPC:(ic + 1) * TPC],
                                     in_=x_src[ic]).then_inc(ld, 16)
                for oc in range(32):
                    if oc >= 2:
                        if mode == "qffn":
                            sy.wait_ge(wcst, oc - 1)
                        else:
                            sy.wait_ge(mmd, 2 * oc - 2)
                    sy.dma_start(out=wbuf[oc % 2][:].bitcast(I32 if mode == "qffn" else F32),
                                 in_=w_src[oc]).then_inc(wld, 16)
                sy.sem_inc(pb, 1)

            ld_all = 16 * NCH if plain else 0

            @blk.tensor
            def _(t):
                barrier_wait(t)
                t.wait_ge(ld, ld_all)
                for oc in range(32):
                    if mode == "qffn":
                        t.wait_ge(wcst, oc + 1)
                        wv_ = wb16[oc % 2][:]
                    else:
                        t.wait_ge(wld, 16 * (oc + 1))
                        wv_ = wbuf[oc % 2][:].bitcast(BF16)[:, 1:2 * D:2]
                    for tb in range(2):
                        g = 2 * oc + tb
                        if g >= 4:
                            t.wait_ge(pev, g - 3)
                        bank = ps[g % 4]
                        mm = None
                        for ic in range(NCH):
                            mm = t.matmul(out=bank[:],
                                          lhsT=wv_[:, 128 * ic:128 * (ic + 1)],
                                          rhs=xs[:, ic * TPC + 512 * tb:
                                                 ic * TPC + 512 * tb + 512],
                                          start=(ic == 0), stop=(ic == NCH - 1))
                        c.inc(mm, mmd)
                t.wait_ge(pev, 64)
                for tb in range(2):
                    if tb == 1:
                        t.wait_ge(rowd, 2)
                    for ic in range(NCH):
                        t.matmul(out=ps_mu[0:1, :], lhsT=ones_b[:],
                                 rhs=xr[:, ic * TPC + 512 * tb:ic * TPC + 512 * tb + 512],
                                 start=(ic == 0), stop=(ic == NCH - 1))
                    for ic in range(NCH):
                        k = NCH * tb + ic
                        t.wait_ge(sqd, k + 1)
                        c.inc(t.matmul(out=ps_m2[0:1, :], lhsT=ones_b[:],
                                       rhs=sqv[ic % 2][:], start=(ic == 0),
                                       stop=(ic == NCH - 1)), sm2)
                    t.wait_ge(rowd, 2 * tb + 2)
                    t.matmul(out=ps[2 * tb][:], lhsT=ones_r[:], rhs=r1[:],
                             start=True, stop=True)
                    c.inc(t.matmul(out=ps[2 * tb + 1][:], lhsT=ones_r[:], rhs=r3[:],
                                   start=True, stop=True), bcd)
                t.sem_inc(pb, 1)

            @blk.scalar
            def _(sc):
                barrier_wait(sc)
                sc.wait_ge(pev, 64)
                for tb in range(2):
                    for ic in range(NCH):
                        k = NCH * tb + ic
                        if k >= 2:
                            sc.wait_ge(sm2, k - 1)
                        c.inc(sc.activation(out=sqv[ic % 2][:],
                                            in_=xr[:, ic * TPC + 512 * tb:
                                                   ic * TPC + 512 * tb + 512],
                                            func=AF.Square), sqd)
                    sc.wait_ge(rowv, tb + 1)
                    c.inc(sc.activation(out=r3[:], in_=r2[:], func=AF.Sqrt,
                                        bias=eps_s[0:1, :], scale=1.0), ssx)
                sc.sem_inc(pb, 1)

            @blk.vector
            def _(v):
                barrier_wait(v)
                if mode == "qffn":
                    for oc in range(2):
                        v.wait_ge(wld, 16 * (oc + 1))
                        c.inc(v.tensor_copy(out=wb16[oc % 2][:],
                                            in_=wbuf[oc % 2][:].bitcast(I32)), wcst)
                for oc in range(32):
                    for tb in range(2):
                        g = 2 * oc + tb
                        v.wait_ge(mmd, g + 1)
                        bank = ps[g % 4]
                        cols = slice(oc * TPC + 512 * tb, oc * TPC + 512 * tb + 512)
                        if mode == "qffn":
                            i1 = c.inc(v.tensor_scalar(
                                out=u_s[:], in0=bank[:], scalar1=vsc_s[:, oc:oc + 1],
                                scalar2=vbi_s[:, oc:oc + 1], op0=ALU.mult,
                                op1=ALU.add), svx)
                            v.wait_ge(svx, i1)
                            c.inc(v.tensor_tensor(out=xr[:, cols], in0=u_s[:],
                                                  in1=strm[:, cols], op=ALU.add), pev)
                        else:
                            c.inc(v.tensor_tensor(out=xr[:, cols], in0=bank[:],
                                                  in1=strm[:, cols], op=ALU.add), pev)
                    if mode == "qffn" and oc + 2 < 32:
                        v.wait_ge(wld, 16 * (oc + 3))
                        c.inc(v.tensor_copy(out=wb16[oc % 2][:],
                                            in_=wbuf[oc % 2][:].bitcast(I32)), wcst)
                for tb in range(2):
                    v.wait_ge(sm2, NCH * (tb + 1))
                    i1 = c.inc(v.tensor_scalar(out=r1[:], in0=ps_mu[0:1, :],
                                               scalar1=-1.0 / D, scalar2=None,
                                               op0=ALU.mult), svx)
                    i2 = c.inc(v.tensor_scalar(out=r2[:], in0=ps_m2[0:1, :],
                                               scalar1=1.0 / D, scalar2=None,
                                               op0=ALU.mult), svx)
                    v.wait_ge(svx, i2)
                    i3 = c.inc(v.tensor_tensor(out=r3[:], in0=r1[:], in1=r1[:],
                                               op=ALU.mult), svx)
                    v.wait_ge(svx, i3)
                    i4 = c.inc(v.scalar_tensor_tensor(out=r2[:], in0=r3[:],
                                                      scalar=-1.0, in1=r2[:],
                                                      op0=ALU.mult, op1=ALU.add), svx)
                    v.wait_ge(svx, i4)
                    c.inc(v.tensor_copy(out=r2[:], in_=r2[:]), rowv)
                    v.wait_ge(ssx, tb + 1)
                    i5 = c.inc(v.reciprocal(out=r3[:], in_=r3[:]), svx)
                    v.wait_ge(svx, i5)
                    c.inc(v.tensor_copy(out=r1[:], in_=r1[:]), rowd)
                    c.inc(v.tensor_copy(out=r3[:], in_=r3[:]), rowd)
                v.wait_ge(bcd, 2)
                for ic in range(NCH):
                    if final and ic >= 2:
                        v.wait_ge(sto, 16 * (ic - 1))
                    for tb in range(2):
                        cols = slice(ic * TPC + 512 * tb, ic * TPC + 512 * tb + 512)
                        j1 = c.inc(v.tensor_tensor(out=u_s[:], in0=xr[:, cols],
                                                   in1=ps[2 * tb][:], op=ALU.add), svx)
                        v.wait_ge(svx, j1)
                        lastv = v.tensor_tensor(
                            out=(uov[ic % 2][:, 512 * tb:512 * tb + 512] if final
                                 else strm[:, cols]),
                            in0=u_s[:], in1=ps[2 * tb + 1][:], op=ALU.mult)
                        if tb == 1:
                            c.inc(lastv, appd)
                v.sem_inc(pb, 1)

            @blk.gpsimd
            def _(gp):
                barrier_wait(gp)
                if final:
                    for ic in range(NCH):
                        gp.wait_ge(appd, ic + 1)
                        gp.dma_start(out=out_dst[ic], in_=uov[ic % 2][:]).then_inc(sto, 16)
                    gp.wait_ge(sto, 16 * NCH)
                else:
                    gp.wait_ge(appd, NCH)
                gp.sem_inc(pb, 1)
            phase_n[0] += 1

        def emit_ffn1(x_src, out_dst):
            p = "f1"
            ld = c.sem(p + "ld")
            ldab = c.sem(p + "ldab")
            abct = c.sem(p + "abct")
            rt = c.sem(p + "rt")
            rv = c.sem(p + "rv")
            wld = c.sem(p + "wld")
            mmd = c.sem(p + "mmd")
            pev = c.sem(p + "pev")
            sqd = c.sem(p + "sqd")
            sm2 = c.sem(p + "sm2")
            rowv = c.sem(p + "rowv")
            ssx = c.sem(p + "ssx")
            rowd = c.sem(p + "rowd")
            bcd = c.sem(p + "bcd")
            appd = c.sem(p + "appd")
            sto = c.sem(p + "sto")
            svx = gsvx
            xs = S_XR
            xr = S_XS
            strm = S_XR
            wbuf = [S_W[:, 0:4096], S_W[:, 4096:8192]]
            a_bf = AB[:, 0:4096]
            b_bf = AB[:, 4096:8192]

            @blk.sync
            def _(sy):
                barrier_wait(sy)
                sy.dma_start(out=wbuf[0][:], in_=a_d).then_inc(ldab, 16)
                sy.dma_start(out=wbuf[1][:], in_=b_d).then_inc(ldab, 16)
                sy.wait_ge(abct, 2)
                for oc in range(32):
                    if oc >= 2:
                        sy.wait_ge(mmd, 2 * oc - 2)
                    sy.dma_start(out=wbuf[oc % 2][:], in_=wb_d[oc]).then_inc(wld, 16)
                sy.sem_inc(pb, 1)

            @blk.tensor
            def _(t):
                barrier_wait(t)
                t.wait_ge(abct, 2)
                for tt in range(8):
                    bank = [ps_a, ps_b][tt % 2]
                    if tt >= 2:
                        t.wait_ge(rv, tt - 1)
                    mm = None
                    for ic in range(NCH):
                        mm = t.matmul(out=bank[:, 0:8],
                                      lhsT=xs[:, ic * TPC + 128 * tt:ic * TPC + 128 * (tt + 1)],
                                      rhs=wr_s[:].bitcast(BF16)[:, 1:512:2][:, 8 * ic:8 * (ic + 1)],
                                      start=(ic == 0), stop=(ic == NCH - 1))
                    c.inc(mm, rt)
                for tt in range(8):
                    t.wait_ge(rv, 8 + tt + 1)
                    bank = [ps_a, ps_b][tt % 2]
                    if tt >= 2:
                        t.wait_ge(rv, 16 + tt - 1)
                    c.inc(t.transpose(out=bank[0:8, 0:128],
                                      in_=wts_s[:, 8 * tt:8 * (tt + 1)],
                                      identity=idf_s[:]), rt)
                for tb in range(2):
                    t.wait_ge(rv, 24)
                    bank = [ps_a, ps_b][tb]
                    c.inc(t.matmul(out=bank[:], lhsT=d2_s2[:],
                                   rhs=wt_s[0:8, 512 * tb:512 * (tb + 1)],
                                   start=True, stop=True), rt)
                for tb in range(2):
                    t.wait_ge(rv, 24 + tb + 1)
                    bank = [ps_a, ps_b][tb]
                    mm = None
                    for ic in range(NCH):
                        mm = t.matmul(out=bank[:],
                                      lhsT=a_bf[:, 128 * ic:128 * (ic + 1)],
                                      rhs=xs[:, ic * TPC + 512 * tb:ic * TPC + 512 * tb + 512],
                                      start=(ic == 0), stop=(ic == NCH - 1))
                    c.inc(mm, rt)
                for oc in range(32):
                    t.wait_ge(wld, 16 * (oc + 1))
                    wv_ = wbuf[oc % 2][:].bitcast(BF16)[:, 1:2 * D:2]
                    for tb in range(2):
                        g = 2 * oc + tb
                        if g >= 4:
                            t.wait_ge(pev, g - 3)
                        if oc == 0:
                            t.wait_ge(rv, 26 + tb + 1)
                        bank = ps[g % 4]
                        for ic in range(NCH):
                            t.matmul(out=bank[:],
                                     lhsT=wv_[:, 128 * ic:128 * (ic + 1)],
                                     rhs=xs[:, ic * TPC + 512 * tb:
                                            ic * TPC + 512 * tb + 512],
                                     start=(ic == 0), stop=False)
                        c.inc(t.matmul(out=bank[:],
                                       lhsT=b_bf[:, 128 * oc:128 * (oc + 1)],
                                       rhs=g_s[tb][:], start=False, stop=True), mmd)
                t.wait_ge(pev, 64)
                for tb in range(2):
                    if tb == 1:
                        t.wait_ge(rowd, 2)
                    for ic in range(NCH):
                        t.matmul(out=ps_mu[0:1, :], lhsT=ones_b[:],
                                 rhs=xr[:, ic * TPC + 512 * tb:ic * TPC + 512 * tb + 512],
                                 start=(ic == 0), stop=(ic == NCH - 1))
                    for ic in range(NCH):
                        k = NCH * tb + ic
                        t.wait_ge(sqd, k + 1)
                        c.inc(t.matmul(out=ps_m2[0:1, :], lhsT=ones_b[:],
                                       rhs=sq_w[ic % 2][:], start=(ic == 0),
                                       stop=(ic == NCH - 1)), sm2)
                    t.wait_ge(rowd, 2 * tb + 2)
                    t.matmul(out=ps[2 * tb][:], lhsT=ones_r[:], rhs=r1[:],
                             start=True, stop=True)
                    c.inc(t.matmul(out=ps[2 * tb + 1][:], lhsT=ones_r[:], rhs=r3[:],
                                   start=True, stop=True), bcd)
                t.sem_inc(pb, 1)

            @blk.vector
            def _(v):
                barrier_wait(v)
                v.wait_ge(ldab, 32)
                c.inc(v.tensor_copy(out=a_bf[:], in_=wbuf[0][:]), abct)
                c.inc(v.tensor_copy(out=b_bf[:], in_=wbuf[1][:]), abct)
                for tt in range(8):
                    v.wait_ge(rt, tt + 1)
                    bank = [ps_a, ps_b][tt % 2]
                    ecols = slice(8 * tt, 8 * (tt + 1))
                    i1 = c.inc(v.tensor_reduce(out=m1_s[:], in_=bank[:, 0:8],
                                               axis=AX.X, op=ALU.max), svx)
                    v.wait_ge(svx, i1)
                    i2 = c.inc(v.tensor_scalar(out=mneg_s[:], in0=m1_s[:],
                                               scalar1=-1.0, scalar2=None,
                                               op0=ALU.mult), svx)
                    v.wait_ge(svx, i2)
                    i3 = c.inc(v.tensor_scalar(out=lgt_s[:, ecols], in0=bank[:, 0:8],
                                               scalar1=mneg_s[:, :], scalar2=None,
                                               op0=ALU.add), svx)
                    v.wait_ge(svx, i3)
                    c.inc(v.tensor_copy(out=lgt_s[:, ecols], in_=lgt_s[:, ecols]), rv)
                for tt in range(8):
                    v.wait_ge(ssx, tt + 1)
                    ecols = slice(8 * tt, 8 * (tt + 1))
                    i1 = c.inc(v.tensor_reduce(out=m1_s[:], in_=e_s[:, ecols],
                                               axis=AX.X, op=ALU.max), svx)
                    v.wait_ge(svx, i1)
                    i2 = c.inc(v.tensor_scalar(out=msk_s[:], in0=e_s[:, ecols],
                                               scalar1=m1_s[:, :], scalar2=None,
                                               op0=ALU.is_ge), svx)
                    v.wait_ge(svx, i2)
                    i3 = c.inc(v.scalar_tensor_tensor(out=wts_s[:, ecols], in0=msk_s[:],
                                                      scalar=-1000.0, in1=e_s[:, ecols],
                                                      op0=ALU.mult, op1=ALU.add), svx)
                    v.wait_ge(svx, i3)
                    i4 = c.inc(v.tensor_reduce(out=m2_s[:], in_=wts_s[:, ecols],
                                               axis=AX.X, op=ALU.max), svx)
                    v.wait_ge(svx, i4)
                    i5 = c.inc(v.tensor_tensor(out=rcpr_s[:], in0=m1_s[:], in1=m2_s[:],
                                               op=ALU.add), svx)
                    v.wait_ge(svx, i5)
                    i6 = c.inc(v.reciprocal(out=rcpr_s[:], in_=rcpr_s[:]), svx)
                    i7 = c.inc(v.tensor_scalar(out=msk_s[:], in0=e_s[:, ecols],
                                               scalar1=m2_s[:, :], scalar2=None,
                                               op0=ALU.is_ge), svx)
                    v.wait_ge(svx, i7)
                    i8 = c.inc(v.tensor_tensor(out=msk_s[:], in0=msk_s[:],
                                               in1=e_s[:, ecols], op=ALU.mult), svx)
                    v.wait_ge(svx, max(i6, i8))
                    c.inc(v.tensor_scalar(out=wts_s[:, ecols], in0=msk_s[:],
                                          scalar1=rcpr_s[:, :], scalar2=None,
                                          op0=ALU.mult), rv)
                for tt in range(8):
                    v.wait_ge(rt, 8 + tt + 1)
                    bank = [ps_a, ps_b][tt % 2]
                    c.inc(v.tensor_copy(out=wt_s[0:8, 128 * tt:128 * (tt + 1)],
                                        in_=bank[0:8, 0:128]), rv)
                for tb in range(2):
                    v.wait_ge(rt, 16 + tb + 1)
                    bank = [ps_a, ps_b][tb]
                    c.inc(v.tensor_copy(out=wb_s[:, 512 * tb:512 * (tb + 1)],
                                        in_=bank[:]), rv)
                for tb in range(2):
                    v.wait_ge(rt, 18 + tb + 1)
                    v.wait_ge(rv, 26)
                    bank = [ps_a, ps_b][tb]
                    c.inc(v.tensor_tensor(out=g_s[tb][:], in0=bank[:],
                                          in1=wb_s[:, 512 * tb:512 * (tb + 1)],
                                          op=ALU.mult), rv)
                for oc in range(32):
                    for tb in range(2):
                        g = 2 * oc + tb
                        v.wait_ge(mmd, g + 1)
                        bank = ps[g % 4]
                        cols = slice(oc * TPC + 512 * tb, oc * TPC + 512 * tb + 512)
                        i1 = c.inc(v.tensor_scalar(
                            out=u_s[:], in0=bank[:], scalar1=fb_s[:, oc:oc + 1],
                            scalar2=None, op0=ALU.add), svx)
                        v.wait_ge(svx, i1)
                        c.inc(v.tensor_tensor(out=xr[:, cols], in0=u_s[:],
                                              in1=strm[:, cols], op=ALU.add), pev)
                for tb in range(2):
                    v.wait_ge(sm2, NCH * (tb + 1))
                    i1 = c.inc(v.tensor_scalar(out=r1[:], in0=ps_mu[0:1, :],
                                               scalar1=-1.0 / D, scalar2=None,
                                               op0=ALU.mult), svx)
                    i2 = c.inc(v.tensor_scalar(out=r2[:], in0=ps_m2[0:1, :],
                                               scalar1=1.0 / D, scalar2=None,
                                               op0=ALU.mult), svx)
                    v.wait_ge(svx, i2)
                    i3 = c.inc(v.tensor_tensor(out=r3[:], in0=r1[:], in1=r1[:],
                                               op=ALU.mult), svx)
                    v.wait_ge(svx, i3)
                    i4 = c.inc(v.scalar_tensor_tensor(out=r2[:], in0=r3[:],
                                                      scalar=-1.0, in1=r2[:],
                                                      op0=ALU.mult, op1=ALU.add), svx)
                    v.wait_ge(svx, i4)
                    c.inc(v.tensor_copy(out=r2[:], in_=r2[:]), rowv)
                    v.wait_ge(ssx, 8 + tb + 1)
                    i5 = c.inc(v.reciprocal(out=r3[:], in_=r3[:]), svx)
                    v.wait_ge(svx, i5)
                    c.inc(v.tensor_copy(out=r1[:], in_=r1[:]), rowd)
                    c.inc(v.tensor_copy(out=r3[:], in_=r3[:]), rowd)
                v.wait_ge(bcd, 2)
                for ic in range(NCH):
                    if ic >= 2:
                        v.wait_ge(sto, 16 * (ic - 1))
                    for tb in range(2):
                        cols = slice(ic * TPC + 512 * tb, ic * TPC + 512 * tb + 512)
                        j1 = c.inc(v.tensor_tensor(out=u_s[:], in0=xr[:, cols],
                                                   in1=ps[2 * tb][:], op=ALU.add), svx)
                        v.wait_ge(svx, j1)
                        lastv = v.tensor_tensor(out=uo_w[ic % 2][:, 512 * tb:512 * tb + 512],
                                                in0=u_s[:], in1=ps[2 * tb + 1][:],
                                                op=ALU.mult)
                        if tb == 1:
                            c.inc(lastv, appd)
                v.sem_inc(pb, 1)

            @blk.scalar
            def _(sc):
                barrier_wait(sc)
                for tt in range(8):
                    sc.wait_ge(rv, tt + 1)
                    ecols = slice(8 * tt, 8 * (tt + 1))
                    c.inc(sc.activation(out=e_s[:, ecols], in_=lgt_s[:, ecols],
                                        func=AF.Exp), ssx)
                sc.wait_ge(pev, 64)
                for tb in range(2):
                    for ic in range(NCH):
                        k = NCH * tb + ic
                        if k >= 2:
                            sc.wait_ge(sm2, k - 1)
                        c.inc(sc.activation(out=sq_w[ic % 2][:],
                                            in_=xr[:, ic * TPC + 512 * tb:
                                                   ic * TPC + 512 * tb + 512],
                                            func=AF.Square), sqd)
                    sc.wait_ge(rowv, tb + 1)
                    c.inc(sc.activation(out=r3[:], in_=r2[:], func=AF.Sqrt,
                                        bias=eps_s[0:1, :], scale=1.0), ssx)
                sc.sem_inc(pb, 1)

            @blk.gpsimd
            def _(gp):
                barrier_wait(gp)
                for ic in range(NCH):
                    gp.wait_ge(appd, ic + 1)
                    gp.dma_start(out=out_dst[ic], in_=uo_w[ic % 2][:]).then_inc(sto, 16)
                gp.wait_ge(sto, 16 * NCH)
            phase_n[0] += 1

        emit_const_loads()
        emit_qkv(0)
        emit_attn(0)
        emit_projln("o0", "plain", at_h[0], xs0_h, w_l0["wo0"], xh[1])
        emit_projln("qf", "qffn", xh[1], xh[1], qw_d, xh[2])
        emit_qkv(1)
        emit_attn(1)
        emit_projln("o1", "plain", at_h[1], xh[2], w_l1["wo1"], xh[3])
        emit_ffn1(xh[3], x_o)

    return nc


def kernel_fused(L0_attn_x, L0_attn_Wq, L0_attn_Wk, L0_attn_Wv, L0_attn_Wo,
                 L0_ffn_quantized_weights, L0_ffn_scales, L0_ffn_bias,
                 L1_attn_wq, L1_attn_wk, L1_attn_wv, L1_attn_wo,
                 L1_ffn_W_base, L1_ffn_b_base, L1_ffn_W_router,
                 L1_ffn_lora_A, L1_ffn_lora_B):
    cores = list(range(NCORES))
    toks = [core_tokens(j) for j in cores]
    ident_bf = np.ascontiguousarray(np.eye(128, dtype=np.float32).astype(bfloat16))
    ident_f = np.ascontiguousarray(np.eye(128, dtype=np.float32))
    ones_bm = np.ones((128, 128), bfloat16)
    ones_col = np.ones((128, 1), bfloat16)
    ones_row = np.ones((1, 128), np.float32)
    epsv = np.full((1, 1), LN_EPS, np.float32)
    dpat = diag_patterns()
    cos0, sin0 = rope_tables_l0(S)
    cos1, sin1 = rope_tables_l1(S)
    rope0 = [rope_fm(cos0, sin0, toks[j]) for j in cores]
    rope1 = [rope_fm(cos1, sin1, toks[j]) for j in cores]

    wq0 = colchunk(np.ascontiguousarray(L0_attn_Wq.T))
    wk0 = colchunk(np.ascontiguousarray(L0_attn_Wk.T))
    wv0 = colchunk(np.ascontiguousarray(L0_attn_Wv.T))
    wo0 = colchunk(np.ascontiguousarray(L0_attn_Wo.T))
    qw0 = colchunk(np.ascontiguousarray(L0_ffn_quantized_weights.T))
    sc0 = np.ascontiguousarray(L0_ffn_scales.reshape(32, 128).T)
    bi0 = np.ascontiguousarray(L0_ffn_bias.reshape(32, 128).T)
    wq1 = colchunk(np.ascontiguousarray(L1_attn_wq))
    wk1 = colchunk(np.ascontiguousarray(L1_attn_wk))
    wv1 = colchunk(np.ascontiguousarray(L1_attn_wv))
    wo1 = colchunk(np.ascontiguousarray(L1_attn_wo))
    wb1 = colchunk(np.ascontiguousarray(L1_ffn_W_base.T))
    fb1 = np.ascontiguousarray(L1_ffn_b_base.reshape(32, 128).T)
    wr1 = np.ascontiguousarray(
        L1_ffn_W_router.reshape(8, 32, 128).transpose(2, 1, 0).reshape(128, 256))
    la1 = np.ascontiguousarray(
        L1_ffn_lora_A.reshape(128, D).T.reshape(32, 128, 128)
        .transpose(1, 0, 2).reshape(128, 4096))
    lb1 = np.ascontiguousarray(L1_ffn_lora_B.transpose(0, 2, 1).reshape(128, D))
    d2 = np.zeros((8, 128), np.float32)
    for e in range(8):
        d2[e, 16 * e:16 * (e + 1)] = 2.0
    x_fm = [np.ascontiguousarray(L0_attn_x[j // 4].T[:, toks[j]]) for j in cores]

    nc = _prog("fused", build_fused)
    maps = []
    for j in cores:
        maps.append({
            "x": _stack(x_fm[j]), "wq0": wq0, "wk0": wk0, "wv0": wv0, "wo0": wo0,
            "qw": qw0, "scales": sc0, "qbias": bi0,
            "wq1": wq1, "wk1": wk1, "wv1": wv1, "wo1": wo1,
            "wbase": wb1, "fbias": fb1, "wr": wr1, "la": la1, "lb": lb1, "d2": d2,
            "cos0": rope0[j][0], "sin0": rope0[j][1],
            "cos1": rope1[j][0], "sin1": rope1[j][1],
            "ident": ident_bf, "idf": ident_f, "onesb": ones_col,
            "onesbm": ones_bm, "onesr": ones_row, "epsv": epsv,
            "dpat": dpat, "ebias": fused_ebias(j),
        })
    res = run_bass_kernel_spmd(nc, maps, core_ids=cores)
    out = np.empty((B, S, D), np.float32)
    for j in cores:
        b = j // 4
        out[b, toks[j], :] = res.results[j]["xout"].reshape(D, TPC).T
    return out


def kernel(**inputs):
    return kernel_fused(**inputs)


# version 10
# speedup vs baseline: 1.0106x; 1.0106x over previous
import numpy as np
import ml_dtypes
from contextlib import ExitStack

import concourse.bass as bass
import concourse.mybir as mybir
from concourse.bass_utils import run_bass_kernel_spmd

BF16 = mybir.dt.bfloat16
F32 = mybir.dt.float32
I32 = mybir.dt.int32
AF = mybir.ActivationFunctionType
ALU = mybir.AluOpType
AX = mybir.AxisListType
bfloat16 = ml_dtypes.bfloat16

NCORES = 8
B = 2
S = 4096
D = 4096
HD = 128
TPC = 1024
NCH = D // 128
NKT = (20, 32)
NT = NKT[0] + NKT[1]
EXP_SHIFT = -30.0
PAD_BIAS = -1.0e9
LN_EPS = 1e-5


def core_chunks(j):
    q = j % 4
    return q, 7 - q


def core_tokens(j):
    c0, c1 = core_chunks(j)
    return np.concatenate([np.arange(512 * c0, 512 * (c0 + 1)),
                           np.arange(512 * c1, 512 * (c1 + 1))])


def colchunk(Weff):
    I, O = Weff.shape
    return np.ascontiguousarray(
        Weff.reshape(I // 128, 128, O // 128, 128).transpose(2, 1, 0, 3)
        .reshape(O // 128, 128, I))


def rope_tables_l0(n):
    inv = 1.0 / (10000.0 ** (np.arange(0, HD, 2, dtype=np.float32) / HD))
    fr = np.arange(n, dtype=np.float32)[:, None] * inv[None, :]
    emb = np.concatenate([fr, fr], -1)
    return np.cos(emb).astype(np.float32), np.sin(emb).astype(np.float32)


def rope_tables_l1(n):
    TH, FA, HI, LO, OM = 500000.0, 8.0, 4.0, 1.0, 8192
    inv = 1.0 / TH ** (np.arange(0, HD, 2, dtype=np.float64) / HD)
    wl = 2 * np.pi / inv
    sm = (OM / wl - LO) / (HI - LO)
    sc = np.where(wl > OM / LO, inv / FA, inv)
    mid = (wl <= OM / LO) & (wl >= OM / HI)
    sc = np.where(mid, (1 - sm) * inv / FA + sm * inv, sc).astype(np.float32)
    fr = np.arange(n, dtype=np.float32)[:, None] * sc[None, :]
    emb = np.concatenate([fr, fr], -1)
    return np.cos(emb).astype(np.float32), np.sin(emb).astype(np.float32)


def rope_fm(cos, sin, toks):
    c = np.ascontiguousarray(cos[toks, :].T)
    s = sin[toks, :].T.copy()
    s[0:64, :] *= -1.0
    return c, np.ascontiguousarray(s)


def diag_patterns():
    pats = np.zeros((4, 128, 512), np.float32)
    r = np.arange(128)[:, None]
    cc = np.arange(128)[None, :]
    tri = (r <= cc).astype(np.float32)
    for v in range(4):
        pats[v, :, 128 * v:128 * (v + 1)] = tri
        pats[v, :, 128 * (v + 1):] = 1.0
    return np.ascontiguousarray(pats.astype(bfloat16))


def pack_kv(core_j, full_fm):
    c0, c1 = core_chunks(core_j)
    Hkv = full_fm.shape[0]
    out = np.zeros((Hkv, 128, 128 * NT), bfloat16)
    bias = np.full(NT, PAD_BIAS, np.float32)
    for qb, ch in ((0, c0), (1, c1)):
        base = 0 if qb == 0 else NKT[0]
        nfull = 4 * ch
        out[:, :, 128 * base:128 * (base + nfull)] = full_fm[:, :, :128 * nfull]
        bias[base:base + nfull] = EXP_SHIFT
        dbase = base + NKT[qb] - 4
        out[:, :, 128 * dbase:128 * (dbase + 4)] = \
            full_fm[:, :, 128 * nfull:128 * (nfull + 4)]
        bias[dbase:dbase + 4] = EXP_SHIFT
    return out, bias



class Ctx:
    def __init__(self):
        self.nc = bass.Bass("TRN2", target_bir_lowering=False)
        self.es = ExitStack()
        self.cnt = {}

    def sbuf(self, name, shape, dt):
        return self.es.enter_context(self.nc.sbuf_tensor(name, shape, dt))

    def psum(self, name, shape, dt):
        return self.es.enter_context(self.nc.psum_tensor(name, shape, dt))

    def sem(self, name):
        h = self.es.enter_context(self.nc.semaphore(name))
        self.cnt[id(h)] = 0
        return h

    def inc(self, inst, semh, n=1):
        self.cnt[id(semh)] += n
        inst.then_inc(semh, n)
        return self.cnt[id(semh)]

    def val(self, semh):
        return self.cnt[id(semh)]


def bfview(t, cols):
    return t[:].bitcast(BF16)[:, 1:2 * cols:2]



def build_qkv(layer, only=None):
    nkv = 32 if layer == 0 else 8
    c = Ctx()
    nc = c.nc
    x_in = nc.dram_tensor("x", [NCH, 128, TPC], F32 if layer == 0 else BF16,
                          kind="ExternalInput").ap()
    wq_d = nc.dram_tensor("wq", [32, 128, D], F32, kind="ExternalInput").ap()
    wk_d = nc.dram_tensor("wk", [nkv, 128, D], F32, kind="ExternalInput").ap()
    wv_d = nc.dram_tensor("wv", [nkv, 128, D], F32, kind="ExternalInput").ap()
    cos_d = nc.dram_tensor("cos", [128, TPC], F32, kind="ExternalInput").ap()
    sin_d = nc.dram_tensor("sinS", [128, TPC], F32, kind="ExternalInput").ap()
    id_d = nc.dram_tensor("ident", [128, 128], BF16, kind="ExternalInput").ap()
    q_o = nc.dram_tensor("q", [32, 128, TPC], BF16, kind="ExternalOutput").ap()
    k_o = nc.dram_tensor("k", [nkv, 128, TPC], BF16, kind="ExternalOutput").ap()
    v_o = nc.dram_tensor("v", [nkv, 128, TPC], BF16, kind="ExternalOutput").ap()
    xs_o = None
    if layer == 0:
        xs_o = nc.dram_tensor("xs", [NCH, 128, TPC], BF16, kind="ExternalOutput").ap()

    projs = [("v", wv_d, nkv), ("k", wk_d, nkv), ("q", wq_d, 32)]
    if only is not None:
        projs = [p for p in projs if p[0] in only]

    with c.es:
        xs = c.sbuf("xs_s", [128, NCH * TPC], BF16)
        xl = c.sbuf("xl", [128, 2 * TPC], F32)
        wbuf = [c.sbuf("w0", [128, D], F32), c.sbuf("w1", [128, D], F32)]
        cos_s = c.sbuf("cos_s", [128, TPC], F32)
        sin_s = c.sbuf("sin_s", [128, TPC], F32)
        id_s = c.sbuf("id_s", [128, 128], BF16)
        tmp = c.sbuf("tmp", [128, 512], F32)
        u = c.sbuf("u", [128, 512], F32)
        qk_out = [c.sbuf("qo%d" % i, [128, TPC], BF16) for i in range(2)]
        vfm = [c.sbuf("vf%d" % i, [128, TPC], BF16) for i in range(2)]
        vtm = [c.sbuf("vt%d" % i, [128, TPC], BF16) for i in range(2)]
        ps = [c.psum("ps%d" % i, [128, 512], F32) for i in range(4)]
        ps_tr = [c.psum("ps_tr%d" % i, [128, 1024], BF16) for i in range(2)]

        ld = c.sem("ld")
        ldx = c.sem("ldx")
        wld = c.sem("wld")
        wfree = c.sem("wfree")
        mmd = c.sem("mmd")
        pev = c.sem("pev")
        xca = c.sem("xca")
        trd = c.sem("trd")
        tre = c.sem("tre")
        sto = c.sem("sto")
        svx = c.sem("svx")

        blk = c.es.enter_context(nc.Block())

        n_xs_st = NCH if layer == 0 else 0
        store_base = {}
        _si = n_xs_st
        for _pn, _wd, _noc in projs:
            store_base[_pn] = _si
            _si += _noc

        @blk.sync
        def _(sy):
            sy.dma_start(out=cos_s[:], in_=cos_d).then_inc(ld, 16)
            sy.dma_start(out=sin_s[:], in_=sin_d).then_inc(ld, 16)
            sy.dma_start(out=id_s[:], in_=id_d).then_inc(ld, 16)
            for ic in range(NCH):
                if layer == 0:
                    if ic >= 2:
                        sy.wait_ge(xca, ic - 1)
                    sy.dma_start(out=xl[:, (ic % 2) * TPC:(ic % 2 + 1) * TPC],
                                 in_=x_in[ic]).then_inc(ldx, 16)
                else:
                    sy.dma_start(out=xs[:, ic * TPC:(ic + 1) * TPC],
                                 in_=x_in[ic]).then_inc(ldx, 16)
            si = 0
            for _pn, wd, noc in projs:
                for oc in range(noc):
                    if si >= 2:
                        sy.wait_ge(pev, 2 * si - 2)
                    sy.dma_start(out=wbuf[si % 2][:], in_=wd[oc]).then_inc(wld, 16)
                    si += 1

        if layer == 0:
            @blk.vector
            def _(v):
                for ic in range(NCH):
                    v.wait_ge(ldx, 16 * (ic + 1))
                    c.inc(v.tensor_copy(out=xs[:, ic * TPC:(ic + 1) * TPC],
                                        in_=xl[:, (ic % 2) * TPC:(ic % 2 + 1) * TPC]),
                          xca)

        @blk.tensor
        def _(t):
            if layer == 0:
                t.wait_ge(xca, NCH)
            else:
                t.wait_ge(ldx, 16 * NCH)
            si = 0
            g = 0

            def v_transpose(t, oc):
                t.wait_ge(pev, 2 * oc + 2)
                if oc >= 2:
                    t.wait_ge(tre, oc - 1)
                tr = None
                for tt in range(8):
                    tr = t.transpose(out=ps_tr[oc % 2][:, 128 * tt:128 * (tt + 1)],
                                     in_=vfm[oc % 2][:, 128 * tt:128 * (tt + 1)],
                                     identity=id_s[:])
                c.inc(tr, trd)

            for _pn, wd, noc in projs:
                for oc in range(noc):
                    t.wait_ge(wld, 16 * (si + 1))
                    wv_ = bfview(wbuf[si % 2], D)
                    for tb in range(2):
                        if g >= 4:
                            t.wait_ge(pev, g - 3)
                        bank = ps[g % 4]
                        mm = None
                        for ic in range(NCH):
                            mm = t.matmul(out=bank[:],
                                          lhsT=wv_[:, 128 * ic:128 * (ic + 1)],
                                          rhs=xs[:, ic * TPC + 512 * tb:
                                                 ic * TPC + 512 * tb + 512],
                                          start=(ic == 0), stop=(ic == NCH - 1))
                        c.inc(mm, mmd)
                        g += 1
                    si += 1
                    if _pn == "v" and oc >= 2:
                        v_transpose(t, oc - 2)
                if _pn == "v":
                    v_transpose(t, nkv - 2)
                    v_transpose(t, nkv - 1)

        @blk.vector
        def _(v):
            v.wait_ge(ld, 48)
            g = 0

            def vtm_evac(v, oc):
                v.wait_ge(trd, oc + 1)
                if oc >= 2:
                    v.wait_ge(sto, 16 * (store_base["v"] + oc - 1))
                c.inc(v.tensor_copy(out=vtm[oc % 2][:], in_=ps_tr[oc % 2][:]), tre)

            if any(p[0] == "v" for p in projs):
                for oc in range(nkv):
                    for tb in range(2):
                        v.wait_ge(mmd, g + 1)
                        bank = ps[g % 4]
                        cols = slice(512 * tb, 512 * tb + 512)
                        if oc >= 2 and tb == 0:
                            v.wait_ge(trd, oc - 1)
                        c.inc(v.tensor_copy(out=vfm[oc % 2][:, cols],
                                            in_=bank[:]), pev)
                        g += 1
                    if oc >= 2:
                        vtm_evac(v, oc - 2)
                vtm_evac(v, nkv - 2)
                vtm_evac(v, nkv - 1)
            for pn, noc in [(p[0], p[2]) for p in projs if p[0] != "v"]:
                for oc in range(noc):
                    for tb in range(2):
                        v.wait_ge(mmd, g + 1)
                        bank = ps[g % 4]
                        cols = slice(512 * tb, 512 * tb + 512)
                        if tb == 0 and oc >= 2:
                            v.wait_ge(sto, 16 * (store_base[pn] + oc - 1))
                        i1 = c.inc(v.tensor_tensor(
                            out=tmp[0:64, :], in0=bank[64:128, :],
                            in1=sin_s[0:64, cols], op=ALU.mult), svx)
                        i2 = c.inc(v.tensor_tensor(
                            out=tmp[64:128, :], in0=bank[0:64, :],
                            in1=sin_s[64:128, cols], op=ALU.mult), svx)
                        i3 = c.inc(v.tensor_tensor(
                            out=u[:], in0=bank[:], in1=cos_s[:, cols],
                            op=ALU.mult), svx)
                        v.wait_ge(svx, i3)
                        c.inc(v.tensor_tensor(
                            out=qk_out[oc % 2][:, cols], in0=u[:],
                            in1=tmp[:], op=ALU.add), pev)
                        g += 1

        @blk.gpsimd
        def _(gp):
            nst = 0
            if layer == 0:
                for ic in range(NCH):
                    gp.wait_ge(xca, ic + 1)
                    gp.dma_start(out=xs_o[ic],
                                 in_=xs[:, ic * TPC:(ic + 1) * TPC]).then_inc(sto, 16)
                    nst += 1
            base_g = 0
            for pn, _wd, noc in projs:
                if pn == "v":
                    for oc in range(nkv):
                        gp.wait_ge(tre, oc + 1)
                        gp.dma_start(out=v_o[oc], in_=vtm[oc % 2][:]).then_inc(sto, 16)
                        nst += 1
                else:
                    out_d = k_o if pn == "k" else q_o
                    for oc in range(noc):
                        gp.wait_ge(pev, base_g + 2 * oc + 2)
                        gp.dma_start(out=out_d[oc], in_=qk_out[oc % 2][:]).then_inc(sto, 16)
                        nst += 1
                base_g += 2 * noc
            gp.wait_ge(sto, 16 * nst)

    return nc



def build_attn(layer):
    nkv = 32 if layer == 0 else 8
    rep = 32 // nkv
    c = Ctx()
    nc = c.nc
    q_in = nc.dram_tensor("q", [32, 128, TPC], BF16, kind="ExternalInput").ap()
    k_in = nc.dram_tensor("kp", [nkv, 128, 128 * NT], BF16, kind="ExternalInput").ap()
    v_in = nc.dram_tensor("vp", [nkv, 128, 128 * NT], BF16, kind="ExternalInput").ap()
    bias_in = nc.dram_tensor("ebias", [128, NT], F32, kind="ExternalInput").ap()
    dpat_in = nc.dram_tensor("dpat", [4, 128, 512], BF16, kind="ExternalInput").ap()
    ones_in = nc.dram_tensor("onesb", [128, 128], BF16, kind="ExternalInput").ap()
    a_o = nc.dram_tensor("attn", [32, 128, TPC], BF16, kind="ExternalOutput").ap()

    def kt_seq():
        gkt = 0
        for h in range(32):
            for qb in range(2):
                for kt in range(NKT[qb]):
                    yield h, qb, kt, gkt, 2 * h + qb
                    gkt += 1

    with c.es:
        q_s = [c.sbuf("q%d" % i, [128, TPC], BF16) for i in range(2)]
        k_s = [c.sbuf("k%d" % i, [128, 128 * NT], BF16) for i in range(2)]
        v_s = [c.sbuf("v%d" % i, [128, 128 * NT], BF16) for i in range(2)]
        pt = [c.sbuf("pt%d" % i, [128, 512], BF16) for i in range(4)]
        bias_s = c.sbuf("bias_s", [128, NT], F32)
        dpat_s = c.sbuf("dpat_s", [128, 4 * 512], BF16)
        ones_s = c.sbuf("ones_s", [128, 128], BF16)
        rcp_s = c.sbuf("rcp_s", [128, 512], F32)
        ao_s = [c.sbuf("ao%d" % i, [128, TPC], BF16) for i in range(2)]
        ps_s = [c.psum("pss%d" % i, [128, 512], F32) for i in range(2)]
        ps_o = [c.psum("pso%d" % i, [128, 512], F32) for i in range(2)]
        ps_z = [c.psum("psz%d" % i, [128, 512], F32) for i in range(2)]

        ld = c.sem("ld")
        qld = c.sem("qld")
        kvld = c.sem("kvld")
        smm = c.sem("smm")
        sexp = c.sem("sexp")
        sdg = c.sem("sdg")
        spv = c.sem("spv")
        sev = c.sem("sev")
        sdone = c.sem("sdone")
        svx = c.sem("svx")

        blk = c.es.enter_context(nc.Block())

        @blk.sync
        def _(sy):
            sy.dma_start(out=bias_s[:], in_=bias_in).then_inc(ld, 16)
            for a_ in range(4):
                sy.dma_start(out=dpat_s[:, 512 * a_:512 * (a_ + 1)],
                             in_=dpat_in[a_]).then_inc(ld, 16)
            sy.dma_start(out=ones_s[:], in_=ones_in).then_inc(ld, 16)
            for h in range(32):
                if h >= 2:
                    sy.wait_ge(sev, 2 * (h - 1))
                sy.dma_start(out=q_s[h % 2][:], in_=q_in[h]).then_inc(qld, 16)
                if h % rep == 0:
                    kv = h // rep
                    if kv >= 2:
                        sy.wait_ge(sev, 2 * rep * (kv - 1))
                    sy.dma_start(out=k_s[kv % 2][:], in_=k_in[kv]).then_inc(kvld, 16)
                    sy.dma_start(out=v_s[kv % 2][:], in_=v_in[kv]).then_inc(kvld, 16)

        @blk.tensor
        def _(t):
            t.wait_ge(ld, 96)
            for h in range(32):
                t.wait_ge(qld, 16 * (h + 1))
                t.wait_ge(kvld, 32 * (h // rep + 1))
                kbuf = k_s[(h // rep) % 2]
                vbuf = v_s[(h // rep) % 2]
                for qb in range(2):
                    nkt = NKT[qb]
                    base = 0 if qb == 0 else NKT[0]
                    g2 = 2 * h + qb
                    gkt0 = 52 * h + base
                    qcols = slice(512 * qb, 512 * qb + 512)
                    if g2 >= 2:
                        t.wait_ge(sev, g2 - 1)

                    def scores(kt):
                        gkt = gkt0 + kt
                        if gkt >= 2:
                            t.wait_ge(sexp, gkt - 1)
                        c.inc(t.matmul(
                            out=ps_s[gkt % 2][:],
                            lhsT=kbuf[:, 128 * (base + kt):128 * (base + kt + 1)],
                            rhs=q_s[h % 2][:, qcols], start=True, stop=True), smm)

                    def pv(kt):
                        gkt = gkt0 + kt
                        if kt >= nkt - 4:
                            t.wait_ge(sdg, 4 * g2 + (kt - (nkt - 4)) + 1)
                        else:
                            t.wait_ge(sexp, gkt + 1)
                        ptile = pt[(base + kt) % 4]
                        t.matmul(out=ps_o[g2 % 2][:],
                                 lhsT=vbuf[:, 128 * (base + kt):128 * (base + kt + 1)],
                                 rhs=ptile[:], start=(kt == 0), stop=(kt == nkt - 1))
                        c.inc(t.matmul(out=ps_z[g2 % 2][:], lhsT=ones_s[:],
                                       rhs=ptile[:], start=(kt == 0),
                                       stop=(kt == nkt - 1)), spv)

                    scores(0)
                    for kt in range(1, nkt):
                        scores(kt)
                        pv(kt - 1)
                    pv(nkt - 1)

        @blk.scalar
        def _(sc):
            sc.wait_ge(ld, 96)
            for h, qb, kt, gkt, g2 in kt_seq():
                base = 0 if qb == 0 else NKT[0]
                sc.wait_ge(smm, gkt + 1)
                if gkt >= 4:
                    sc.wait_ge(spv, gkt - 3)
                c.inc(sc.activation(out=pt[(base + kt) % 4][:],
                                    in_=ps_s[gkt % 2][:], func=AF.Exp,
                                    bias=bias_s[:, base + kt:base + kt + 1],
                                    scale=float(HD) ** -0.5), sexp)

        @blk.vector
        def _(v):
            v.wait_ge(ld, 96)
            for h in range(32):
                if h >= 2:
                    v.wait_ge(sdone, 16 * (h - 1))
                for qb in range(2):
                    nkt = NKT[qb]
                    base = 0 if qb == 0 else NKT[0]
                    g2 = 2 * h + qb
                    gkt0 = 52 * h + base
                    for kt in range(nkt - 4, nkt):
                        vv = kt - (nkt - 4)
                        v.wait_ge(sexp, gkt0 + kt + 1)
                        ptile = pt[(base + kt) % 4]
                        c.inc(v.tensor_tensor(
                            out=ptile[:], in0=ptile[:],
                            in1=dpat_s[:, 512 * vv:512 * (vv + 1)],
                            op=ALU.mult), sdg)
                    v.wait_ge(spv, gkt0 + nkt)
                    i1 = c.inc(v.reciprocal(out=rcp_s[:], in_=ps_z[g2 % 2][:]), svx)
                    v.wait_ge(svx, i1)
                    c.inc(v.tensor_tensor(
                        out=ao_s[h % 2][:, 512 * qb:512 * qb + 512],
                        in0=ps_o[g2 % 2][:], in1=rcp_s[:], op=ALU.mult), sev)

        @blk.gpsimd
        def _(gp):
            for h in range(32):
                gp.wait_ge(sev, 2 * h + 2)
                gp.dma_start(out=a_o[h], in_=ao_s[h % 2][:]).then_inc(sdone, 16)
            gp.wait_ge(sdone, 16 * 32)

    return nc



def build_proj_ln(mode, final=False):
    c = Ctx()
    nc = c.nc
    wdt = I32 if mode == "qffn" else F32
    x_in = nc.dram_tensor("xin", [NCH, 128, TPC], BF16, kind="ExternalInput").ap()
    xr_in = nc.dram_tensor("xres", [NCH, 128, TPC], BF16, kind="ExternalInput").ap()
    w_d = nc.dram_tensor("w", [32, 128, D], wdt, kind="ExternalInput").ap()
    ob_d = nc.dram_tensor("onesb", [128, 1], BF16, kind="ExternalInput").ap()
    or_d = nc.dram_tensor("onesr", [1, 128], F32, kind="ExternalInput").ap()
    eps_d = nc.dram_tensor("epsv", [1, 1], F32, kind="ExternalInput").ap()
    if mode == "qffn":
        sc_d = nc.dram_tensor("scales", [128, 32], F32, kind="ExternalInput").ap()
        bi_d = nc.dram_tensor("fbias", [128, 32], F32, kind="ExternalInput").ap()
    out_dt = F32 if final else BF16
    x_o = nc.dram_tensor("xout", [NCH, 128, TPC], out_dt, kind="ExternalOutput").ap()

    with c.es:
        xs = c.sbuf("xs_s", [128, NCH * TPC], BF16)
        xr = c.sbuf("xr_s", [128, NCH * TPC], BF16)
        wbuf = [c.sbuf("w%d" % i, [128, D], wdt) for i in range(2)]
        wb16 = [c.sbuf("wb%d" % i, [128, D], BF16) for i in range(2)] \
            if mode == "qffn" else None
        ones_b = c.sbuf("ones_b", [128, 1], BF16)
        ones_r = c.sbuf("ones_r", [1, 128], F32)
        eps_s = c.sbuf("eps_s", [1, 1], F32)
        sq = [c.sbuf("sq%d" % i, [128, 512], BF16) for i in range(2)]
        u = c.sbuf("u", [128, 512], F32)
        uo = [c.sbuf("uo%d" % i, [128, TPC], out_dt) for i in range(2)]
        r1 = c.sbuf("r1", [1, 512], F32)
        r2 = c.sbuf("r2", [1, 512], F32)
        r3 = c.sbuf("r3", [1, 512], F32)
        if mode == "qffn":
            sc_s = c.sbuf("sc_s", [128, 32], F32)
            bi_s = c.sbuf("bi_s", [128, 32], F32)
        ps = [c.psum("ps%d" % i, [128, 512], F32) for i in range(4)]
        ps_mu = c.psum("ps_mu", [128, 512], F32)
        ps_m2 = c.psum("ps_m2", [128, 512], F32)

        ld = c.sem("ld")
        wld = c.sem("wld")
        wfree = c.sem("wfree")
        wcst = c.sem("wcst")
        mmd = c.sem("mmd")
        pev = c.sem("pev")
        sqd = c.sem("sqd")
        sm2 = c.sem("sm2")
        stat = c.sem("stat")
        rowv = c.sem("rowv")
        ssx = c.sem("ssx")
        rowd = c.sem("rowd")
        bcd = c.sem("bcd")
        appd = c.sem("appd")
        sto = c.sem("sto")
        svx = c.sem("svx")

        blk = c.es.enter_context(nc.Block())
        n_misc = 3 + (2 if mode == "qffn" else 0)

        @blk.sync
        def _(sy):
            sy.dma_start(out=ones_b[:], in_=ob_d).then_inc(ld, 16)
            sy.dma_start(out=ones_r[:], in_=or_d).then_inc(ld, 16)
            sy.dma_start(out=eps_s[:], in_=eps_d).then_inc(ld, 16)
            if mode == "qffn":
                sy.dma_start(out=sc_s[:], in_=sc_d).then_inc(ld, 16)
                sy.dma_start(out=bi_s[:], in_=bi_d).then_inc(ld, 16)
            for ic in range(NCH):
                sy.dma_start(out=xs[:, ic * TPC:(ic + 1) * TPC],
                             in_=x_in[ic]).then_inc(ld, 16)
            for ic in range(NCH):
                sy.dma_start(out=xr[:, ic * TPC:(ic + 1) * TPC],
                             in_=xr_in[ic]).then_inc(ld, 16)
            for oc in range(32):
                if oc >= 2:
                    if mode == "qffn":
                        sy.wait_ge(wcst, oc - 1)
                    else:
                        sy.wait_ge(pev, 2 * oc - 2)
                sy.dma_start(out=wbuf[oc % 2][:], in_=w_d[oc]).then_inc(wld, 16)

        ld_all = 16 * (n_misc + 2 * NCH)


        @blk.tensor
        def _(t):
            t.wait_ge(ld, ld_all)
            for oc in range(32):
                if mode == "qffn":
                    t.wait_ge(wcst, oc + 1)
                    wv_ = wb16[oc % 2][:]
                else:
                    t.wait_ge(wld, 16 * (oc + 1))
                    wv_ = bfview(wbuf[oc % 2], D)
                for tb in range(2):
                    g = 2 * oc + tb
                    if g >= 4:
                        t.wait_ge(pev, g - 3)
                    bank = ps[g % 4]
                    mm = None
                    for ic in range(NCH):
                        mm = t.matmul(out=bank[:],
                                      lhsT=wv_[:, 128 * ic:128 * (ic + 1)],
                                      rhs=xs[:, ic * TPC + 512 * tb:
                                             ic * TPC + 512 * tb + 512],
                                      start=(ic == 0), stop=(ic == NCH - 1))
                    c.inc(mm, mmd)
            t.wait_ge(pev, 64)
            for tb in range(2):
                if tb == 1:
                    t.wait_ge(rowd, 2)
                for ic in range(NCH):
                    t.matmul(out=ps_mu[0:1, :], lhsT=ones_b[:],
                             rhs=xr[:, ic * TPC + 512 * tb:ic * TPC + 512 * tb + 512],
                             start=(ic == 0), stop=(ic == NCH - 1))
                for ic in range(NCH):
                    k = NCH * tb + ic
                    t.wait_ge(sqd, k + 1)
                    c.inc(t.matmul(out=ps_m2[0:1, :], lhsT=ones_b[:],
                                   rhs=sq[ic % 2][:],
                                   start=(ic == 0), stop=(ic == NCH - 1)), sm2)
                t.wait_ge(rowd, 2 * tb + 2)
                t.matmul(out=ps[2 * tb][:], lhsT=ones_r[:], rhs=r1[:],
                         start=True, stop=True)
                c.inc(t.matmul(out=ps[2 * tb + 1][:], lhsT=ones_r[:], rhs=r3[:],
                               start=True, stop=True), bcd)

        @blk.scalar
        def _(sc):
            sc.wait_ge(ld, ld_all)
            sc.wait_ge(pev, 64)
            for tb in range(2):
                for ic in range(NCH):
                    k = NCH * tb + ic
                    if k >= 2:
                        sc.wait_ge(sm2, k - 1)
                    c.inc(sc.activation(out=sq[ic % 2][:],
                                        in_=xr[:, ic * TPC + 512 * tb:
                                               ic * TPC + 512 * tb + 512],
                                        func=AF.Square), sqd)
                sc.wait_ge(rowv, tb + 1)
                c.inc(sc.activation(out=r3[:], in_=r2[:], func=AF.Sqrt,
                                    bias=eps_s[0:1, :], scale=1.0), ssx)

        @blk.vector
        def _(v):
            if mode == "qffn":
                for oc in range(2):
                    v.wait_ge(wld, 16 * (oc + 1))
                    c.inc(v.tensor_copy(out=wb16[oc % 2][:], in_=wbuf[oc % 2][:]),
                          wcst)
            for oc in range(32):
                for tb in range(2):
                    g = 2 * oc + tb
                    v.wait_ge(mmd, g + 1)
                    bank = ps[g % 4]
                    cols = slice(oc * TPC + 512 * tb, oc * TPC + 512 * tb + 512)
                    if mode == "qffn":
                        i1 = c.inc(v.tensor_scalar(
                            out=u[:], in0=bank[:], scalar1=sc_s[:, oc:oc + 1],
                            scalar2=bi_s[:, oc:oc + 1], op0=ALU.mult,
                            op1=ALU.add), svx)
                        v.wait_ge(svx, i1)
                        c.inc(v.tensor_tensor(out=xr[:, cols], in0=u[:],
                                              in1=xr[:, cols], op=ALU.add), pev)
                    else:
                        c.inc(v.tensor_tensor(out=xr[:, cols], in0=bank[:],
                                              in1=xr[:, cols], op=ALU.add), pev)
                if mode == "qffn" and oc + 2 < 32:
                    v.wait_ge(wld, 16 * (oc + 3))
                    c.inc(v.tensor_copy(out=wb16[oc % 2][:], in_=wbuf[oc % 2][:]),
                          wcst)
            for tb in range(2):
                v.wait_ge(sm2, NCH * (tb + 1))
                i1 = c.inc(v.tensor_scalar(out=r1[:], in0=ps_mu[0:1, :],
                                           scalar1=-1.0 / D, scalar2=None,
                                           op0=ALU.mult), svx)
                i2 = c.inc(v.tensor_scalar(out=r2[:], in0=ps_m2[0:1, :],
                                           scalar1=1.0 / D, scalar2=None,
                                           op0=ALU.mult), svx)
                v.wait_ge(svx, i2)
                i3 = c.inc(v.tensor_tensor(out=r3[:], in0=r1[:], in1=r1[:],
                                           op=ALU.mult), svx)
                v.wait_ge(svx, i3)
                i4 = c.inc(v.scalar_tensor_tensor(out=r2[:], in0=r3[:],
                                                  scalar=-1.0, in1=r2[:],
                                                  op0=ALU.mult, op1=ALU.add), svx)
                v.wait_ge(svx, i4)
                c.inc(v.tensor_copy(out=r2[:], in_=r2[:]), rowv)
                v.wait_ge(ssx, tb + 1)
                i5 = c.inc(v.reciprocal(out=r3[:], in_=r3[:]), svx)
                v.wait_ge(svx, i5)
                c.inc(v.tensor_copy(out=r1[:], in_=r1[:]), rowd)
                c.inc(v.tensor_copy(out=r3[:], in_=r3[:]), rowd)
            v.wait_ge(bcd, 2)
            for ic in range(NCH):
                if ic >= 2:
                    v.wait_ge(sto, 16 * (ic - 1))
                for tb in range(2):
                    cols = slice(ic * TPC + 512 * tb, ic * TPC + 512 * tb + 512)
                    j1 = c.inc(v.tensor_tensor(out=u[:], in0=xr[:, cols],
                                               in1=ps[2 * tb][:], op=ALU.add), svx)
                    v.wait_ge(svx, j1)
                    last = v.tensor_tensor(out=uo[ic % 2][:, 512 * tb:512 * tb + 512],
                                           in0=u[:], in1=ps[2 * tb + 1][:],
                                           op=ALU.mult)
                    if tb == 1:
                        c.inc(last, appd)

        @blk.gpsimd
        def _(gp):
            for ic in range(NCH):
                gp.wait_ge(appd, ic + 1)
                gp.dma_start(out=x_o[ic], in_=uo[ic % 2][:]).then_inc(sto, 16)
            gp.wait_ge(sto, 16 * NCH)

    return nc



def build_ffn1(debug=False, lora=True, loraonly=False):
    c = Ctx()
    nc = c.nc
    x_in = nc.dram_tensor("xin", [NCH, 128, TPC], BF16, kind="ExternalInput").ap()
    w_d = nc.dram_tensor("w", [32, 128, D], F32, kind="ExternalInput").ap()
    fb_d = nc.dram_tensor("fbias", [128, 32], F32, kind="ExternalInput").ap()
    wr_d = nc.dram_tensor("wr", [128, 256], F32, kind="ExternalInput").ap()
    a_d = nc.dram_tensor("la", [128, 4096], F32, kind="ExternalInput").ap()
    b_d = nc.dram_tensor("lb", [128, 4096], F32, kind="ExternalInput").ap()
    d2_d = nc.dram_tensor("d2", [8, 128], F32, kind="ExternalInput").ap()
    idf_d = nc.dram_tensor("idf", [128, 128], F32, kind="ExternalInput").ap()
    ob_d = nc.dram_tensor("onesb", [128, 1], BF16, kind="ExternalInput").ap()
    or_d = nc.dram_tensor("onesr", [1, 128], F32, kind="ExternalInput").ap()
    eps_d = nc.dram_tensor("epsv", [1, 1], F32, kind="ExternalInput").ap()
    x_o = nc.dram_tensor("xout", [NCH, 128, TPC], F32, kind="ExternalOutput").ap()
    dbg_o = {}
    if debug:
        for n, shp in [("d_e", [128, 64]), ("d_wts", [128, 64]), ("d_wt", [8, TPC]),
                       ("d_wb", [128, TPC]), ("d_g", [128, TPC]), ("d_lo", [128, 512]),
                       ("d_bb", [128, 2048])]:
            dbg_o[n] = nc.dram_tensor(n, shp, F32, kind="ExternalOutput").ap()

    with c.es:
        xs = c.sbuf("xs_s", [128, NCH * TPC], BF16)
        xr = c.sbuf("xr_s", [128, NCH * TPC], BF16)
        wbuf = [c.sbuf("w%d" % i, [128, D], F32) for i in range(2)]
        fb_s = c.sbuf("fb_s", [128, 32], F32)
        wr_s = c.sbuf("wr_s", [128, 256], F32)
        a_bf = c.sbuf("a_bf", [128, 4096], BF16)
        b_bf = c.sbuf("b_bf", [128, 4096], BF16)
        d2_s = c.sbuf("d2_s", [8, 128], F32)
        idf_s = c.sbuf("idf_s", [128, 128], F32)
        ones_b = c.sbuf("ones_b", [128, 1], BF16)
        ones_r = c.sbuf("ones_r", [1, 128], F32)
        eps_s = c.sbuf("eps_s", [1, 1], F32)
        lgt_s = c.sbuf("lgt_s", [128, 8 * 8], F32)
        e_s = c.sbuf("e_s", [128, 8 * 8], F32)
        m1 = c.sbuf("m1", [128, 1], F32)
        mneg = c.sbuf("mneg", [128, 1], F32)
        m2 = c.sbuf("m2", [128, 1], F32)
        msk = c.sbuf("msk", [128, 8], F32)
        rcpr = c.sbuf("rcpr", [128, 1], F32)
        wts = c.sbuf("wts", [128, 8 * 8], F32)
        wt_s = c.sbuf("wt_s", [8, TPC], F32)
        wb_s = c.sbuf("wb_s", [128, TPC], F32)
        g_s = [c.sbuf("g%d" % i, [128, 512], BF16) for i in range(2)]
        sq = [c.sbuf("sq%d" % i, [128, 512], BF16) for i in range(2)]
        u = c.sbuf("u", [128, 512], F32)
        uo = [c.sbuf("uo%d" % i, [128, TPC], F32) for i in range(2)]
        r1 = c.sbuf("r1", [1, 512], F32)
        r2 = c.sbuf("r2", [1, 512], F32)
        r3 = c.sbuf("r3", [1, 512], F32)
        dbg_lo = None
        ps = [c.psum("ps%d" % i, [128, 512], F32) for i in range(4)]
        ps_mu = c.psum("ps_mu", [128, 512], F32)
        ps_m2 = c.psum("ps_m2", [128, 512], F32)
        ps_a = c.psum("ps_a", [128, 512], F32)
        ps_b = c.psum("ps_b", [128, 512], F32)

        ld = c.sem("ld")
        wld = c.sem("wld")
        wfree = c.sem("wfree")
        ldab = c.sem("ldab")
        abct = c.sem("abct")
        rt = c.sem("rt")
        rv = c.sem("rv")
        mmd = c.sem("mmd")
        pev = c.sem("pev")
        sqd = c.sem("sqd")
        sm2 = c.sem("sm2")
        stat = c.sem("stat")
        rowv = c.sem("rowv")
        ssx = c.sem("ssx")
        rowd = c.sem("rowd")
        bcd = c.sem("bcd")
        appd = c.sem("appd")
        sto = c.sem("sto")
        svx = c.sem("svx")

        blk = c.es.enter_context(nc.Block())

        @blk.sync
        def _(sy):
            for src, dst in [(fb_d, fb_s), (wr_d, wr_s), (d2_d, d2_s),
                             (idf_d, idf_s), (ob_d, ones_b), (or_d, ones_r),
                             (eps_d, eps_s)]:
                sy.dma_start(out=dst[:], in_=src).then_inc(ld, 16)
            sy.dma_start(out=wbuf[0][:], in_=a_d).then_inc(ldab, 16)
            sy.dma_start(out=wbuf[1][:], in_=b_d).then_inc(ldab, 16)
            for ic in range(NCH):
                sy.dma_start(out=xs[:, ic * TPC:(ic + 1) * TPC],
                             in_=x_in[ic]).then_inc(ld, 16)
            for ic in range(NCH):
                sy.dma_start(out=xr[:, ic * TPC:(ic + 1) * TPC],
                             in_=x_in[ic]).then_inc(ld, 16)
            sy.wait_ge(abct, 2)
            for oc in range(32):
                if oc >= 2:
                    sy.wait_ge(pev, 2 * oc - 2)
                sy.dma_start(out=wbuf[oc % 2][:], in_=w_d[oc]).then_inc(wld, 16)

        ld_all = 16 * (7 + 2 * NCH)

        @blk.vector
        def _(v):
            v.wait_ge(ldab, 32)
            c.inc(v.tensor_copy(out=a_bf[:], in_=wbuf[0][:]), abct)
            c.inc(v.tensor_copy(out=b_bf[:], in_=wbuf[1][:]), abct)

        @blk.tensor
        def _(t):
            t.wait_ge(ld, ld_all)
            for tt in range(8):
                bank = [ps_a, ps_b][tt % 2]
                if tt >= 2:
                    t.wait_ge(rv, tt - 1)
                mm = None
                for ic in range(NCH):
                    mm = t.matmul(out=bank[:, 0:8],
                                  lhsT=xs[:, ic * TPC + 128 * tt:ic * TPC + 128 * (tt + 1)],
                                  rhs=bfview(wr_s, 256)[:, 8 * ic:8 * (ic + 1)],
                                  start=(ic == 0), stop=(ic == NCH - 1))
                c.inc(mm, rt)
            for tt in range(8):
                t.wait_ge(rv, 8 + tt + 1)
                bank = [ps_a, ps_b][tt % 2]
                if tt >= 2:
                    t.wait_ge(rv, 16 + tt - 1)
                c.inc(t.transpose(out=bank[0:8, 0:128],
                                  in_=wts[:, 8 * tt:8 * (tt + 1)],
                                  identity=idf_s[:]), rt)
            for tb in range(2):
                t.wait_ge(rv, 24)
                bank = [ps_a, ps_b][tb]
                c.inc(t.matmul(out=bank[:], lhsT=d2_s[:],
                               rhs=wt_s[0:8, 512 * tb:512 * (tb + 1)],
                               start=True, stop=True), rt)
            t.wait_ge(abct, 2)
            for tb in range(2):
                t.wait_ge(rv, 24 + tb + 1)
                bank = [ps_a, ps_b][tb]
                mm = None
                for ic in range(NCH):
                    mm = t.matmul(out=bank[:],
                                  lhsT=a_bf[:, 128 * ic:128 * (ic + 1)],
                                  rhs=xs[:, ic * TPC + 512 * tb:ic * TPC + 512 * tb + 512],
                                  start=(ic == 0), stop=(ic == NCH - 1))
                c.inc(mm, rt)
            for oc in range(32):
                t.wait_ge(wld, 16 * (oc + 1))
                wv_ = bfview(wbuf[oc % 2], D)
                for tb in range(2):
                    g = 2 * oc + tb
                    if g >= 4:
                        t.wait_ge(pev, g - 3)
                    if oc == 0:
                        t.wait_ge(rv, 26 + tb + 1)
                    bank = ps[g % 4]
                    if loraonly:
                        c.inc(t.matmul(out=bank[:],
                                       lhsT=b_bf[:, 128 * oc:128 * (oc + 1)],
                                       rhs=g_s[tb][:], start=True, stop=True), mmd)
                    elif lora:
                        for ic in range(NCH):
                            t.matmul(out=bank[:],
                                     lhsT=wv_[:, 128 * ic:128 * (ic + 1)],
                                     rhs=xs[:, ic * TPC + 512 * tb:
                                            ic * TPC + 512 * tb + 512],
                                     start=(ic == 0), stop=False)
                        c.inc(t.matmul(out=bank[:],
                                       lhsT=b_bf[:, 128 * oc:128 * (oc + 1)],
                                       rhs=g_s[tb][:], start=False, stop=True), mmd)
                    else:
                        mm = None
                        for ic in range(NCH):
                            mm = t.matmul(out=bank[:],
                                          lhsT=wv_[:, 128 * ic:128 * (ic + 1)],
                                          rhs=xs[:, ic * TPC + 512 * tb:
                                                 ic * TPC + 512 * tb + 512],
                                          start=(ic == 0), stop=(ic == NCH - 1))
                        c.inc(mm, mmd)
            t.wait_ge(pev, 64)
            for tb in range(2):
                if tb == 1:
                    t.wait_ge(rowd, 2)
                for ic in range(NCH):
                    t.matmul(out=ps_mu[0:1, :], lhsT=ones_b[:],
                             rhs=xr[:, ic * TPC + 512 * tb:ic * TPC + 512 * tb + 512],
                             start=(ic == 0), stop=(ic == NCH - 1))
                for ic in range(NCH):
                    k = NCH * tb + ic
                    t.wait_ge(sqd, k + 1)
                    c.inc(t.matmul(out=ps_m2[0:1, :], lhsT=ones_b[:],
                                   rhs=sq[ic % 2][:],
                                   start=(ic == 0), stop=(ic == NCH - 1)), sm2)
                t.wait_ge(rowd, 2 * tb + 2)
                t.matmul(out=ps[2 * tb][:], lhsT=ones_r[:], rhs=r1[:],
                         start=True, stop=True)
                c.inc(t.matmul(out=ps[2 * tb + 1][:], lhsT=ones_r[:], rhs=r3[:],
                               start=True, stop=True), bcd)

        @blk.vector
        def _(v):
            v.wait_ge(ld, ld_all)
            for tt in range(8):
                v.wait_ge(rt, tt + 1)
                bank = [ps_a, ps_b][tt % 2]
                ecols = slice(8 * tt, 8 * (tt + 1))
                i1 = c.inc(v.tensor_reduce(out=m1[:], in_=bank[:, 0:8],
                                           axis=AX.X, op=ALU.max), svx)
                v.wait_ge(svx, i1)
                i2 = c.inc(v.tensor_scalar(out=mneg[:], in0=m1[:], scalar1=-1.0,
                                           scalar2=None, op0=ALU.mult), svx)
                v.wait_ge(svx, i2)
                i3 = c.inc(v.tensor_scalar(out=lgt_s[:, ecols], in0=bank[:, 0:8],
                                           scalar1=mneg[:, :], scalar2=None,
                                           op0=ALU.add), svx)
                v.wait_ge(svx, i3)
                c.inc(v.tensor_copy(out=lgt_s[:, ecols], in_=lgt_s[:, ecols]), rv)

        @blk.scalar
        def _(sc):
            for tt in range(8):
                sc.wait_ge(rv, tt + 1)
                ecols = slice(8 * tt, 8 * (tt + 1))
                c.inc(sc.activation(out=e_s[:, ecols], in_=lgt_s[:, ecols],
                                    func=AF.Exp), ssx)

        @blk.vector
        def _(v):
            for tt in range(8):
                v.wait_ge(ssx, tt + 1)
                ecols = slice(8 * tt, 8 * (tt + 1))
                i1 = c.inc(v.tensor_reduce(out=m1[:], in_=e_s[:, ecols],
                                           axis=AX.X, op=ALU.max), svx)
                v.wait_ge(svx, i1)
                i2 = c.inc(v.tensor_scalar(out=msk[:], in0=e_s[:, ecols],
                                           scalar1=m1[:, :], scalar2=None,
                                           op0=ALU.is_ge), svx)
                v.wait_ge(svx, i2)
                i3 = c.inc(v.scalar_tensor_tensor(out=wts[:, ecols], in0=msk[:],
                                                  scalar=-1000.0, in1=e_s[:, ecols],
                                                  op0=ALU.mult, op1=ALU.add), svx)
                v.wait_ge(svx, i3)
                i4 = c.inc(v.tensor_reduce(out=m2[:], in_=wts[:, ecols],
                                           axis=AX.X, op=ALU.max), svx)
                v.wait_ge(svx, i4)
                i5 = c.inc(v.tensor_tensor(out=rcpr[:], in0=m1[:], in1=m2[:],
                                           op=ALU.add), svx)
                v.wait_ge(svx, i5)
                i6 = c.inc(v.reciprocal(out=rcpr[:], in_=rcpr[:]), svx)
                i7 = c.inc(v.tensor_scalar(out=msk[:], in0=e_s[:, ecols],
                                           scalar1=m2[:, :], scalar2=None,
                                           op0=ALU.is_ge), svx)
                v.wait_ge(svx, i7)
                i8 = c.inc(v.tensor_tensor(out=msk[:], in0=msk[:],
                                           in1=e_s[:, ecols], op=ALU.mult), svx)
                v.wait_ge(svx, max(i6, i8))
                c.inc(v.tensor_scalar(out=wts[:, ecols], in0=msk[:],
                                      scalar1=rcpr[:, :], scalar2=None,
                                      op0=ALU.mult), rv)
            for tt in range(8):
                v.wait_ge(rt, 8 + tt + 1)
                bank = [ps_a, ps_b][tt % 2]
                c.inc(v.tensor_copy(out=wt_s[0:8, 128 * tt:128 * (tt + 1)],
                                    in_=bank[0:8, 0:128]), rv)
            for tb in range(2):
                v.wait_ge(rt, 16 + tb + 1)
                bank = [ps_a, ps_b][tb]
                c.inc(v.tensor_copy(out=wb_s[:, 512 * tb:512 * (tb + 1)],
                                    in_=bank[:]), rv)
            for tb in range(2):
                v.wait_ge(rt, 18 + tb + 1)
                v.wait_ge(rv, 26)
                bank = [ps_a, ps_b][tb]
                c.inc(v.tensor_tensor(out=g_s[tb][:], in0=bank[:],
                                      in1=wb_s[:, 512 * tb:512 * (tb + 1)],
                                      op=ALU.mult), rv)

        @blk.scalar
        def _(sc):
            sc.wait_ge(pev, 64)
            for tb in range(2):
                for ic in range(NCH):
                    k = NCH * tb + ic
                    if k >= 2:
                        sc.wait_ge(sm2, k - 1)
                    c.inc(sc.activation(out=sq[ic % 2][:],
                                        in_=xr[:, ic * TPC + 512 * tb:
                                               ic * TPC + 512 * tb + 512],
                                        func=AF.Square), sqd)
                sc.wait_ge(rowv, tb + 1)
                c.inc(sc.activation(out=r3[:], in_=r2[:], func=AF.Sqrt,
                                    bias=eps_s[0:1, :], scale=1.0), ssx)

        @blk.vector
        def _(v):
            for oc in range(32):
                for tb in range(2):
                    g = 2 * oc + tb
                    v.wait_ge(mmd, g + 1)
                    bank = ps[g % 4]
                    cols = slice(oc * TPC + 512 * tb, oc * TPC + 512 * tb + 512)
                    if debug and oc == 0 and tb == 0:
                        i0 = c.inc(v.tensor_copy(out=uo[1][:, 0:512], in_=bank[:]), svx)
                        v.wait_ge(svx, i0)
                    i1 = c.inc(v.tensor_scalar(
                        out=u[:], in0=bank[:], scalar1=fb_s[:, oc:oc + 1],
                        scalar2=None, op0=ALU.add), svx)
                    v.wait_ge(svx, i1)
                    c.inc(v.tensor_tensor(out=xr[:, cols], in0=u[:],
                                          in1=xr[:, cols], op=ALU.add), pev)
            for tb in range(2):
                v.wait_ge(sm2, NCH * (tb + 1))
                i1 = c.inc(v.tensor_scalar(out=r1[:], in0=ps_mu[0:1, :],
                                           scalar1=-1.0 / D, scalar2=None,
                                           op0=ALU.mult), svx)
                i2 = c.inc(v.tensor_scalar(out=r2[:], in0=ps_m2[0:1, :],
                                           scalar1=1.0 / D, scalar2=None,
                                           op0=ALU.mult), svx)
                v.wait_ge(svx, i2)
                i3 = c.inc(v.tensor_tensor(out=r3[:], in0=r1[:], in1=r1[:],
                                           op=ALU.mult), svx)
                v.wait_ge(svx, i3)
                i4 = c.inc(v.scalar_tensor_tensor(out=r2[:], in0=r3[:],
                                                  scalar=-1.0, in1=r2[:],
                                                  op0=ALU.mult, op1=ALU.add), svx)
                v.wait_ge(svx, i4)
                c.inc(v.tensor_copy(out=r2[:], in_=r2[:]), rowv)
                v.wait_ge(ssx, 8 + tb + 1)
                i5 = c.inc(v.reciprocal(out=r3[:], in_=r3[:]), svx)
                v.wait_ge(svx, i5)
                c.inc(v.tensor_copy(out=r1[:], in_=r1[:]), rowd)
                c.inc(v.tensor_copy(out=r3[:], in_=r3[:]), rowd)
            v.wait_ge(bcd, 2)
            for ic in range(NCH):
                if ic >= 2:
                    v.wait_ge(sto, 16 * (ic - 1))
                for tb in range(2):
                    cols = slice(ic * TPC + 512 * tb, ic * TPC + 512 * tb + 512)
                    j1 = c.inc(v.tensor_tensor(out=u[:], in0=xr[:, cols],
                                               in1=ps[2 * tb][:], op=ALU.add), svx)
                    v.wait_ge(svx, j1)
                    last = v.tensor_tensor(out=uo[ic % 2][:, 512 * tb:512 * tb + 512],
                                           in0=u[:], in1=ps[2 * tb + 1][:],
                                           op=ALU.mult)
                    if tb == 1:
                        c.inc(last, appd)

        @blk.gpsimd
        def _(gp):
            nst = 0
            if debug:
                gp.wait_ge(pev, 1)
                gp.dma_start(out=dbg_o["d_lo"], in_=uo[1][:, 0:512]).then_inc(sto, 16)
                nst += 1
            for ic in range(NCH):
                gp.wait_ge(appd, ic + 1)
                gp.dma_start(out=x_o[ic], in_=uo[ic % 2][:]).then_inc(sto, 16)
            nst += NCH
            if debug:
                gp.wait_ge(rv, 28)
                gp.dma_start(out=dbg_o["d_e"], in_=e_s[:]).then_inc(sto, 16)
                gp.dma_start(out=dbg_o["d_wts"], in_=wts[:]).then_inc(sto, 16)
                gp.dma_start(out=dbg_o["d_wt"], in_=wt_s[:]).then_inc(sto, 16)
                gp.dma_start(out=dbg_o["d_wb"], in_=wb_s[:]).then_inc(sto, 16)
                gp.dma_start(out=dbg_o["d_g"][:, 0:512], in_=g_s[0][:]).then_inc(sto, 16)
                gp.dma_start(out=dbg_o["d_g"][:, 512:1024], in_=g_s[1][:]).then_inc(sto, 16)
                pass
                gp.dma_start(out=dbg_o["d_bb"], in_=b_bf[:].bitcast(F32)[:, 0:2048]).then_inc(sto, 16)
                nst += 7
            gp.wait_ge(sto, 16 * nst)

    return nc



_PROGS = {}
_DBG = {}


def _prog(key, fn, *a):
    if key not in _PROGS:
        _PROGS[key] = fn(*a)
    return _PROGS[key]


def _stack(arrs):
    return np.ascontiguousarray(arrs.reshape(NCH, 128, TPC))


def _tm_from_chunks(out_chunks):
    return out_chunks.reshape(D, TPC)


def kernel_unfused(L0_attn_x, L0_attn_Wq, L0_attn_Wk, L0_attn_Wv, L0_attn_Wo,
           L0_ffn_quantized_weights, L0_ffn_scales, L0_ffn_bias,
           L1_attn_wq, L1_attn_wk, L1_attn_wv, L1_attn_wo,
           L1_ffn_W_base, L1_ffn_b_base, L1_ffn_W_router,
           L1_ffn_lora_A, L1_ffn_lora_B):
    cores = list(range(NCORES))
    toks = [core_tokens(j) for j in cores]

    ident_bf = np.ascontiguousarray(np.eye(128, dtype=np.float32).astype(bfloat16))
    ident_f = np.ascontiguousarray(np.eye(128, dtype=np.float32))
    ones_bf = np.ones((128, 128), bfloat16)
    ones_col = np.ones((128, 1), bfloat16)
    ones_row = np.ones((1, 128), np.float32)
    epsv = np.full((1, 1), LN_EPS, np.float32)
    dpat = diag_patterns()
    cos0, sin0 = rope_tables_l0(S)
    cos1, sin1 = rope_tables_l1(S)
    rope0 = [rope_fm(cos0, sin0, toks[j]) for j in cores]
    rope1 = [rope_fm(cos1, sin1, toks[j]) for j in cores]
    ebias = {}
    for j in cores:
        _, bias = pack_kv(j, np.zeros((1, 128, S), bfloat16))
        ebias[j] = np.ascontiguousarray(
            np.broadcast_to(bias[None, :], (128, NT)).astype(np.float32))

    wq0 = colchunk(np.ascontiguousarray(L0_attn_Wq.T))
    wk0 = colchunk(np.ascontiguousarray(L0_attn_Wk.T))
    wv0 = colchunk(np.ascontiguousarray(L0_attn_Wv.T))
    wo0 = colchunk(np.ascontiguousarray(L0_attn_Wo.T))
    qw0 = colchunk(np.ascontiguousarray(L0_ffn_quantized_weights.T))
    sc0 = np.ascontiguousarray(L0_ffn_scales.reshape(32, 128).T)
    bi0 = np.ascontiguousarray(L0_ffn_bias.reshape(32, 128).T)
    wq1 = colchunk(np.ascontiguousarray(L1_attn_wq))
    wk1 = colchunk(np.ascontiguousarray(L1_attn_wk))
    wv1 = colchunk(np.ascontiguousarray(L1_attn_wv))
    wo1 = colchunk(np.ascontiguousarray(L1_attn_wo))
    wb1 = colchunk(np.ascontiguousarray(L1_ffn_W_base.T))
    fb1 = np.ascontiguousarray(L1_ffn_b_base.reshape(32, 128).T)
    wr1 = np.ascontiguousarray(
        L1_ffn_W_router.reshape(8, 32, 128).transpose(2, 1, 0).reshape(128, 256))
    la1 = np.ascontiguousarray(
        L1_ffn_lora_A.reshape(128, D).T.reshape(32, 128, 128)
        .transpose(1, 0, 2).reshape(128, 4096))
    lb1 = np.ascontiguousarray(
        L1_ffn_lora_B.transpose(0, 2, 1).reshape(128, D))
    d2 = np.zeros((8, 128), np.float32)
    for e in range(8):
        d2[e, 16 * e:16 * (e + 1)] = 2.0

    x_fm = [np.ascontiguousarray(L0_attn_x[j // 4].T[:, toks[j]]) for j in cores]

    def launch(nc, maps):
        res = run_bass_kernel_spmd(nc, maps, core_ids=cores)
        return res.results

    def gather_kv(q_r, k_r, v_r, nkv):
        kfull = [np.zeros((nkv, 128, S), bfloat16) for _ in range(B)]
        vfull = [np.zeros((nkv, 128, S), bfloat16) for _ in range(B)]
        for j in cores:
            b = j // 4
            karr = k_r[j]["k"]
            varr = v_r[j]["v"]
            va = varr.reshape(nkv, 128, 8, 128)
            c0, c1 = core_chunks(j)
            for li, ch in ((0, c0), (1, c1)):
                gsl = slice(512 * ch, 512 * (ch + 1))
                lsl = slice(512 * li, 512 * (li + 1))
                kfull[b][:, :, gsl] = karr[:, :, lsl]
            for tt in range(8):
                ch = c0 if tt < 4 else c1
                ktg = 4 * ch + (tt % 4)
                vfull[b][:, :, 128 * ktg:128 * (ktg + 1)] = va[:, :, tt, :]
        kp, vp = {}, {}
        for j in cores:
            b = j // 4
            kp[j], _ = pack_kv(j, kfull[b])
            vp[j], _ = pack_kv(j, vfull[b])
        return kp, vp

    _DBG.clear()
    nc_a = _prog("qkv0", build_qkv, 0)
    maps = [{"x": _stack(x_fm[j]), "wq": wq0, "wk": wk0, "wv": wv0,
             "cos": rope0[j][0], "sinS": rope0[j][1], "ident": ident_bf}
            for j in cores]
    r_a = launch(nc_a, maps)
    xs_cur = [r_a[j]["xs"] for j in cores]
    kp, vp = gather_kv(None, r_a, r_a, 32)

    nc_b = _prog("attn0", build_attn, 0)
    maps = [{"q": r_a[j]["q"], "kp": kp[j], "vp": vp[j], "ebias": ebias[j],
             "dpat": dpat, "onesb": ones_bf} for j in cores]
    r_b = launch(nc_b, maps)

    nc_c = _prog("oln", build_proj_ln, "plain", False)
    maps = [{"xin": r_b[j]["attn"], "xres": xs_cur[j], "w": wo0,
             "onesb": ones_col, "onesr": ones_row, "epsv": epsv}
            for j in cores]
    r_c = launch(nc_c, maps)
    xs_cur = [r_c[j]["xout"] for j in cores]
    _DBG["x1"] = list(xs_cur)

    nc_d = _prog("qffn", build_proj_ln, "qffn", False)
    maps = [{"xin": xs_cur[j], "xres": xs_cur[j], "w": qw0, "scales": sc0,
             "fbias": bi0, "onesb": ones_col, "onesr": ones_row, "epsv": epsv}
            for j in cores]
    r_d = launch(nc_d, maps)
    xs_cur = [r_d[j]["xout"] for j in cores]
    _DBG["x2"] = list(xs_cur)

    nc_e = _prog("qkv1", build_qkv, 1)
    maps = [{"x": xs_cur[j], "wq": wq1, "wk": wk1, "wv": wv1,
             "cos": rope1[j][0], "sinS": rope1[j][1], "ident": ident_bf}
            for j in cores]
    r_e = launch(nc_e, maps)
    kp, vp = gather_kv(None, r_e, r_e, 8)

    nc_f = _prog("attn1", build_attn, 1)
    maps = [{"q": r_e[j]["q"], "kp": kp[j], "vp": vp[j], "ebias": ebias[j],
             "dpat": dpat, "onesb": ones_bf} for j in cores]
    r_f = launch(nc_f, maps)

    maps = [{"xin": r_f[j]["attn"], "xres": xs_cur[j], "w": wo1,
             "onesb": ones_col, "onesr": ones_row, "epsv": epsv}
            for j in cores]
    r_g = launch(nc_c, maps)
    xs_cur = [r_g[j]["xout"] for j in cores]
    _DBG["x3"] = list(xs_cur)

    nc_h = _prog("ffn1", build_ffn1)
    maps = [{"xin": xs_cur[j], "w": wb1, "fbias": fb1, "wr": wr1, "la": la1,
             "lb": lb1, "d2": d2, "idf": ident_f, "onesb": ones_col,
             "onesr": ones_row, "epsv": epsv} for j in cores]
    r_h = launch(nc_h, maps)

    out = np.empty((B, S, D), np.float32)
    for j in cores:
        b = j // 4
        xo = _tm_from_chunks(r_h[j]["xout"])
        out[b, toks[j], :] = xo.T
    return out



NKTM = (16, 28)
CHUNK_RANK = [min(cg, 7 - cg) for cg in range(8)]
CHUNK_HALF = [0 if cg == CHUNK_RANK[cg] else 1 for cg in range(8)]


def fused_ebias(core_j):
    c0, c1 = core_chunks(core_j)
    b = np.full(52, PAD_BIAS, np.float32)
    b[0:4 * c0] = EXP_SHIFT
    b[16:20] = EXP_SHIFT
    b[20:20 + 4 * (c1 - 4) + 16] = EXP_SHIFT
    b[48:52] = EXP_SHIFT
    b[20:48] = np.where(np.arange(28) < 4 * c1, EXP_SHIFT, PAD_BIAS)
    return np.ascontiguousarray(np.broadcast_to(b[None, :], (128, 52)).astype(np.float32))


def build_fused():
    c = Ctx()
    nc = c.nc
    ap = lambda t: t.ap()

    x_in = nc.dram_tensor("x", [NCH, 128, TPC], F32, kind="ExternalInput").ap()
    w_l0 = {n: nc.dram_tensor(n, [noc, 128, D], F32, kind="ExternalInput").ap()
            for n, noc in [("wq0", 32), ("wk0", 32), ("wv0", 32), ("wo0", 32)]}
    qw_d = nc.dram_tensor("qw", [32, 128, D], I32, kind="ExternalInput").ap()
    sc_d = nc.dram_tensor("scales", [128, 32], F32, kind="ExternalInput").ap()
    bi_d = nc.dram_tensor("qbias", [128, 32], F32, kind="ExternalInput").ap()
    w_l1 = {n: nc.dram_tensor(n, [noc, 128, D], F32, kind="ExternalInput").ap()
            for n, noc in [("wq1", 32), ("wk1", 8), ("wv1", 8), ("wo1", 32)]}
    wb_d = nc.dram_tensor("wbase", [32, 128, D], F32, kind="ExternalInput").ap()
    fb_d = nc.dram_tensor("fbias", [128, 32], F32, kind="ExternalInput").ap()
    wr_d = nc.dram_tensor("wr", [128, 256], F32, kind="ExternalInput").ap()
    a_d = nc.dram_tensor("la", [128, 4096], F32, kind="ExternalInput").ap()
    b_d = nc.dram_tensor("lb", [128, 4096], F32, kind="ExternalInput").ap()
    d2_d = nc.dram_tensor("d2", [8, 128], F32, kind="ExternalInput").ap()
    rope_d = {n: nc.dram_tensor(n, [128, TPC], F32, kind="ExternalInput").ap()
              for n in ["cos0", "sin0", "cos1", "sin1"]}
    id_d = nc.dram_tensor("ident", [128, 128], BF16, kind="ExternalInput").ap()
    idf_d = nc.dram_tensor("idf", [128, 128], F32, kind="ExternalInput").ap()
    ob_d = nc.dram_tensor("onesb", [128, 1], BF16, kind="ExternalInput").ap()
    obm_d = nc.dram_tensor("onesbm", [128, 128], BF16, kind="ExternalInput").ap()
    or_d = nc.dram_tensor("onesr", [1, 128], F32, kind="ExternalInput").ap()
    eps_d = nc.dram_tensor("epsv", [1, 1], F32, kind="ExternalInput").ap()
    dpat_d = nc.dram_tensor("dpat", [4, 128, 512], BF16, kind="ExternalInput").ap()
    eb_d = nc.dram_tensor("ebias", [128, 52], F32, kind="ExternalInput").ap()
    x_o = nc.dram_tensor("xout", [NCH, 128, TPC], F32, kind="ExternalOutput").ap()

    q_h = {L: nc.dram_tensor("qh%d" % L, [32, 128, TPC], BF16).ap() for L in (0, 1)}
    k_h = {L: nc.dram_tensor("kh%d" % L, [(32 if L == 0 else 8) * 128, TPC], BF16).ap()
           for L in (0, 1)}
    v_h = {L: nc.dram_tensor("vh%d" % L, [(32 if L == 0 else 8) * 128, TPC], BF16).ap()
           for L in (0, 1)}
    kg_h = {L: nc.dram_tensor("kg%d" % L, [4 * (32 if L == 0 else 8) * 128, TPC], BF16)
            for L in (0, 1)}
    vg_h = {L: nc.dram_tensor("vg%d" % L, [4 * (32 if L == 0 else 8) * 128, TPC], BF16)
            for L in (0, 1)}
    at_h = {L: nc.dram_tensor("at%d" % L, [32, 128, TPC], BF16).ap() for L in (0, 1)}
    xs0_h = nc.dram_tensor("xs0", [NCH, 128, TPC], BF16).ap()
    xh = {i: nc.dram_tensor("xh%d" % i, [NCH, 128, TPC], BF16).ap() for i in (1, 2, 3)}

    with c.es:
        S_XS = c.sbuf("S_XS", [128, 32768], BF16)
        S_XR = c.sbuf("S_XR", [128, 32768], BF16)
        S_W = c.sbuf("S_W", [128, 8192], F32)
        AB = c.sbuf("AB", [128, 8192], BF16)
        cos_s = c.sbuf("cos_s", [128, TPC], F32)
        sin_s = c.sbuf("sin_s", [128, TPC], F32)
        id_s = c.sbuf("id_s", [128, 128], BF16)
        idf_s = c.sbuf("idf_s", [128, 128], F32)
        ones_b = c.sbuf("ones_b", [128, 1], BF16)
        ones_bm = c.sbuf("ones_bm", [128, 128], BF16)
        ones_r = c.sbuf("ones_r", [1, 128], F32)
        eps_s = c.sbuf("eps_s", [1, 1], F32)
        eb_s = c.sbuf("eb_s", [128, 52], F32)
        vsc_s = c.sbuf("vsc_s", [128, 32], F32)
        vbi_s = c.sbuf("vbi_s", [128, 32], F32)
        fb_s = c.sbuf("fb_s", [128, 32], F32)
        wr_s = c.sbuf("wr_s", [128, 256], F32)
        u_s = c.sbuf("u_s", [128, 512], F32)
        r1 = c.sbuf("r1", [1, 512], F32)
        r2 = c.sbuf("r2", [1, 512], F32)
        r3 = c.sbuf("r3", [1, 512], F32)
        lgt_s = c.sbuf("lgt_s", [128, 64], F32)
        e_s = c.sbuf("e_s", [128, 64], F32)
        wts_s = c.sbuf("wts_s", [128, 64], F32)
        m1_s = c.sbuf("m1_s", [128, 1], F32)
        mneg_s = c.sbuf("mneg_s", [128, 1], F32)
        m2_s = c.sbuf("m2_s", [128, 1], F32)
        msk_s = c.sbuf("msk_s", [128, 8], F32)
        rcpr_s = c.sbuf("rcpr_s", [128, 1], F32)
        wt_s = c.sbuf("wt_s", [8, TPC], F32)
        wb_s = c.sbuf("wb_s", [128, TPC], F32)
        g_s = [c.sbuf("g%d" % i, [128, 512], BF16) for i in range(2)]
        ps = [c.psum("ps%d" % i, [128, 512], F32) for i in range(4)]
        ps_mu = c.psum("ps_mu", [128, 512], F32)
        ps_m2 = c.psum("ps_m2", [128, 512], F32)
        ps_a = c.psum("ps_a", [128, 512], F32)
        ps_b = c.psum("ps_b", [128, 512], F32)
        ps_trA = ps_a[:].bitcast(BF16)
        ps_trB = ps_b[:].bitcast(BF16)
        d2_s2 = c.sbuf("d2_s2", [8, 128], F32)

        pb = c.sem("pb")
        gsvx = c.sem("gsvx")
        blk = c.es.enter_context(nc.Block())
        phase_n = [0]

        def barrier_wait(eng):
            eng.wait_ge(pb, 5 * phase_n[0])

        def barrier_done(incs):
            phase_n[0] += 1

        XSf = S_XS[:].bitcast(F32)
        XRf = S_XR[:].bitcast(F32)
        ABf = AB[:].bitcast(F32)
        uo = [XSf[:, 1024 * i:1024 * (i + 1)] for i in range(2)]
        sq = [S_XS[:, 4096 + 512 * i:4096 + 512 * (i + 1)] for i in range(2)]
        uo_w = [S_W[:, 2048 + 1024 * i:2048 + 1024 * (i + 1)] for i in range(2)]
        sq_w = [S_W[:].bitcast(BF16)[:, 512 * i:512 * (i + 1)] for i in range(2)]
        dpat_s = S_XS[:, 24576:26624]
        rcp_s = XSf[:, 13312:13824]


        def emit_const_loads():
            p = "cl"
            ld = c.sem(p + "ld")
            ms = c.sem(p + "ms")

            @blk.gpsimd
            def _(gp):
                gp.memset(S_W[:], 0.0)
                gp.memset(S_XS[:], 0.0)
                c.inc(gp.tensor_copy(out=S_XR[:, 0:1], in_=S_XS[:, 0:1]), ms)
                gp.memset(S_XR[:], 0.0)
                c.inc(gp.tensor_copy(out=S_W[:, 0:1], in_=S_W[:, 0:1]), ms)

            @blk.sync
            def _(sy):
                sy.wait_ge(ms, 2)
                for src, dst in [(id_d, id_s), (idf_d, idf_s), (ob_d, ones_b),
                                 (obm_d, ones_bm), (or_d, ones_r), (eps_d, eps_s),
                                 (eb_d, eb_s), (sc_d, vsc_s), (bi_d, vbi_s),
                                 (fb_d, fb_s), (wr_d, wr_s), (d2_d, d2_s2)]:
                    sy.dma_start(out=dst[:], in_=src).then_inc(ld, 16)
                sy.wait_ge(ld, 16 * 12)
                sy.sem_inc(pb, 5)
            phase_n[0] += 1

        def emit_qkv(L):
            p = "qk%d" % L
            nkv = 32 if L == 0 else 8
            ldx = c.sem(p + "ldx")
            ldxs = [c.sem(p + "ldxa"), c.sem(p + "ldxb")]
            wld = c.sem(p + "wld")
            mmd = c.sem(p + "mmd")
            pev = c.sem(p + "pev")
            xca = c.sem(p + "xca")
            trd = c.sem(p + "trd")
            tre = c.sem(p + "tre")
            sto = c.sem(p + "sto")
            svx = gsvx
            ccs = c.sem(p + "ccs")
            projs = [("v", w_l0["wv0"] if L == 0 else w_l1["wv1"], nkv),
                     ("k", w_l0["wk0"] if L == 0 else w_l1["wk1"], nkv),
                     ("q", w_l0["wq0"] if L == 0 else w_l1["wq1"], 32)]
            xs = S_XR
            xl = S_W[:, 4096:6144]
            qk_out = [S_XS[:, 1024 * i:1024 * (i + 1)] for i in range(2)]
            vfm = [S_XS[:, 2048 + 1024 * i:2048 + 1024 * (i + 1)] for i in range(2)]
            vtm = [S_XS[:, 4096 + 1024 * i:4096 + 1024 * (i + 1)] for i in range(2)]
            tmp = XSf[:, 3072:3584]
            u = XSf[:, 3584:4096]
            wbuf = [S_W[:, 0:4096], S_W[:, 4096:8192]]
            ps_tr = [ps_trA, ps_trB]
            n_xs_st = 0
            store_base = {}
            _si = n_xs_st
            for _pn, _wd, _noc in projs:
                store_base[_pn] = _si
                _si += _noc
            x_src = x_in if L == 0 else xh[2]
            cosd = rope_d["cos0"] if L == 0 else rope_d["cos1"]
            sind = rope_d["sin0"] if L == 0 else rope_d["sin1"]

            @blk.sync
            def _(sy):
                barrier_wait(sy)
                sy.dma_start(out=cos_s[:], in_=cosd).then_inc(ldx, 16)
                sy.dma_start(out=sin_s[:], in_=sind).then_inc(ldx, 16)
                if L == 0:
                    for ic in range(NCH):
                        if ic >= 2:
                            sy.wait_ge(xca, ic - 1)
                        sy.dma_start(out=xl[:, (ic % 2) * TPC:(ic % 2 + 1) * TPC],
                                     in_=x_src[ic]).then_inc(ldxs[ic % 2], 16)
                si = 0
                for _pn, wd, noc in projs:
                    for oc in range(noc):
                        if si == 1 and L == 0:
                            sy.wait_ge(xca, NCH)
                        if si >= 2:
                            sy.wait_ge(pev, 2 * si - 2)
                        sy.dma_start(out=wbuf[si % 2][:],
                                     in_=wd[oc]).then_inc(wld, 16)
                        si += 1
                sy.sem_inc(pb, 1)

            if L == 0:
                @blk.vector
                def _(v):
                    barrier_wait(v)
                    for ic in range(NCH):
                        v.wait_ge(ldxs[ic % 2], 16 * (ic // 2 + 1))
                        c.inc(v.tensor_copy(out=xs[:, ic * TPC:(ic + 1) * TPC],
                                            in_=xl[:, (ic % 2) * TPC:(ic % 2 + 1) * TPC]),
                              xca)

            @blk.tensor
            def _(t):
                barrier_wait(t)
                if L == 0:
                    t.wait_ge(xca, NCH)
                t.wait_ge(ldx, 32)
                si = 0
                g = 0

                def v_transpose(oc):
                    t.wait_ge(pev, 2 * oc + 2)
                    if oc >= 2:
                        t.wait_ge(tre, oc - 1)
                    tr = None
                    for tt in range(8):
                        tr = t.transpose(out=ps_tr[oc % 2][:, 128 * tt:128 * (tt + 1)],
                                         in_=vfm[oc % 2][:, 128 * tt:128 * (tt + 1)],
                                         identity=id_s[:])
                    c.inc(tr, trd)

                for _pn, wd, noc in projs:
                    for oc in range(noc):
                        t.wait_ge(wld, 16 * (si + 1))
                        wv_ = wbuf[si % 2][:].bitcast(BF16)[:, 1:2 * D:2]
                        for tb in range(2):
                            if g >= 4:
                                t.wait_ge(pev, g - 3)
                            bank = ps[g % 4]
                            mm = None
                            for ic in range(NCH):
                                mm = t.matmul(out=bank[:],
                                              lhsT=wv_[:, 128 * ic:128 * (ic + 1)],
                                              rhs=xs[:, ic * TPC + 512 * tb:
                                                     ic * TPC + 512 * tb + 512],
                                              start=(ic == 0), stop=(ic == NCH - 1))
                            c.inc(mm, mmd)
                            g += 1
                        si += 1
                        if _pn == "v" and oc >= 2:
                            v_transpose(oc - 2)
                    if _pn == "v":
                        v_transpose(nkv - 2)
                        v_transpose(nkv - 1)
                c.inc(t.nop() if False else mm, pb) if False else None
                t.sem_inc(pb, 1)

            @blk.vector
            def _(v):
                v.wait_ge(ldx, 32)
                g = 0

                def vtm_evac(oc):
                    v.wait_ge(trd, oc + 1)
                    if oc >= 2:
                        v.wait_ge(sto, 16 * (store_base["v"] + oc - 1))
                    c.inc(v.tensor_copy(out=vtm[oc % 2][:],
                                        in_=ps_tr[oc % 2][:, 0:1024]), tre)

                for oc in range(nkv):
                    for tb in range(2):
                        v.wait_ge(mmd, g + 1)
                        bank = ps[g % 4]
                        cols = slice(512 * tb, 512 * tb + 512)
                        if oc >= 2 and tb == 0:
                            v.wait_ge(trd, oc - 1)
                        c.inc(v.tensor_copy(out=vfm[oc % 2][:, cols], in_=bank[:]), pev)
                        g += 1
                    if oc >= 2:
                        vtm_evac(oc - 2)
                vtm_evac(nkv - 2)
                vtm_evac(nkv - 1)
                for pn, noc in (("k", nkv), ("q", 32)):
                    for oc in range(noc):
                        for tb in range(2):
                            v.wait_ge(mmd, g + 1)
                            bank = ps[g % 4]
                            cols = slice(512 * tb, 512 * tb + 512)
                            if tb == 0 and oc >= 2:
                                v.wait_ge(sto, 16 * (store_base[pn] + oc - 1))
                            i1 = c.inc(v.tensor_tensor(
                                out=tmp[0:64, :], in0=bank[64:128, :],
                                in1=sin_s[0:64, cols], op=ALU.mult), svx)
                            i2 = c.inc(v.tensor_tensor(
                                out=tmp[64:128, :], in0=bank[0:64, :],
                                in1=sin_s[64:128, cols], op=ALU.mult), svx)
                            i3 = c.inc(v.tensor_tensor(
                                out=u[:], in0=bank[:], in1=cos_s[:, cols],
                                op=ALU.mult), svx)
                            v.wait_ge(svx, i3)
                            last = v.tensor_tensor(out=qk_out[oc % 2][:, cols],
                                                   in0=u[:], in1=tmp[:], op=ALU.add)
                            c.inc(last, pev)
                            g += 1
                last.then_inc(pb, 1) if False else None
                v.sem_inc(pb, 1)

            @blk.scalar
            def _(s):
                barrier_wait(s)
                s.sem_inc(pb, 1)

            @blk.gpsimd
            def _(gp):
                barrier_wait(gp)
                nst = 0
                base_g = 0
                for pn, _wd, noc in projs:
                    if pn == "v":
                        for oc in range(nkv):
                            gp.wait_ge(tre, oc + 1)
                            gp.dma_start(out=v_h[L][128 * oc:128 * (oc + 1), :],
                                         in_=vtm[oc % 2][:]).then_inc(sto, 16)
                            nst += 1
                        gp.wait_ge(sto, 16 * nst)
                        gp.collective_compute(
                            "AllGather", ALU.bypass,
                            replica_groups=[[0, 1, 2, 3], [4, 5, 6, 7]],
                            ins=[v_h[L].opt()],
                            outs=[vg_h[L].ap().opt()]).then_inc(ccs)
                    else:
                        for oc in range(noc):
                            gp.wait_ge(pev, base_g + 2 * oc + 2)
                            out_d = (k_h[L][128 * oc:128 * (oc + 1), :] if pn == "k"
                                     else q_h[L][oc])
                            gp.dma_start(out=out_d, in_=qk_out[oc % 2][:]).then_inc(sto, 16)
                            nst += 1
                        if pn == "k":
                            gp.wait_ge(sto, 16 * nst)
                            gp.collective_compute(
                                "AllGather", ALU.bypass,
                                replica_groups=[[0, 1, 2, 3], [4, 5, 6, 7]],
                                ins=[k_h[L].opt()],
                                outs=[kg_h[L].ap().opt()]).then_inc(ccs)
                    base_g += 2 * noc
                gp.wait_ge(sto, 16 * nst)
                gp.wait_ge(ccs, 2)
                gp.sem_inc(pb, 1)
            phase_n[0] += 1

        def emit_attn(L):
            p = "at%d" % L
            nkv = 32 if L == 0 else 8
            rep = 32 // nkv
            qld = c.sem(p + "qld")
            kvld = c.sem(p + "kvld")
            smm = c.sem(p + "smm")
            sexp = c.sem(p + "sexp")
            sdg = c.sem(p + "sdg")
            spv = c.sem(p + "spv")
            sev = c.sem(p + "sev")
            sdone = c.sem(p + "sdone")
            svx = gsvx
            q_s = [S_XS[:, 1024 * i:1024 * (i + 1)] for i in range(2)]
            k_s = [S_XS[:, 2048 + 4608 * i:2048 + 4608 * (i + 1)] for i in range(2)]
            v_s = [S_XS[:, 11264 + 4608 * i:11264 + 4608 * (i + 1)] for i in range(2)]
            pt = [S_XS[:, 20480 + 512 * i:20480 + 512 * (i + 1)] for i in range(4)] + \
                 [S_XS[:, 27648 + 512 * i:27648 + 512 * (i + 1)] for i in range(2)]
            ao_s = [S_XS[:, 22528 + 1024 * i:22528 + 1024 * (i + 1)] for i in range(2)]
            ps_s = [ps[0], ps[1], ps_a, ps_b]
            ps_o = [ps[2], ps[3]]
            ps_z = [ps_mu, ps_m2]
            kgr = kg_h[L].ap()
            vgr = vg_h[L].ap()

            def slot_col(qb, kt):
                if kt < NKTM[qb]:
                    return 128 * kt
                return (3584 + 128 * (kt - NKTM[qb])) if qb == 0 else \
                       (4096 + 128 * (kt - NKTM[qb]))

            def bias_slot(qb, kt):
                return kt if qb == 0 else 20 + kt

            NTOT = (NKTM[0] + 4) + (NKTM[1] + 4)

            dpl = c.sem(p + "dpl")

            @blk.sync
            def _(sy):
                barrier_wait(sy)
                for a_ in range(4):
                    sy.dma_start(out=dpat_s[:, 512 * a_:512 * (a_ + 1)],
                                 in_=dpat_d[a_]).then_inc(dpl, 16)
                for h in range(32):
                    if h >= 2:
                        sy.wait_ge(sev, 2 * (h - 1))
                    sy.dma_start(out=q_s[h % 2][:], in_=q_h[L][h]).then_inc(qld, 16)
                    if h % rep == 0:
                        kv = h // rep
                        if kv >= 2:
                            sy.wait_ge(sev, 2 * rep * (kv - 1))
                        for cg in range(7):
                            rk, hf = CHUNK_RANK[cg], CHUNK_HALF[cg]
                            row = (rk * nkv + kv) * 128
                            sy.dma_start(
                                out=k_s[kv % 2][:, 512 * cg:512 * (cg + 1)],
                                in_=kgr[row:row + 128, 512 * hf:512 * (hf + 1)]
                            ).then_inc(kvld, 16)
                            sy.dma_start(
                                out=v_s[kv % 2][:, 512 * cg:512 * (cg + 1)],
                                in_=vgr[row:row + 128, 512 * hf:512 * (hf + 1)]
                            ).then_inc(kvld, 16)
                        for qb in range(2):
                            lc = slice(512 * qb, 512 * (qb + 1))
                            dc = slice(3584, 4096) if qb == 0 else slice(4096, 4608)
                            sy.dma_start(out=k_s[kv % 2][:, dc],
                                         in_=k_h[L][128 * kv:128 * (kv + 1), lc]
                                         ).then_inc(kvld, 16)
                            sy.dma_start(out=v_s[kv % 2][:, dc],
                                         in_=v_h[L][128 * kv:128 * (kv + 1), lc]
                                         ).then_inc(kvld, 16)
                sy.sem_inc(pb, 1)

            n_kv_dma = 18

            @blk.tensor
            def _(t):
                barrier_wait(t)
                for h in range(32):
                    t.wait_ge(qld, 16 * (h + 1))
                    t.wait_ge(kvld, 16 * n_kv_dma * (h // rep + 1))
                    kbuf = k_s[(h // rep) % 2]
                    vbuf = v_s[(h // rep) % 2]
                    for qb in range(2):
                        nkt = NKTM[qb] + 4
                        base = 0 if qb == 0 else NKTM[0] + 4
                        g2 = 2 * h + qb
                        gkt0 = NTOT * h + base
                        qcols = slice(512 * qb, 512 * qb + 512)
                        if g2 >= 2:
                            t.wait_ge(sev, g2 - 1)

                        def scores(kt):
                            gkt = gkt0 + kt
                            if gkt >= 4:
                                t.wait_ge(sexp, gkt - 3)
                            cc = slot_col(qb, kt)
                            c.inc(t.matmul(out=ps_s[gkt % 4][:],
                                           lhsT=kbuf[:, cc:cc + 128],
                                           rhs=q_s[h % 2][:, qcols],
                                           start=True, stop=True), smm)

                        def pv(kt):
                            gkt = gkt0 + kt
                            if kt >= nkt - 4:
                                t.wait_ge(sdg, 4 * g2 + (kt - (nkt - 4)) + 1)
                            else:
                                t.wait_ge(sexp, gkt + 1)
                            ptile = pt[gkt % 6]
                            cc = slot_col(qb, kt)
                            t.matmul(out=ps_o[g2 % 2][:], lhsT=vbuf[:, cc:cc + 128],
                                     rhs=ptile[:], start=(kt == 0), stop=(kt == nkt - 1))

                        def psum_mm(kt):
                            gkt = gkt0 + kt
                            ptile = pt[gkt % 6]
                            c.inc(t.matmul(out=ps_z[g2 % 2][:], lhsT=ones_bm[:],
                                           rhs=ptile[:], start=(kt == 0),
                                           stop=(kt == nkt - 1)), spv)

                        for kt_ in range(4):
                            scores(kt_)
                        for kt2 in range(0, nkt, 2):
                            if kt2 + 4 < nkt:
                                scores(kt2 + 4)
                            if kt2 + 5 < nkt:
                                scores(kt2 + 5)
                            pv(kt2)
                            pv(kt2 + 1)
                            psum_mm(kt2)
                            psum_mm(kt2 + 1)
                t.sem_inc(pb, 1)

            @blk.scalar
            def _(sc):
                barrier_wait(sc)
                for h in range(32):
                    for qb in range(2):
                        nkt = NKTM[qb] + 4
                        base = 0 if qb == 0 else NKTM[0] + 4
                        gkt0 = NTOT * h + base
                        for kt in range(nkt):
                            gkt = gkt0 + kt
                            sc.wait_ge(smm, gkt + 1)
                            if gkt >= 6:
                                sc.wait_ge(spv, gkt - 5)
                            bs = bias_slot(qb, kt)
                            c.inc(sc.activation(out=pt[gkt % 6][:],
                                                in_=ps_s[gkt % 4][:], func=AF.Exp,
                                                bias=eb_s[:, bs:bs + 1],
                                                scale=float(HD) ** -0.5), sexp)
                sc.sem_inc(pb, 1)

            @blk.vector
            def _(v):
                barrier_wait(v)
                v.wait_ge(dpl, 64)
                for h in range(32):
                    if h >= 2:
                        v.wait_ge(sdone, 16 * (h - 1))
                    for qb in range(2):
                        nkt = NKTM[qb] + 4
                        base = 0 if qb == 0 else NKTM[0] + 4
                        g2 = 2 * h + qb
                        gkt0 = NTOT * h + base
                        for kt in range(nkt - 4, nkt):
                            vv = kt - (nkt - 4)
                            v.wait_ge(sexp, gkt0 + kt + 1)
                            ptile = pt[(gkt0 + kt) % 6]
                            c.inc(v.tensor_tensor(out=ptile[:], in0=ptile[:],
                                                  in1=dpat_s[:, 512 * vv:512 * (vv + 1)],
                                                  op=ALU.mult), sdg)
                        v.wait_ge(spv, gkt0 + nkt)
                        i1 = c.inc(v.reciprocal(out=rcp_s[:], in_=ps_z[g2 % 2][:]), svx)
                        v.wait_ge(svx, i1)
                        c.inc(v.tensor_tensor(
                            out=ao_s[h % 2][:, 512 * qb:512 * qb + 512],
                            in0=ps_o[g2 % 2][:], in1=rcp_s[:], op=ALU.mult), sev)
                v.sem_inc(pb, 1)

            @blk.gpsimd
            def _(gp):
                barrier_wait(gp)
                for h in range(32):
                    gp.wait_ge(sev, 2 * h + 2)
                    gp.dma_start(out=at_h[L][h], in_=ao_s[h % 2][:]).then_inc(sdone, 16)
                gp.wait_ge(sdone, 16 * 32)
                gp.sem_inc(pb, 1)
            phase_n[0] += 1

        def emit_projln(name, mode, x_src, res_src, w_src, out_dst, final=False):
            p = name
            ld = c.sem(p + "ld")
            wld = c.sem(p + "wld")
            wcst = c.sem(p + "wcst")
            mmd = c.sem(p + "mmd")
            pev = c.sem(p + "pev")
            sqd = c.sem(p + "sqd")
            sm2 = c.sem(p + "sm2")
            rowv = c.sem(p + "rowv")
            ssx = c.sem(p + "ssx")
            rowd = c.sem(p + "rowd")
            bcd = c.sem(p + "bcd")
            appd = c.sem(p + "appd")
            sto = c.sem(p + "sto")
            svx = gsvx
            plain = (mode != "qffn")
            mub = [S_W[:].bitcast(BF16)[:, 1024 + 512 * i:1024 + 512 * (i + 1)]
                   for i in range(2)]
            rsb = [S_W[:].bitcast(BF16)[:, 2048 + 512 * i:2048 + 512 * (i + 1)]
                   for i in range(2)]
            xs = S_XS if plain else S_XR
            xr = S_XR if plain else S_XS
            strm = S_XR
            sqv = sq if plain else sq_w
            uov = uo if plain else uo_w
            wbuf = [S_W[:, 0:4096], S_W[:, 4096:8192]]
            wb16 = [AB[:, 4096 * i:4096 * (i + 1)] for i in range(2)]
            out_dt = F32 if final else BF16

            @blk.sync
            def _(sy):
                barrier_wait(sy)
                if plain:
                    for ic in range(NCH):
                        sy.dma_start(out=xs[:, ic * TPC:(ic + 1) * TPC],
                                     in_=x_src[ic]).then_inc(ld, 16)
                for oc in range(32):
                    if oc >= 2:
                        if mode == "qffn":
                            sy.wait_ge(wcst, oc - 1)
                        else:
                            sy.wait_ge(pev, 2 * oc - 2)
                    sy.dma_start(out=wbuf[oc % 2][:].bitcast(I32 if mode == "qffn" else F32),
                                 in_=w_src[oc]).then_inc(wld, 16)
                sy.sem_inc(pb, 1)

            ld_all = 16 * NCH if plain else 0

            @blk.tensor
            def _(t):
                barrier_wait(t)
                t.wait_ge(ld, ld_all)
                for oc in range(32):
                    if mode == "qffn":
                        t.wait_ge(wcst, oc + 1)
                        wv_ = wb16[oc % 2][:]
                    else:
                        t.wait_ge(wld, 16 * (oc + 1))
                        wv_ = wbuf[oc % 2][:].bitcast(BF16)[:, 1:2 * D:2]
                    for tb in range(2):
                        g = 2 * oc + tb
                        if g >= 4:
                            t.wait_ge(pev, g - 3)
                        bank = ps[g % 4]
                        mm = None
                        for ic in range(NCH):
                            mm = t.matmul(out=bank[:],
                                          lhsT=wv_[:, 128 * ic:128 * (ic + 1)],
                                          rhs=xs[:, ic * TPC + 512 * tb:
                                                 ic * TPC + 512 * tb + 512],
                                          start=(ic == 0), stop=(ic == NCH - 1))
                        c.inc(mm, mmd)
                t.wait_ge(pev, 64)
                for tb in range(2):
                    if tb == 1:
                        t.wait_ge(rowd, 2)
                    for ic in range(NCH):
                        t.matmul(out=ps_mu[0:1, :], lhsT=ones_b[:],
                                 rhs=xr[:, ic * TPC + 512 * tb:ic * TPC + 512 * tb + 512],
                                 start=(ic == 0), stop=(ic == NCH - 1))
                    for ic in range(NCH):
                        k = NCH * tb + ic
                        t.wait_ge(sqd, k + 1)
                        c.inc(t.matmul(out=ps_m2[0:1, :], lhsT=ones_b[:],
                                       rhs=sqv[ic % 2][:], start=(ic == 0),
                                       stop=(ic == NCH - 1)), sm2)
                    t.wait_ge(rowd, 2 * tb + 2)
                    t.matmul(out=ps[2 * tb][:], lhsT=ones_r[:], rhs=r1[:],
                             start=True, stop=True)
                    c.inc(t.matmul(out=ps[2 * tb + 1][:], lhsT=ones_r[:], rhs=r3[:],
                                   start=True, stop=True), bcd)
                t.sem_inc(pb, 1)

            @blk.scalar
            def _(sc):
                barrier_wait(sc)
                sc.wait_ge(pev, 64)
                for tb in range(2):
                    for ic in range(NCH):
                        k = NCH * tb + ic
                        if k >= 2:
                            sc.wait_ge(sm2, k - 1)
                        c.inc(sc.activation(out=sqv[ic % 2][:],
                                            in_=xr[:, ic * TPC + 512 * tb:
                                                   ic * TPC + 512 * tb + 512],
                                            func=AF.Square), sqd)
                    sc.wait_ge(rowv, tb + 1)
                    c.inc(sc.activation(out=r3[:], in_=r2[:], func=AF.Sqrt,
                                        bias=eps_s[0:1, :], scale=1.0), ssx)
                sc.sem_inc(pb, 1)

            @blk.vector
            def _(v):
                barrier_wait(v)
                if mode == "qffn":
                    for oc in range(2):
                        v.wait_ge(wld, 16 * (oc + 1))
                        c.inc(v.tensor_copy(out=wb16[oc % 2][:],
                                            in_=wbuf[oc % 2][:].bitcast(I32)), wcst)
                for oc in range(32):
                    for tb in range(2):
                        g = 2 * oc + tb
                        v.wait_ge(mmd, g + 1)
                        bank = ps[g % 4]
                        cols = slice(oc * TPC + 512 * tb, oc * TPC + 512 * tb + 512)
                        if mode == "qffn":
                            i1 = c.inc(v.tensor_scalar(
                                out=u_s[:], in0=bank[:], scalar1=vsc_s[:, oc:oc + 1],
                                scalar2=vbi_s[:, oc:oc + 1], op0=ALU.mult,
                                op1=ALU.add), svx)
                            v.wait_ge(svx, i1)
                            c.inc(v.tensor_tensor(out=xr[:, cols], in0=u_s[:],
                                                  in1=strm[:, cols], op=ALU.add), pev)
                        else:
                            c.inc(v.tensor_tensor(out=xr[:, cols], in0=bank[:],
                                                  in1=strm[:, cols], op=ALU.add), pev)
                    if mode == "qffn" and oc + 2 < 32:
                        v.wait_ge(wld, 16 * (oc + 3))
                        c.inc(v.tensor_copy(out=wb16[oc % 2][:],
                                            in_=wbuf[oc % 2][:].bitcast(I32)), wcst)
                for tb in range(2):
                    v.wait_ge(sm2, NCH * (tb + 1))
                    i1 = c.inc(v.tensor_scalar(out=r1[:], in0=ps_mu[0:1, :],
                                               scalar1=-1.0 / D, scalar2=None,
                                               op0=ALU.mult), svx)
                    i2 = c.inc(v.tensor_scalar(out=r2[:], in0=ps_m2[0:1, :],
                                               scalar1=1.0 / D, scalar2=None,
                                               op0=ALU.mult), svx)
                    v.wait_ge(svx, i2)
                    i3 = c.inc(v.tensor_tensor(out=r3[:], in0=r1[:], in1=r1[:],
                                               op=ALU.mult), svx)
                    v.wait_ge(svx, i3)
                    i4 = c.inc(v.scalar_tensor_tensor(out=r2[:], in0=r3[:],
                                                      scalar=-1.0, in1=r2[:],
                                                      op0=ALU.mult, op1=ALU.add), svx)
                    v.wait_ge(svx, i4)
                    c.inc(v.tensor_copy(out=r2[:], in_=r2[:]), rowv)
                    v.wait_ge(ssx, tb + 1)
                    i5 = c.inc(v.reciprocal(out=r3[:], in_=r3[:]), svx)
                    v.wait_ge(svx, i5)
                    c.inc(v.tensor_copy(out=r1[:], in_=r1[:]), rowd)
                    c.inc(v.tensor_copy(out=r3[:], in_=r3[:]), rowd)
                v.wait_ge(bcd, 2)
                for tb in range(2):
                    c.inc(v.tensor_copy(out=mub[tb][:], in_=ps[2 * tb][:]), svx)
                    jb = c.inc(v.tensor_copy(out=rsb[tb][:], in_=ps[2 * tb + 1][:]), svx)
                v.wait_ge(svx, jb)
                for ic in range(NCH):
                    if final and ic >= 2:
                        v.wait_ge(sto, 16 * (ic - 1))
                    for tb in range(2):
                        cols = slice(ic * TPC + 512 * tb, ic * TPC + 512 * tb + 512)
                        j1 = c.inc(v.tensor_tensor(out=u_s[:].bitcast(BF16)[:, 0:512],
                                                   in0=xr[:, cols],
                                                   in1=mub[tb][:], op=ALU.add), svx)
                        v.wait_ge(svx, j1)
                        lastv = v.tensor_tensor(
                            out=(uov[ic % 2][:, 512 * tb:512 * tb + 512] if final
                                 else strm[:, cols]),
                            in0=u_s[:].bitcast(BF16)[:, 0:512],
                            in1=rsb[tb][:], op=ALU.mult)
                        if tb == 1:
                            c.inc(lastv, appd)
                v.sem_inc(pb, 1)

            @blk.gpsimd
            def _(gp):
                barrier_wait(gp)
                if final:
                    for ic in range(NCH):
                        gp.wait_ge(appd, ic + 1)
                        gp.dma_start(out=out_dst[ic], in_=uov[ic % 2][:]).then_inc(sto, 16)
                    gp.wait_ge(sto, 16 * NCH)
                else:
                    gp.wait_ge(appd, NCH)
                gp.sem_inc(pb, 1)
            phase_n[0] += 1

        def emit_ffn1(x_src, out_dst):
            p = "f1"
            ld = c.sem(p + "ld")
            ldab = c.sem(p + "ldab")
            abct = c.sem(p + "abct")
            rt = c.sem(p + "rt")
            rv = c.sem(p + "rv")
            wld = c.sem(p + "wld")
            mmd = c.sem(p + "mmd")
            pev = c.sem(p + "pev")
            sqd = c.sem(p + "sqd")
            sm2 = c.sem(p + "sm2")
            rowv = c.sem(p + "rowv")
            ssx = c.sem(p + "ssx")
            rowd = c.sem(p + "rowd")
            bcd = c.sem(p + "bcd")
            appd = c.sem(p + "appd")
            sto = c.sem(p + "sto")
            svx = gsvx
            xs = S_XR
            xr = S_XS
            strm = S_XR
            wbuf = [S_W[:, 0:4096], S_W[:, 4096:8192]]
            a_bf = AB[:, 0:4096]
            b_bf = AB[:, 4096:8192]

            @blk.sync
            def _(sy):
                barrier_wait(sy)
                sy.dma_start(out=wbuf[0][:], in_=a_d).then_inc(ldab, 16)
                sy.dma_start(out=wbuf[1][:], in_=b_d).then_inc(ldab, 16)
                sy.wait_ge(abct, 2)
                for oc in range(32):
                    if oc >= 2:
                        sy.wait_ge(pev, 2 * oc - 2)
                    sy.dma_start(out=wbuf[oc % 2][:], in_=wb_d[oc]).then_inc(wld, 16)
                sy.sem_inc(pb, 1)

            @blk.tensor
            def _(t):
                barrier_wait(t)
                t.wait_ge(abct, 2)
                for tt in range(8):
                    bank = [ps_a, ps_b][tt % 2]
                    if tt >= 2:
                        t.wait_ge(rv, tt - 1)
                    mm = None
                    for ic in range(NCH):
                        mm = t.matmul(out=bank[:, 0:8],
                                      lhsT=xs[:, ic * TPC + 128 * tt:ic * TPC + 128 * (tt + 1)],
                                      rhs=wr_s[:].bitcast(BF16)[:, 1:512:2][:, 8 * ic:8 * (ic + 1)],
                                      start=(ic == 0), stop=(ic == NCH - 1))
                    c.inc(mm, rt)
                for tt in range(8):
                    t.wait_ge(rv, 8 + tt + 1)
                    bank = [ps_a, ps_b][tt % 2]
                    if tt >= 2:
                        t.wait_ge(rv, 16 + tt - 1)
                    c.inc(t.transpose(out=bank[0:8, 0:128],
                                      in_=wts_s[:, 8 * tt:8 * (tt + 1)],
                                      identity=idf_s[:]), rt)
                for tb in range(2):
                    t.wait_ge(rv, 24)
                    bank = [ps_a, ps_b][tb]
                    c.inc(t.matmul(out=bank[:], lhsT=d2_s2[:],
                                   rhs=wt_s[0:8, 512 * tb:512 * (tb + 1)],
                                   start=True, stop=True), rt)
                for tb in range(2):
                    t.wait_ge(rv, 24 + tb + 1)
                    bank = [ps_a, ps_b][tb]
                    mm = None
                    for ic in range(NCH):
                        mm = t.matmul(out=bank[:],
                                      lhsT=a_bf[:, 128 * ic:128 * (ic + 1)],
                                      rhs=xs[:, ic * TPC + 512 * tb:ic * TPC + 512 * tb + 512],
                                      start=(ic == 0), stop=(ic == NCH - 1))
                    c.inc(mm, rt)
                for oc in range(32):
                    t.wait_ge(wld, 16 * (oc + 1))
                    wv_ = wbuf[oc % 2][:].bitcast(BF16)[:, 1:2 * D:2]
                    for tb in range(2):
                        g = 2 * oc + tb
                        if g >= 4:
                            t.wait_ge(pev, g - 3)
                        if oc == 0:
                            t.wait_ge(rv, 26 + tb + 1)
                        bank = ps[g % 4]
                        for ic in range(NCH):
                            t.matmul(out=bank[:],
                                     lhsT=wv_[:, 128 * ic:128 * (ic + 1)],
                                     rhs=xs[:, ic * TPC + 512 * tb:
                                            ic * TPC + 512 * tb + 512],
                                     start=(ic == 0), stop=False)
                        c.inc(t.matmul(out=bank[:],
                                       lhsT=b_bf[:, 128 * oc:128 * (oc + 1)],
                                       rhs=g_s[tb][:], start=False, stop=True), mmd)
                t.wait_ge(pev, 64)
                for tb in range(2):
                    if tb == 1:
                        t.wait_ge(rowd, 2)
                    for ic in range(NCH):
                        t.matmul(out=ps_mu[0:1, :], lhsT=ones_b[:],
                                 rhs=xr[:, ic * TPC + 512 * tb:ic * TPC + 512 * tb + 512],
                                 start=(ic == 0), stop=(ic == NCH - 1))
                    for ic in range(NCH):
                        k = NCH * tb + ic
                        t.wait_ge(sqd, k + 1)
                        c.inc(t.matmul(out=ps_m2[0:1, :], lhsT=ones_b[:],
                                       rhs=sq_w[ic % 2][:], start=(ic == 0),
                                       stop=(ic == NCH - 1)), sm2)
                    t.wait_ge(rowd, 2 * tb + 2)
                    t.matmul(out=ps[2 * tb][:], lhsT=ones_r[:], rhs=r1[:],
                             start=True, stop=True)
                    c.inc(t.matmul(out=ps[2 * tb + 1][:], lhsT=ones_r[:], rhs=r3[:],
                                   start=True, stop=True), bcd)
                t.sem_inc(pb, 1)

            @blk.vector
            def _(v):
                barrier_wait(v)
                v.wait_ge(ldab, 32)
                c.inc(v.tensor_copy(out=a_bf[:], in_=wbuf[0][:]), abct)
                c.inc(v.tensor_copy(out=b_bf[:], in_=wbuf[1][:]), abct)
                for tt in range(8):
                    v.wait_ge(rt, tt + 1)
                    bank = [ps_a, ps_b][tt % 2]
                    ecols = slice(8 * tt, 8 * (tt + 1))
                    i1 = c.inc(v.tensor_reduce(out=m1_s[:], in_=bank[:, 0:8],
                                               axis=AX.X, op=ALU.max), svx)
                    v.wait_ge(svx, i1)
                    i2 = c.inc(v.tensor_scalar(out=mneg_s[:], in0=m1_s[:],
                                               scalar1=-1.0, scalar2=None,
                                               op0=ALU.mult), svx)
                    v.wait_ge(svx, i2)
                    i3 = c.inc(v.tensor_scalar(out=lgt_s[:, ecols], in0=bank[:, 0:8],
                                               scalar1=mneg_s[:, :], scalar2=None,
                                               op0=ALU.add), svx)
                    v.wait_ge(svx, i3)
                    c.inc(v.tensor_copy(out=lgt_s[:, ecols], in_=lgt_s[:, ecols]), rv)
                for tt in range(8):
                    v.wait_ge(ssx, tt + 1)
                    ecols = slice(8 * tt, 8 * (tt + 1))
                    i1 = c.inc(v.tensor_reduce(out=m1_s[:], in_=e_s[:, ecols],
                                               axis=AX.X, op=ALU.max), svx)
                    v.wait_ge(svx, i1)
                    i2 = c.inc(v.tensor_scalar(out=msk_s[:], in0=e_s[:, ecols],
                                               scalar1=m1_s[:, :], scalar2=None,
                                               op0=ALU.is_ge), svx)
                    v.wait_ge(svx, i2)
                    i3 = c.inc(v.scalar_tensor_tensor(out=wts_s[:, ecols], in0=msk_s[:],
                                                      scalar=-1000.0, in1=e_s[:, ecols],
                                                      op0=ALU.mult, op1=ALU.add), svx)
                    v.wait_ge(svx, i3)
                    i4 = c.inc(v.tensor_reduce(out=m2_s[:], in_=wts_s[:, ecols],
                                               axis=AX.X, op=ALU.max), svx)
                    v.wait_ge(svx, i4)
                    i5 = c.inc(v.tensor_tensor(out=rcpr_s[:], in0=m1_s[:], in1=m2_s[:],
                                               op=ALU.add), svx)
                    v.wait_ge(svx, i5)
                    i6 = c.inc(v.reciprocal(out=rcpr_s[:], in_=rcpr_s[:]), svx)
                    i7 = c.inc(v.tensor_scalar(out=msk_s[:], in0=e_s[:, ecols],
                                               scalar1=m2_s[:, :], scalar2=None,
                                               op0=ALU.is_ge), svx)
                    v.wait_ge(svx, i7)
                    i8 = c.inc(v.tensor_tensor(out=msk_s[:], in0=msk_s[:],
                                               in1=e_s[:, ecols], op=ALU.mult), svx)
                    v.wait_ge(svx, max(i6, i8))
                    c.inc(v.tensor_scalar(out=wts_s[:, ecols], in0=msk_s[:],
                                          scalar1=rcpr_s[:, :], scalar2=None,
                                          op0=ALU.mult), rv)
                for tt in range(8):
                    v.wait_ge(rt, 8 + tt + 1)
                    bank = [ps_a, ps_b][tt % 2]
                    c.inc(v.tensor_copy(out=wt_s[0:8, 128 * tt:128 * (tt + 1)],
                                        in_=bank[0:8, 0:128]), rv)
                for tb in range(2):
                    v.wait_ge(rt, 16 + tb + 1)
                    bank = [ps_a, ps_b][tb]
                    c.inc(v.tensor_copy(out=wb_s[:, 512 * tb:512 * (tb + 1)],
                                        in_=bank[:]), rv)
                for tb in range(2):
                    v.wait_ge(rt, 18 + tb + 1)
                    v.wait_ge(rv, 26)
                    bank = [ps_a, ps_b][tb]
                    c.inc(v.tensor_tensor(out=g_s[tb][:], in0=bank[:],
                                          in1=wb_s[:, 512 * tb:512 * (tb + 1)],
                                          op=ALU.mult), rv)
                for oc in range(32):
                    for tb in range(2):
                        g = 2 * oc + tb
                        v.wait_ge(mmd, g + 1)
                        bank = ps[g % 4]
                        cols = slice(oc * TPC + 512 * tb, oc * TPC + 512 * tb + 512)
                        i1 = c.inc(v.tensor_scalar(
                            out=u_s[:], in0=bank[:], scalar1=fb_s[:, oc:oc + 1],
                            scalar2=None, op0=ALU.add), svx)
                        v.wait_ge(svx, i1)
                        c.inc(v.tensor_tensor(out=xr[:, cols], in0=u_s[:],
                                              in1=strm[:, cols], op=ALU.add), pev)
                for tb in range(2):
                    v.wait_ge(sm2, NCH * (tb + 1))
                    i1 = c.inc(v.tensor_scalar(out=r1[:], in0=ps_mu[0:1, :],
                                               scalar1=-1.0 / D, scalar2=None,
                                               op0=ALU.mult), svx)
                    i2 = c.inc(v.tensor_scalar(out=r2[:], in0=ps_m2[0:1, :],
                                               scalar1=1.0 / D, scalar2=None,
                                               op0=ALU.mult), svx)
                    v.wait_ge(svx, i2)
                    i3 = c.inc(v.tensor_tensor(out=r3[:], in0=r1[:], in1=r1[:],
                                               op=ALU.mult), svx)
                    v.wait_ge(svx, i3)
                    i4 = c.inc(v.scalar_tensor_tensor(out=r2[:], in0=r3[:],
                                                      scalar=-1.0, in1=r2[:],
                                                      op0=ALU.mult, op1=ALU.add), svx)
                    v.wait_ge(svx, i4)
                    c.inc(v.tensor_copy(out=r2[:], in_=r2[:]), rowv)
                    v.wait_ge(ssx, 8 + tb + 1)
                    i5 = c.inc(v.reciprocal(out=r3[:], in_=r3[:]), svx)
                    v.wait_ge(svx, i5)
                    c.inc(v.tensor_copy(out=r1[:], in_=r1[:]), rowd)
                    c.inc(v.tensor_copy(out=r3[:], in_=r3[:]), rowd)
                v.wait_ge(bcd, 2)
                for ic in range(NCH):
                    if ic >= 2:
                        v.wait_ge(sto, 16 * (ic - 1))
                    for tb in range(2):
                        cols = slice(ic * TPC + 512 * tb, ic * TPC + 512 * tb + 512)
                        j1 = c.inc(v.tensor_tensor(out=u_s[:], in0=xr[:, cols],
                                                   in1=ps[2 * tb][:], op=ALU.add), svx)
                        v.wait_ge(svx, j1)
                        lastv = v.tensor_tensor(out=uo_w[ic % 2][:, 512 * tb:512 * tb + 512],
                                                in0=u_s[:], in1=ps[2 * tb + 1][:],
                                                op=ALU.mult)
                        if tb == 1:
                            c.inc(lastv, appd)
                v.sem_inc(pb, 1)

            @blk.scalar
            def _(sc):
                barrier_wait(sc)
                for tt in range(8):
                    sc.wait_ge(rv, tt + 1)
                    ecols = slice(8 * tt, 8 * (tt + 1))
                    c.inc(sc.activation(out=e_s[:, ecols], in_=lgt_s[:, ecols],
                                        func=AF.Exp), ssx)
                sc.wait_ge(pev, 64)
                for tb in range(2):
                    for ic in range(NCH):
                        k = NCH * tb + ic
                        if k >= 2:
                            sc.wait_ge(sm2, k - 1)
                        c.inc(sc.activation(out=sq_w[ic % 2][:],
                                            in_=xr[:, ic * TPC + 512 * tb:
                                                   ic * TPC + 512 * tb + 512],
                                            func=AF.Square), sqd)
                    sc.wait_ge(rowv, tb + 1)
                    c.inc(sc.activation(out=r3[:], in_=r2[:], func=AF.Sqrt,
                                        bias=eps_s[0:1, :], scale=1.0), ssx)
                sc.sem_inc(pb, 1)

            @blk.gpsimd
            def _(gp):
                barrier_wait(gp)
                for ic in range(NCH):
                    gp.wait_ge(appd, ic + 1)
                    gp.dma_start(out=out_dst[ic], in_=uo_w[ic % 2][:]).then_inc(sto, 16)
                gp.wait_ge(sto, 16 * NCH)
            phase_n[0] += 1

        emit_const_loads()
        emit_qkv(0)
        emit_attn(0)
        emit_projln("o0", "plain", at_h[0], xs0_h, w_l0["wo0"], xh[1])
        emit_projln("qf", "qffn", xh[1], xh[1], qw_d, xh[2])
        emit_qkv(1)
        emit_attn(1)
        emit_projln("o1", "plain", at_h[1], xh[2], w_l1["wo1"], xh[3])
        emit_ffn1(xh[3], x_o)

    return nc


def kernel_fused(L0_attn_x, L0_attn_Wq, L0_attn_Wk, L0_attn_Wv, L0_attn_Wo,
                 L0_ffn_quantized_weights, L0_ffn_scales, L0_ffn_bias,
                 L1_attn_wq, L1_attn_wk, L1_attn_wv, L1_attn_wo,
                 L1_ffn_W_base, L1_ffn_b_base, L1_ffn_W_router,
                 L1_ffn_lora_A, L1_ffn_lora_B):
    cores = list(range(NCORES))
    toks = [core_tokens(j) for j in cores]
    ident_bf = np.ascontiguousarray(np.eye(128, dtype=np.float32).astype(bfloat16))
    ident_f = np.ascontiguousarray(np.eye(128, dtype=np.float32))
    ones_bm = np.ones((128, 128), bfloat16)
    ones_col = np.ones((128, 1), bfloat16)
    ones_row = np.ones((1, 128), np.float32)
    epsv = np.full((1, 1), LN_EPS, np.float32)
    dpat = diag_patterns()
    cos0, sin0 = rope_tables_l0(S)
    cos1, sin1 = rope_tables_l1(S)
    rope0 = [rope_fm(cos0, sin0, toks[j]) for j in cores]
    rope1 = [rope_fm(cos1, sin1, toks[j]) for j in cores]

    wq0 = colchunk(np.ascontiguousarray(L0_attn_Wq.T))
    wk0 = colchunk(np.ascontiguousarray(L0_attn_Wk.T))
    wv0 = colchunk(np.ascontiguousarray(L0_attn_Wv.T))
    wo0 = colchunk(np.ascontiguousarray(L0_attn_Wo.T))
    qw0 = colchunk(np.ascontiguousarray(L0_ffn_quantized_weights.T))
    sc0 = np.ascontiguousarray(L0_ffn_scales.reshape(32, 128).T)
    bi0 = np.ascontiguousarray(L0_ffn_bias.reshape(32, 128).T)
    wq1 = colchunk(np.ascontiguousarray(L1_attn_wq))
    wk1 = colchunk(np.ascontiguousarray(L1_attn_wk))
    wv1 = colchunk(np.ascontiguousarray(L1_attn_wv))
    wo1 = colchunk(np.ascontiguousarray(L1_attn_wo))
    wb1 = colchunk(np.ascontiguousarray(L1_ffn_W_base.T))
    fb1 = np.ascontiguousarray(L1_ffn_b_base.reshape(32, 128).T)
    wr1 = np.ascontiguousarray(
        L1_ffn_W_router.reshape(8, 32, 128).transpose(2, 1, 0).reshape(128, 256))
    la1 = np.ascontiguousarray(
        L1_ffn_lora_A.reshape(128, D).T.reshape(32, 128, 128)
        .transpose(1, 0, 2).reshape(128, 4096))
    lb1 = np.ascontiguousarray(L1_ffn_lora_B.transpose(0, 2, 1).reshape(128, D))
    d2 = np.zeros((8, 128), np.float32)
    for e in range(8):
        d2[e, 16 * e:16 * (e + 1)] = 2.0
    x_fm = [np.ascontiguousarray(L0_attn_x[j // 4].T[:, toks[j]]) for j in cores]

    nc = _prog("fused", build_fused)
    maps = []
    for j in cores:
        maps.append({
            "x": _stack(x_fm[j]), "wq0": wq0, "wk0": wk0, "wv0": wv0, "wo0": wo0,
            "qw": qw0, "scales": sc0, "qbias": bi0,
            "wq1": wq1, "wk1": wk1, "wv1": wv1, "wo1": wo1,
            "wbase": wb1, "fbias": fb1, "wr": wr1, "la": la1, "lb": lb1, "d2": d2,
            "cos0": rope0[j][0], "sin0": rope0[j][1],
            "cos1": rope1[j][0], "sin1": rope1[j][1],
            "ident": ident_bf, "idf": ident_f, "onesb": ones_col,
            "onesbm": ones_bm, "onesr": ones_row, "epsv": epsv,
            "dpat": dpat, "ebias": fused_ebias(j),
        })
    res = run_bass_kernel_spmd(nc, maps, core_ids=cores)
    out = np.empty((B, S, D), np.float32)
    for j in cores:
        b = j // 4
        out[b, toks[j], :] = res.results[j]["xout"].reshape(D, TPC).T
    return out


def kernel(**inputs):
    return kernel_fused(**inputs)


# version 11
# speedup vs baseline: 1.0715x; 1.0603x over previous
import numpy as np
import ml_dtypes
from contextlib import ExitStack

import concourse.bass as bass
import concourse.mybir as mybir
from concourse.bass_utils import run_bass_kernel_spmd

BF16 = mybir.dt.bfloat16
F32 = mybir.dt.float32
I32 = mybir.dt.int32
AF = mybir.ActivationFunctionType
ALU = mybir.AluOpType
AX = mybir.AxisListType
bfloat16 = ml_dtypes.bfloat16

NCORES = 8
B = 2
S = 4096
D = 4096
HD = 128
TPC = 1024
NCH = D // 128
NKT = (20, 32)
NT = NKT[0] + NKT[1]
EXP_SHIFT = -30.0
PAD_BIAS = -1.0e9
LN_EPS = 1e-5


def core_chunks(j):
    q = j % 4
    return q, 7 - q


def core_tokens(j):
    c0, c1 = core_chunks(j)
    return np.concatenate([np.arange(512 * c0, 512 * (c0 + 1)),
                           np.arange(512 * c1, 512 * (c1 + 1))])


def colchunk(Weff):
    I, O = Weff.shape
    return np.ascontiguousarray(
        Weff.reshape(I // 128, 128, O // 128, 128).transpose(2, 1, 0, 3)
        .reshape(O // 128, 128, I))


def rope_tables_l0(n):
    inv = 1.0 / (10000.0 ** (np.arange(0, HD, 2, dtype=np.float32) / HD))
    fr = np.arange(n, dtype=np.float32)[:, None] * inv[None, :]
    emb = np.concatenate([fr, fr], -1)
    return np.cos(emb).astype(np.float32), np.sin(emb).astype(np.float32)


def rope_tables_l1(n):
    TH, FA, HI, LO, OM = 500000.0, 8.0, 4.0, 1.0, 8192
    inv = 1.0 / TH ** (np.arange(0, HD, 2, dtype=np.float64) / HD)
    wl = 2 * np.pi / inv
    sm = (OM / wl - LO) / (HI - LO)
    sc = np.where(wl > OM / LO, inv / FA, inv)
    mid = (wl <= OM / LO) & (wl >= OM / HI)
    sc = np.where(mid, (1 - sm) * inv / FA + sm * inv, sc).astype(np.float32)
    fr = np.arange(n, dtype=np.float32)[:, None] * sc[None, :]
    emb = np.concatenate([fr, fr], -1)
    return np.cos(emb).astype(np.float32), np.sin(emb).astype(np.float32)


def rope_fm(cos, sin, toks):
    c = np.ascontiguousarray(cos[toks, :].T)
    s = sin[toks, :].T.copy()
    s[0:64, :] *= -1.0
    return c, np.ascontiguousarray(s)


def diag_patterns():
    pats = np.zeros((4, 128, 512), np.float32)
    r = np.arange(128)[:, None]
    cc = np.arange(128)[None, :]
    tri = (r <= cc).astype(np.float32)
    for v in range(4):
        pats[v, :, 128 * v:128 * (v + 1)] = tri
        pats[v, :, 128 * (v + 1):] = 1.0
    return np.ascontiguousarray(pats.astype(bfloat16))


def pack_kv(core_j, full_fm):
    c0, c1 = core_chunks(core_j)
    Hkv = full_fm.shape[0]
    out = np.zeros((Hkv, 128, 128 * NT), bfloat16)
    bias = np.full(NT, PAD_BIAS, np.float32)
    for qb, ch in ((0, c0), (1, c1)):
        base = 0 if qb == 0 else NKT[0]
        nfull = 4 * ch
        out[:, :, 128 * base:128 * (base + nfull)] = full_fm[:, :, :128 * nfull]
        bias[base:base + nfull] = EXP_SHIFT
        dbase = base + NKT[qb] - 4
        out[:, :, 128 * dbase:128 * (dbase + 4)] = \
            full_fm[:, :, 128 * nfull:128 * (nfull + 4)]
        bias[dbase:dbase + 4] = EXP_SHIFT
    return out, bias



class Ctx:
    def __init__(self):
        self.nc = bass.Bass("TRN2", target_bir_lowering=False)
        self.es = ExitStack()
        self.cnt = {}

    def sbuf(self, name, shape, dt):
        return self.es.enter_context(self.nc.sbuf_tensor(name, shape, dt))

    def psum(self, name, shape, dt):
        return self.es.enter_context(self.nc.psum_tensor(name, shape, dt))

    def sem(self, name):
        h = self.es.enter_context(self.nc.semaphore(name))
        self.cnt[id(h)] = 0
        return h

    def inc(self, inst, semh, n=1):
        self.cnt[id(semh)] += n
        inst.then_inc(semh, n)
        return self.cnt[id(semh)]

    def val(self, semh):
        return self.cnt[id(semh)]


def bfview(t, cols):
    return t[:].bitcast(BF16)[:, 1:2 * cols:2]



def build_qkv(layer, only=None):
    nkv = 32 if layer == 0 else 8
    c = Ctx()
    nc = c.nc
    x_in = nc.dram_tensor("x", [NCH, 128, TPC], F32 if layer == 0 else BF16,
                          kind="ExternalInput").ap()
    wq_d = nc.dram_tensor("wq", [32, 128, D], F32, kind="ExternalInput").ap()
    wk_d = nc.dram_tensor("wk", [nkv, 128, D], F32, kind="ExternalInput").ap()
    wv_d = nc.dram_tensor("wv", [nkv, 128, D], F32, kind="ExternalInput").ap()
    cos_d = nc.dram_tensor("cos", [128, TPC], F32, kind="ExternalInput").ap()
    sin_d = nc.dram_tensor("sinS", [128, TPC], F32, kind="ExternalInput").ap()
    id_d = nc.dram_tensor("ident", [128, 128], BF16, kind="ExternalInput").ap()
    q_o = nc.dram_tensor("q", [32, 128, TPC], BF16, kind="ExternalOutput").ap()
    k_o = nc.dram_tensor("k", [nkv, 128, TPC], BF16, kind="ExternalOutput").ap()
    v_o = nc.dram_tensor("v", [nkv, 128, TPC], BF16, kind="ExternalOutput").ap()
    xs_o = None
    if layer == 0:
        xs_o = nc.dram_tensor("xs", [NCH, 128, TPC], BF16, kind="ExternalOutput").ap()

    projs = [("v", wv_d, nkv), ("k", wk_d, nkv), ("q", wq_d, 32)]
    if only is not None:
        projs = [p for p in projs if p[0] in only]

    with c.es:
        xs = c.sbuf("xs_s", [128, NCH * TPC], BF16)
        xl = c.sbuf("xl", [128, 2 * TPC], F32)
        wbuf = [c.sbuf("w0", [128, D], F32), c.sbuf("w1", [128, D], F32)]
        cos_s = c.sbuf("cos_s", [128, TPC], F32)
        sin_s = c.sbuf("sin_s", [128, TPC], F32)
        id_s = c.sbuf("id_s", [128, 128], BF16)
        tmp = c.sbuf("tmp", [128, 512], F32)
        u = c.sbuf("u", [128, 512], F32)
        qk_out = [c.sbuf("qo%d" % i, [128, TPC], BF16) for i in range(2)]
        vfm = [c.sbuf("vf%d" % i, [128, TPC], BF16) for i in range(2)]
        vtm = [c.sbuf("vt%d" % i, [128, TPC], BF16) for i in range(2)]
        ps = [c.psum("ps%d" % i, [128, 512], F32) for i in range(4)]
        ps_tr = [c.psum("ps_tr%d" % i, [128, 1024], BF16) for i in range(2)]

        ld = c.sem("ld")
        ldx = c.sem("ldx")
        wld = c.sem("wld")
        wfree = c.sem("wfree")
        mmd = c.sem("mmd")
        pev = c.sem("pev")
        xca = c.sem("xca")
        trd = c.sem("trd")
        tre = c.sem("tre")
        sto = c.sem("sto")
        svx = c.sem("svx")

        blk = c.es.enter_context(nc.Block())

        n_xs_st = NCH if layer == 0 else 0
        store_base = {}
        _si = n_xs_st
        for _pn, _wd, _noc in projs:
            store_base[_pn] = _si
            _si += _noc

        @blk.sync
        def _(sy):
            sy.dma_start(out=cos_s[:], in_=cos_d).then_inc(ld, 16)
            sy.dma_start(out=sin_s[:], in_=sin_d).then_inc(ld, 16)
            sy.dma_start(out=id_s[:], in_=id_d).then_inc(ld, 16)
            for ic in range(NCH):
                if layer == 0:
                    if ic >= 2:
                        sy.wait_ge(xca, ic - 1)
                    sy.dma_start(out=xl[:, (ic % 2) * TPC:(ic % 2 + 1) * TPC],
                                 in_=x_in[ic]).then_inc(ldx, 16)
                else:
                    sy.dma_start(out=xs[:, ic * TPC:(ic + 1) * TPC],
                                 in_=x_in[ic]).then_inc(ldx, 16)
            si = 0
            for _pn, wd, noc in projs:
                for oc in range(noc):
                    if si >= 2:
                        sy.wait_ge(pev, 2 * si - 2)
                    sy.dma_start(out=wbuf[si % 2][:], in_=wd[oc]).then_inc(wld, 16)
                    si += 1

        if layer == 0:
            @blk.vector
            def _(v):
                for ic in range(NCH):
                    v.wait_ge(ldx, 16 * (ic + 1))
                    c.inc(v.tensor_copy(out=xs[:, ic * TPC:(ic + 1) * TPC],
                                        in_=xl[:, (ic % 2) * TPC:(ic % 2 + 1) * TPC]),
                          xca)

        @blk.tensor
        def _(t):
            if layer == 0:
                t.wait_ge(xca, NCH)
            else:
                t.wait_ge(ldx, 16 * NCH)
            si = 0
            g = 0

            def v_transpose(t, oc):
                t.wait_ge(pev, 2 * oc + 2)
                if oc >= 2:
                    t.wait_ge(tre, oc - 1)
                tr = None
                for tt in range(8):
                    tr = t.transpose(out=ps_tr[oc % 2][:, 128 * tt:128 * (tt + 1)],
                                     in_=vfm[oc % 2][:, 128 * tt:128 * (tt + 1)],
                                     identity=id_s[:])
                c.inc(tr, trd)

            for _pn, wd, noc in projs:
                for oc in range(noc):
                    t.wait_ge(wld, 16 * (si + 1))
                    wv_ = bfview(wbuf[si % 2], D)
                    for tb in range(2):
                        if g >= 4:
                            t.wait_ge(pev, g - 3)
                        bank = ps[g % 4]
                        mm = None
                        for ic in range(NCH):
                            mm = t.matmul(out=bank[:],
                                          lhsT=wv_[:, 128 * ic:128 * (ic + 1)],
                                          rhs=xs[:, ic * TPC + 512 * tb:
                                                 ic * TPC + 512 * tb + 512],
                                          start=(ic == 0), stop=(ic == NCH - 1))
                        c.inc(mm, mmd)
                        g += 1
                    si += 1
                    if _pn == "v" and oc >= 2:
                        v_transpose(t, oc - 2)
                if _pn == "v":
                    v_transpose(t, nkv - 2)
                    v_transpose(t, nkv - 1)

        @blk.vector
        def _(v):
            v.wait_ge(ld, 48)
            g = 0

            def vtm_evac(v, oc):
                v.wait_ge(trd, oc + 1)
                if oc >= 2:
                    v.wait_ge(sto, 16 * (store_base["v"] + oc - 1))
                c.inc(v.tensor_copy(out=vtm[oc % 2][:], in_=ps_tr[oc % 2][:]), tre)

            if any(p[0] == "v" for p in projs):
                for oc in range(nkv):
                    for tb in range(2):
                        v.wait_ge(mmd, g + 1)
                        bank = ps[g % 4]
                        cols = slice(512 * tb, 512 * tb + 512)
                        if oc >= 2 and tb == 0:
                            v.wait_ge(trd, oc - 1)
                        c.inc(v.tensor_copy(out=vfm[oc % 2][:, cols],
                                            in_=bank[:]), pev)
                        g += 1
                    if oc >= 2:
                        vtm_evac(v, oc - 2)
                vtm_evac(v, nkv - 2)
                vtm_evac(v, nkv - 1)
            for pn, noc in [(p[0], p[2]) for p in projs if p[0] != "v"]:
                for oc in range(noc):
                    for tb in range(2):
                        v.wait_ge(mmd, g + 1)
                        bank = ps[g % 4]
                        cols = slice(512 * tb, 512 * tb + 512)
                        if tb == 0 and oc >= 2:
                            v.wait_ge(sto, 16 * (store_base[pn] + oc - 1))
                        i1 = c.inc(v.tensor_tensor(
                            out=tmp[0:64, :], in0=bank[64:128, :],
                            in1=sin_s[0:64, cols], op=ALU.mult), svx)
                        i2 = c.inc(v.tensor_tensor(
                            out=tmp[64:128, :], in0=bank[0:64, :],
                            in1=sin_s[64:128, cols], op=ALU.mult), svx)
                        i3 = c.inc(v.tensor_tensor(
                            out=u[:], in0=bank[:], in1=cos_s[:, cols],
                            op=ALU.mult), svx)
                        v.wait_ge(svx, i3)
                        c.inc(v.tensor_tensor(
                            out=qk_out[oc % 2][:, cols], in0=u[:],
                            in1=tmp[:], op=ALU.add), pev)
                        g += 1

        @blk.gpsimd
        def _(gp):
            nst = 0
            if layer == 0:
                for ic in range(NCH):
                    gp.wait_ge(xca, ic + 1)
                    gp.dma_start(out=xs_o[ic],
                                 in_=xs[:, ic * TPC:(ic + 1) * TPC]).then_inc(sto, 16)
                    nst += 1
            base_g = 0
            for pn, _wd, noc in projs:
                if pn == "v":
                    for oc in range(nkv):
                        gp.wait_ge(tre, oc + 1)
                        gp.dma_start(out=v_o[oc], in_=vtm[oc % 2][:]).then_inc(sto, 16)
                        nst += 1
                else:
                    out_d = k_o if pn == "k" else q_o
                    for oc in range(noc):
                        gp.wait_ge(pev, base_g + 2 * oc + 2)
                        gp.dma_start(out=out_d[oc], in_=qk_out[oc % 2][:]).then_inc(sto, 16)
                        nst += 1
                base_g += 2 * noc
            gp.wait_ge(sto, 16 * nst)

    return nc



def build_attn(layer):
    nkv = 32 if layer == 0 else 8
    rep = 32 // nkv
    c = Ctx()
    nc = c.nc
    q_in = nc.dram_tensor("q", [32, 128, TPC], BF16, kind="ExternalInput").ap()
    k_in = nc.dram_tensor("kp", [nkv, 128, 128 * NT], BF16, kind="ExternalInput").ap()
    v_in = nc.dram_tensor("vp", [nkv, 128, 128 * NT], BF16, kind="ExternalInput").ap()
    bias_in = nc.dram_tensor("ebias", [128, NT], F32, kind="ExternalInput").ap()
    dpat_in = nc.dram_tensor("dpat", [4, 128, 512], BF16, kind="ExternalInput").ap()
    ones_in = nc.dram_tensor("onesb", [128, 128], BF16, kind="ExternalInput").ap()
    a_o = nc.dram_tensor("attn", [32, 128, TPC], BF16, kind="ExternalOutput").ap()

    def kt_seq():
        gkt = 0
        for h in range(32):
            for qb in range(2):
                for kt in range(NKT[qb]):
                    yield h, qb, kt, gkt, 2 * h + qb
                    gkt += 1

    with c.es:
        q_s = [c.sbuf("q%d" % i, [128, TPC], BF16) for i in range(2)]
        k_s = [c.sbuf("k%d" % i, [128, 128 * NT], BF16) for i in range(2)]
        v_s = [c.sbuf("v%d" % i, [128, 128 * NT], BF16) for i in range(2)]
        pt = [c.sbuf("pt%d" % i, [128, 512], BF16) for i in range(4)]
        bias_s = c.sbuf("bias_s", [128, NT], F32)
        dpat_s = c.sbuf("dpat_s", [128, 4 * 512], BF16)
        ones_s = c.sbuf("ones_s", [128, 128], BF16)
        rcp_s = c.sbuf("rcp_s", [128, 512], F32)
        ao_s = [c.sbuf("ao%d" % i, [128, TPC], BF16) for i in range(2)]
        ps_s = [c.psum("pss%d" % i, [128, 512], F32) for i in range(2)]
        ps_o = [c.psum("pso%d" % i, [128, 512], F32) for i in range(2)]
        ps_z = [c.psum("psz%d" % i, [128, 512], F32) for i in range(2)]

        ld = c.sem("ld")
        qld = c.sem("qld")
        kvld = c.sem("kvld")
        smm = c.sem("smm")
        sexp = c.sem("sexp")
        sdg = c.sem("sdg")
        spv = c.sem("spv")
        sev = c.sem("sev")
        sdone = c.sem("sdone")
        svx = c.sem("svx")

        blk = c.es.enter_context(nc.Block())

        @blk.sync
        def _(sy):
            sy.dma_start(out=bias_s[:], in_=bias_in).then_inc(ld, 16)
            for a_ in range(4):
                sy.dma_start(out=dpat_s[:, 512 * a_:512 * (a_ + 1)],
                             in_=dpat_in[a_]).then_inc(ld, 16)
            sy.dma_start(out=ones_s[:], in_=ones_in).then_inc(ld, 16)
            for h in range(32):
                if h >= 2:
                    sy.wait_ge(sev, 2 * (h - 1))
                sy.dma_start(out=q_s[h % 2][:], in_=q_in[h]).then_inc(qld, 16)
                if h % rep == 0:
                    kv = h // rep
                    if kv >= 2:
                        sy.wait_ge(sev, 2 * rep * (kv - 1))
                    sy.dma_start(out=k_s[kv % 2][:], in_=k_in[kv]).then_inc(kvld, 16)
                    sy.dma_start(out=v_s[kv % 2][:], in_=v_in[kv]).then_inc(kvld, 16)

        @blk.tensor
        def _(t):
            t.wait_ge(ld, 96)
            for h in range(32):
                t.wait_ge(qld, 16 * (h + 1))
                t.wait_ge(kvld, 32 * (h // rep + 1))
                kbuf = k_s[(h // rep) % 2]
                vbuf = v_s[(h // rep) % 2]
                for qb in range(2):
                    nkt = NKT[qb]
                    base = 0 if qb == 0 else NKT[0]
                    g2 = 2 * h + qb
                    gkt0 = 52 * h + base
                    qcols = slice(512 * qb, 512 * qb + 512)
                    if g2 >= 2:
                        t.wait_ge(sev, g2 - 1)

                    def scores(kt):
                        gkt = gkt0 + kt
                        if gkt >= 2:
                            t.wait_ge(sexp, gkt - 1)
                        c.inc(t.matmul(
                            out=ps_s[gkt % 2][:],
                            lhsT=kbuf[:, 128 * (base + kt):128 * (base + kt + 1)],
                            rhs=q_s[h % 2][:, qcols], start=True, stop=True), smm)

                    def pv(kt):
                        gkt = gkt0 + kt
                        if kt >= nkt - 4:
                            t.wait_ge(sdg, 4 * g2 + (kt - (nkt - 4)) + 1)
                        else:
                            t.wait_ge(sexp, gkt + 1)
                        ptile = pt[(base + kt) % 4]
                        t.matmul(out=ps_o[g2 % 2][:],
                                 lhsT=vbuf[:, 128 * (base + kt):128 * (base + kt + 1)],
                                 rhs=ptile[:], start=(kt == 0), stop=(kt == nkt - 1))
                        c.inc(t.matmul(out=ps_z[g2 % 2][:], lhsT=ones_s[:],
                                       rhs=ptile[:], start=(kt == 0),
                                       stop=(kt == nkt - 1)), spv)

                    scores(0)
                    for kt in range(1, nkt):
                        scores(kt)
                        pv(kt - 1)
                    pv(nkt - 1)

        @blk.scalar
        def _(sc):
            sc.wait_ge(ld, 96)
            for h, qb, kt, gkt, g2 in kt_seq():
                base = 0 if qb == 0 else NKT[0]
                sc.wait_ge(smm, gkt + 1)
                if gkt >= 4:
                    sc.wait_ge(spv, gkt - 3)
                c.inc(sc.activation(out=pt[(base + kt) % 4][:],
                                    in_=ps_s[gkt % 2][:], func=AF.Exp,
                                    bias=bias_s[:, base + kt:base + kt + 1],
                                    scale=float(HD) ** -0.5), sexp)

        @blk.vector
        def _(v):
            v.wait_ge(ld, 96)
            for h in range(32):
                if h >= 2:
                    v.wait_ge(sdone, 16 * (h - 1))
                for qb in range(2):
                    nkt = NKT[qb]
                    base = 0 if qb == 0 else NKT[0]
                    g2 = 2 * h + qb
                    gkt0 = 52 * h + base
                    for kt in range(nkt - 4, nkt):
                        vv = kt - (nkt - 4)
                        v.wait_ge(sexp, gkt0 + kt + 1)
                        ptile = pt[(base + kt) % 4]
                        c.inc(v.tensor_tensor(
                            out=ptile[:], in0=ptile[:],
                            in1=dpat_s[:, 512 * vv:512 * (vv + 1)],
                            op=ALU.mult), sdg)
                    v.wait_ge(spv, gkt0 + nkt)
                    i1 = c.inc(v.reciprocal(out=rcp_s[:], in_=ps_z[g2 % 2][:]), svx)
                    v.wait_ge(svx, i1)
                    c.inc(v.tensor_tensor(
                        out=ao_s[h % 2][:, 512 * qb:512 * qb + 512],
                        in0=ps_o[g2 % 2][:], in1=rcp_s[:], op=ALU.mult), sev)

        @blk.gpsimd
        def _(gp):
            for h in range(32):
                gp.wait_ge(sev, 2 * h + 2)
                gp.dma_start(out=a_o[h], in_=ao_s[h % 2][:]).then_inc(sdone, 16)
            gp.wait_ge(sdone, 16 * 32)

    return nc



def build_proj_ln(mode, final=False):
    c = Ctx()
    nc = c.nc
    wdt = I32 if mode == "qffn" else F32
    x_in = nc.dram_tensor("xin", [NCH, 128, TPC], BF16, kind="ExternalInput").ap()
    xr_in = nc.dram_tensor("xres", [NCH, 128, TPC], BF16, kind="ExternalInput").ap()
    w_d = nc.dram_tensor("w", [32, 128, D], wdt, kind="ExternalInput").ap()
    ob_d = nc.dram_tensor("onesb", [128, 1], BF16, kind="ExternalInput").ap()
    or_d = nc.dram_tensor("onesr", [1, 128], F32, kind="ExternalInput").ap()
    eps_d = nc.dram_tensor("epsv", [1, 1], F32, kind="ExternalInput").ap()
    if mode == "qffn":
        sc_d = nc.dram_tensor("scales", [128, 32], F32, kind="ExternalInput").ap()
        bi_d = nc.dram_tensor("fbias", [128, 32], F32, kind="ExternalInput").ap()
    out_dt = F32 if final else BF16
    x_o = nc.dram_tensor("xout", [NCH, 128, TPC], out_dt, kind="ExternalOutput").ap()

    with c.es:
        xs = c.sbuf("xs_s", [128, NCH * TPC], BF16)
        xr = c.sbuf("xr_s", [128, NCH * TPC], BF16)
        wbuf = [c.sbuf("w%d" % i, [128, D], wdt) for i in range(2)]
        wb16 = [c.sbuf("wb%d" % i, [128, D], BF16) for i in range(2)] \
            if mode == "qffn" else None
        ones_b = c.sbuf("ones_b", [128, 1], BF16)
        ones_r = c.sbuf("ones_r", [1, 128], F32)
        eps_s = c.sbuf("eps_s", [1, 1], F32)
        sq = [c.sbuf("sq%d" % i, [128, 512], BF16) for i in range(2)]
        u = c.sbuf("u", [128, 512], F32)
        uo = [c.sbuf("uo%d" % i, [128, TPC], out_dt) for i in range(2)]
        r1 = c.sbuf("r1", [1, 512], F32)
        r2 = c.sbuf("r2", [1, 512], F32)
        r3 = c.sbuf("r3", [1, 512], F32)
        if mode == "qffn":
            sc_s = c.sbuf("sc_s", [128, 32], F32)
            bi_s = c.sbuf("bi_s", [128, 32], F32)
        ps = [c.psum("ps%d" % i, [128, 512], F32) for i in range(4)]
        ps_mu = c.psum("ps_mu", [128, 512], F32)
        ps_m2 = c.psum("ps_m2", [128, 512], F32)

        ld = c.sem("ld")
        wld = c.sem("wld")
        wfree = c.sem("wfree")
        wcst = c.sem("wcst")
        mmd = c.sem("mmd")
        pev = c.sem("pev")
        sqd = c.sem("sqd")
        sm2 = c.sem("sm2")
        stat = c.sem("stat")
        rowv = c.sem("rowv")
        ssx = c.sem("ssx")
        rowd = c.sem("rowd")
        bcd = c.sem("bcd")
        appd = c.sem("appd")
        sto = c.sem("sto")
        svx = c.sem("svx")

        blk = c.es.enter_context(nc.Block())
        n_misc = 3 + (2 if mode == "qffn" else 0)

        @blk.sync
        def _(sy):
            sy.dma_start(out=ones_b[:], in_=ob_d).then_inc(ld, 16)
            sy.dma_start(out=ones_r[:], in_=or_d).then_inc(ld, 16)
            sy.dma_start(out=eps_s[:], in_=eps_d).then_inc(ld, 16)
            if mode == "qffn":
                sy.dma_start(out=sc_s[:], in_=sc_d).then_inc(ld, 16)
                sy.dma_start(out=bi_s[:], in_=bi_d).then_inc(ld, 16)
            for ic in range(NCH):
                sy.dma_start(out=xs[:, ic * TPC:(ic + 1) * TPC],
                             in_=x_in[ic]).then_inc(ld, 16)
            for ic in range(NCH):
                sy.dma_start(out=xr[:, ic * TPC:(ic + 1) * TPC],
                             in_=xr_in[ic]).then_inc(ld, 16)
            for oc in range(32):
                if oc >= 2:
                    if mode == "qffn":
                        sy.wait_ge(wcst, oc - 1)
                    else:
                        sy.wait_ge(pev, 2 * oc - 2)
                sy.dma_start(out=wbuf[oc % 2][:], in_=w_d[oc]).then_inc(wld, 16)

        ld_all = 16 * (n_misc + 2 * NCH)


        @blk.tensor
        def _(t):
            t.wait_ge(ld, ld_all)
            for oc in range(32):
                if mode == "qffn":
                    t.wait_ge(wcst, oc + 1)
                    wv_ = wb16[oc % 2][:]
                else:
                    t.wait_ge(wld, 16 * (oc + 1))
                    wv_ = bfview(wbuf[oc % 2], D)
                for tb in range(2):
                    g = 2 * oc + tb
                    if g >= 4:
                        t.wait_ge(pev, g - 3)
                    bank = ps[g % 4]
                    mm = None
                    for ic in range(NCH):
                        mm = t.matmul(out=bank[:],
                                      lhsT=wv_[:, 128 * ic:128 * (ic + 1)],
                                      rhs=xs[:, ic * TPC + 512 * tb:
                                             ic * TPC + 512 * tb + 512],
                                      start=(ic == 0), stop=(ic == NCH - 1))
                    c.inc(mm, mmd)
            t.wait_ge(pev, 64)
            for tb in range(2):
                if tb == 1:
                    t.wait_ge(rowd, 2)
                for ic in range(NCH):
                    t.matmul(out=ps_mu[0:1, :], lhsT=ones_b[:],
                             rhs=xr[:, ic * TPC + 512 * tb:ic * TPC + 512 * tb + 512],
                             start=(ic == 0), stop=(ic == NCH - 1))
                for ic in range(NCH):
                    k = NCH * tb + ic
                    t.wait_ge(sqd, k + 1)
                    c.inc(t.matmul(out=ps_m2[0:1, :], lhsT=ones_b[:],
                                   rhs=sq[ic % 2][:],
                                   start=(ic == 0), stop=(ic == NCH - 1)), sm2)
                t.wait_ge(rowd, 2 * tb + 2)
                t.matmul(out=ps[2 * tb][:], lhsT=ones_r[:], rhs=r1[:],
                         start=True, stop=True)
                c.inc(t.matmul(out=ps[2 * tb + 1][:], lhsT=ones_r[:], rhs=r3[:],
                               start=True, stop=True), bcd)

        @blk.scalar
        def _(sc):
            sc.wait_ge(ld, ld_all)
            sc.wait_ge(pev, 64)
            for tb in range(2):
                for ic in range(NCH):
                    k = NCH * tb + ic
                    if k >= 2:
                        sc.wait_ge(sm2, k - 1)
                    c.inc(sc.activation(out=sq[ic % 2][:],
                                        in_=xr[:, ic * TPC + 512 * tb:
                                               ic * TPC + 512 * tb + 512],
                                        func=AF.Square), sqd)
                sc.wait_ge(rowv, tb + 1)
                c.inc(sc.activation(out=r3[:], in_=r2[:], func=AF.Sqrt,
                                    bias=eps_s[0:1, :], scale=1.0), ssx)

        @blk.vector
        def _(v):
            if mode == "qffn":
                for oc in range(2):
                    v.wait_ge(wld, 16 * (oc + 1))
                    c.inc(v.tensor_copy(out=wb16[oc % 2][:], in_=wbuf[oc % 2][:]),
                          wcst)
            for oc in range(32):
                for tb in range(2):
                    g = 2 * oc + tb
                    v.wait_ge(mmd, g + 1)
                    bank = ps[g % 4]
                    cols = slice(oc * TPC + 512 * tb, oc * TPC + 512 * tb + 512)
                    if mode == "qffn":
                        i1 = c.inc(v.tensor_scalar(
                            out=u[:], in0=bank[:], scalar1=sc_s[:, oc:oc + 1],
                            scalar2=bi_s[:, oc:oc + 1], op0=ALU.mult,
                            op1=ALU.add), svx)
                        v.wait_ge(svx, i1)
                        c.inc(v.tensor_tensor(out=xr[:, cols], in0=u[:],
                                              in1=xr[:, cols], op=ALU.add), pev)
                    else:
                        c.inc(v.tensor_tensor(out=xr[:, cols], in0=bank[:],
                                              in1=xr[:, cols], op=ALU.add), pev)
                if mode == "qffn" and oc + 2 < 32:
                    v.wait_ge(wld, 16 * (oc + 3))
                    c.inc(v.tensor_copy(out=wb16[oc % 2][:], in_=wbuf[oc % 2][:]),
                          wcst)
            for tb in range(2):
                v.wait_ge(sm2, NCH * (tb + 1))
                i1 = c.inc(v.tensor_scalar(out=r1[:], in0=ps_mu[0:1, :],
                                           scalar1=-1.0 / D, scalar2=None,
                                           op0=ALU.mult), svx)
                i2 = c.inc(v.tensor_scalar(out=r2[:], in0=ps_m2[0:1, :],
                                           scalar1=1.0 / D, scalar2=None,
                                           op0=ALU.mult), svx)
                v.wait_ge(svx, i2)
                i3 = c.inc(v.tensor_tensor(out=r3[:], in0=r1[:], in1=r1[:],
                                           op=ALU.mult), svx)
                v.wait_ge(svx, i3)
                i4 = c.inc(v.scalar_tensor_tensor(out=r2[:], in0=r3[:],
                                                  scalar=-1.0, in1=r2[:],
                                                  op0=ALU.mult, op1=ALU.add), svx)
                v.wait_ge(svx, i4)
                c.inc(v.tensor_copy(out=r2[:], in_=r2[:]), rowv)
                v.wait_ge(ssx, tb + 1)
                i5 = c.inc(v.reciprocal(out=r3[:], in_=r3[:]), svx)
                v.wait_ge(svx, i5)
                c.inc(v.tensor_copy(out=r1[:], in_=r1[:]), rowd)
                c.inc(v.tensor_copy(out=r3[:], in_=r3[:]), rowd)
            v.wait_ge(bcd, 2)
            for ic in range(NCH):
                if ic >= 2:
                    v.wait_ge(sto, 16 * (ic - 1))
                for tb in range(2):
                    cols = slice(ic * TPC + 512 * tb, ic * TPC + 512 * tb + 512)
                    j1 = c.inc(v.tensor_tensor(out=u[:], in0=xr[:, cols],
                                               in1=ps[2 * tb][:], op=ALU.add), svx)
                    v.wait_ge(svx, j1)
                    last = v.tensor_tensor(out=uo[ic % 2][:, 512 * tb:512 * tb + 512],
                                           in0=u[:], in1=ps[2 * tb + 1][:],
                                           op=ALU.mult)
                    if tb == 1:
                        c.inc(last, appd)

        @blk.gpsimd
        def _(gp):
            for ic in range(NCH):
                gp.wait_ge(appd, ic + 1)
                gp.dma_start(out=x_o[ic], in_=uo[ic % 2][:]).then_inc(sto, 16)
            gp.wait_ge(sto, 16 * NCH)

    return nc



def build_ffn1(debug=False, lora=True, loraonly=False):
    c = Ctx()
    nc = c.nc
    x_in = nc.dram_tensor("xin", [NCH, 128, TPC], BF16, kind="ExternalInput").ap()
    w_d = nc.dram_tensor("w", [32, 128, D], F32, kind="ExternalInput").ap()
    fb_d = nc.dram_tensor("fbias", [128, 32], F32, kind="ExternalInput").ap()
    wr_d = nc.dram_tensor("wr", [128, 256], F32, kind="ExternalInput").ap()
    a_d = nc.dram_tensor("la", [128, 4096], F32, kind="ExternalInput").ap()
    b_d = nc.dram_tensor("lb", [128, 4096], F32, kind="ExternalInput").ap()
    d2_d = nc.dram_tensor("d2", [8, 128], F32, kind="ExternalInput").ap()
    idf_d = nc.dram_tensor("idf", [128, 128], F32, kind="ExternalInput").ap()
    ob_d = nc.dram_tensor("onesb", [128, 1], BF16, kind="ExternalInput").ap()
    or_d = nc.dram_tensor("onesr", [1, 128], F32, kind="ExternalInput").ap()
    eps_d = nc.dram_tensor("epsv", [1, 1], F32, kind="ExternalInput").ap()
    x_o = nc.dram_tensor("xout", [NCH, 128, TPC], F32, kind="ExternalOutput").ap()
    dbg_o = {}
    if debug:
        for n, shp in [("d_e", [128, 64]), ("d_wts", [128, 64]), ("d_wt", [8, TPC]),
                       ("d_wb", [128, TPC]), ("d_g", [128, TPC]), ("d_lo", [128, 512]),
                       ("d_bb", [128, 2048])]:
            dbg_o[n] = nc.dram_tensor(n, shp, F32, kind="ExternalOutput").ap()

    with c.es:
        xs = c.sbuf("xs_s", [128, NCH * TPC], BF16)
        xr = c.sbuf("xr_s", [128, NCH * TPC], BF16)
        wbuf = [c.sbuf("w%d" % i, [128, D], F32) for i in range(2)]
        fb_s = c.sbuf("fb_s", [128, 32], F32)
        wr_s = c.sbuf("wr_s", [128, 256], F32)
        a_bf = c.sbuf("a_bf", [128, 4096], BF16)
        b_bf = c.sbuf("b_bf", [128, 4096], BF16)
        d2_s = c.sbuf("d2_s", [8, 128], F32)
        idf_s = c.sbuf("idf_s", [128, 128], F32)
        ones_b = c.sbuf("ones_b", [128, 1], BF16)
        ones_r = c.sbuf("ones_r", [1, 128], F32)
        eps_s = c.sbuf("eps_s", [1, 1], F32)
        lgt_s = c.sbuf("lgt_s", [128, 8 * 8], F32)
        e_s = c.sbuf("e_s", [128, 8 * 8], F32)
        m1 = c.sbuf("m1", [128, 1], F32)
        mneg = c.sbuf("mneg", [128, 1], F32)
        m2 = c.sbuf("m2", [128, 1], F32)
        msk = c.sbuf("msk", [128, 8], F32)
        rcpr = c.sbuf("rcpr", [128, 1], F32)
        wts = c.sbuf("wts", [128, 8 * 8], F32)
        wt_s = c.sbuf("wt_s", [8, TPC], F32)
        wb_s = c.sbuf("wb_s", [128, TPC], F32)
        g_s = [c.sbuf("g%d" % i, [128, 512], BF16) for i in range(2)]
        sq = [c.sbuf("sq%d" % i, [128, 512], BF16) for i in range(2)]
        u = c.sbuf("u", [128, 512], F32)
        uo = [c.sbuf("uo%d" % i, [128, TPC], F32) for i in range(2)]
        r1 = c.sbuf("r1", [1, 512], F32)
        r2 = c.sbuf("r2", [1, 512], F32)
        r3 = c.sbuf("r3", [1, 512], F32)
        dbg_lo = None
        ps = [c.psum("ps%d" % i, [128, 512], F32) for i in range(4)]
        ps_mu = c.psum("ps_mu", [128, 512], F32)
        ps_m2 = c.psum("ps_m2", [128, 512], F32)
        ps_a = c.psum("ps_a", [128, 512], F32)
        ps_b = c.psum("ps_b", [128, 512], F32)

        ld = c.sem("ld")
        wld = c.sem("wld")
        wfree = c.sem("wfree")
        ldab = c.sem("ldab")
        abct = c.sem("abct")
        rt = c.sem("rt")
        rv = c.sem("rv")
        mmd = c.sem("mmd")
        pev = c.sem("pev")
        sqd = c.sem("sqd")
        sm2 = c.sem("sm2")
        stat = c.sem("stat")
        rowv = c.sem("rowv")
        ssx = c.sem("ssx")
        rowd = c.sem("rowd")
        bcd = c.sem("bcd")
        appd = c.sem("appd")
        sto = c.sem("sto")
        svx = c.sem("svx")

        blk = c.es.enter_context(nc.Block())

        @blk.sync
        def _(sy):
            for src, dst in [(fb_d, fb_s), (wr_d, wr_s), (d2_d, d2_s),
                             (idf_d, idf_s), (ob_d, ones_b), (or_d, ones_r),
                             (eps_d, eps_s)]:
                sy.dma_start(out=dst[:], in_=src).then_inc(ld, 16)
            sy.dma_start(out=wbuf[0][:], in_=a_d).then_inc(ldab, 16)
            sy.dma_start(out=wbuf[1][:], in_=b_d).then_inc(ldab, 16)
            for ic in range(NCH):
                sy.dma_start(out=xs[:, ic * TPC:(ic + 1) * TPC],
                             in_=x_in[ic]).then_inc(ld, 16)
            for ic in range(NCH):
                sy.dma_start(out=xr[:, ic * TPC:(ic + 1) * TPC],
                             in_=x_in[ic]).then_inc(ld, 16)
            sy.wait_ge(abct, 2)
            for oc in range(32):
                if oc >= 2:
                    sy.wait_ge(pev, 2 * oc - 2)
                sy.dma_start(out=wbuf[oc % 2][:], in_=w_d[oc]).then_inc(wld, 16)

        ld_all = 16 * (7 + 2 * NCH)

        @blk.vector
        def _(v):
            v.wait_ge(ldab, 32)
            c.inc(v.tensor_copy(out=a_bf[:], in_=wbuf[0][:]), abct)
            c.inc(v.tensor_copy(out=b_bf[:], in_=wbuf[1][:]), abct)

        @blk.tensor
        def _(t):
            t.wait_ge(ld, ld_all)
            for tt in range(8):
                bank = [ps_a, ps_b][tt % 2]
                if tt >= 2:
                    t.wait_ge(rv, tt - 1)
                mm = None
                for ic in range(NCH):
                    mm = t.matmul(out=bank[:, 0:8],
                                  lhsT=xs[:, ic * TPC + 128 * tt:ic * TPC + 128 * (tt + 1)],
                                  rhs=bfview(wr_s, 256)[:, 8 * ic:8 * (ic + 1)],
                                  start=(ic == 0), stop=(ic == NCH - 1))
                c.inc(mm, rt)
            for tt in range(8):
                t.wait_ge(rv, 8 + tt + 1)
                bank = [ps_a, ps_b][tt % 2]
                if tt >= 2:
                    t.wait_ge(rv, 16 + tt - 1)
                c.inc(t.transpose(out=bank[0:8, 0:128],
                                  in_=wts[:, 8 * tt:8 * (tt + 1)],
                                  identity=idf_s[:]), rt)
            for tb in range(2):
                t.wait_ge(rv, 24)
                bank = [ps_a, ps_b][tb]
                c.inc(t.matmul(out=bank[:], lhsT=d2_s[:],
                               rhs=wt_s[0:8, 512 * tb:512 * (tb + 1)],
                               start=True, stop=True), rt)
            t.wait_ge(abct, 2)
            for tb in range(2):
                t.wait_ge(rv, 24 + tb + 1)
                bank = [ps_a, ps_b][tb]
                mm = None
                for ic in range(NCH):
                    mm = t.matmul(out=bank[:],
                                  lhsT=a_bf[:, 128 * ic:128 * (ic + 1)],
                                  rhs=xs[:, ic * TPC + 512 * tb:ic * TPC + 512 * tb + 512],
                                  start=(ic == 0), stop=(ic == NCH - 1))
                c.inc(mm, rt)
            for oc in range(32):
                t.wait_ge(wld, 16 * (oc + 1))
                wv_ = bfview(wbuf[oc % 2], D)
                for tb in range(2):
                    g = 2 * oc + tb
                    if g >= 4:
                        t.wait_ge(pev, g - 3)
                    if oc == 0:
                        t.wait_ge(rv, 26 + tb + 1)
                    bank = ps[g % 4]
                    if loraonly:
                        c.inc(t.matmul(out=bank[:],
                                       lhsT=b_bf[:, 128 * oc:128 * (oc + 1)],
                                       rhs=g_s[tb][:], start=True, stop=True), mmd)
                    elif lora:
                        for ic in range(NCH):
                            t.matmul(out=bank[:],
                                     lhsT=wv_[:, 128 * ic:128 * (ic + 1)],
                                     rhs=xs[:, ic * TPC + 512 * tb:
                                            ic * TPC + 512 * tb + 512],
                                     start=(ic == 0), stop=False)
                        c.inc(t.matmul(out=bank[:],
                                       lhsT=b_bf[:, 128 * oc:128 * (oc + 1)],
                                       rhs=g_s[tb][:], start=False, stop=True), mmd)
                    else:
                        mm = None
                        for ic in range(NCH):
                            mm = t.matmul(out=bank[:],
                                          lhsT=wv_[:, 128 * ic:128 * (ic + 1)],
                                          rhs=xs[:, ic * TPC + 512 * tb:
                                                 ic * TPC + 512 * tb + 512],
                                          start=(ic == 0), stop=(ic == NCH - 1))
                        c.inc(mm, mmd)
            t.wait_ge(pev, 64)
            for tb in range(2):
                if tb == 1:
                    t.wait_ge(rowd, 2)
                for ic in range(NCH):
                    t.matmul(out=ps_mu[0:1, :], lhsT=ones_b[:],
                             rhs=xr[:, ic * TPC + 512 * tb:ic * TPC + 512 * tb + 512],
                             start=(ic == 0), stop=(ic == NCH - 1))
                for ic in range(NCH):
                    k = NCH * tb + ic
                    t.wait_ge(sqd, k + 1)
                    c.inc(t.matmul(out=ps_m2[0:1, :], lhsT=ones_b[:],
                                   rhs=sq[ic % 2][:],
                                   start=(ic == 0), stop=(ic == NCH - 1)), sm2)
                t.wait_ge(rowd, 2 * tb + 2)
                t.matmul(out=ps[2 * tb][:], lhsT=ones_r[:], rhs=r1[:],
                         start=True, stop=True)
                c.inc(t.matmul(out=ps[2 * tb + 1][:], lhsT=ones_r[:], rhs=r3[:],
                               start=True, stop=True), bcd)

        @blk.vector
        def _(v):
            v.wait_ge(ld, ld_all)
            for tt in range(8):
                v.wait_ge(rt, tt + 1)
                bank = [ps_a, ps_b][tt % 2]
                ecols = slice(8 * tt, 8 * (tt + 1))
                i1 = c.inc(v.tensor_reduce(out=m1[:], in_=bank[:, 0:8],
                                           axis=AX.X, op=ALU.max), svx)
                v.wait_ge(svx, i1)
                i2 = c.inc(v.tensor_scalar(out=mneg[:], in0=m1[:], scalar1=-1.0,
                                           scalar2=None, op0=ALU.mult), svx)
                v.wait_ge(svx, i2)
                i3 = c.inc(v.tensor_scalar(out=lgt_s[:, ecols], in0=bank[:, 0:8],
                                           scalar1=mneg[:, :], scalar2=None,
                                           op0=ALU.add), svx)
                v.wait_ge(svx, i3)
                c.inc(v.tensor_copy(out=lgt_s[:, ecols], in_=lgt_s[:, ecols]), rv)

        @blk.scalar
        def _(sc):
            for tt in range(8):
                sc.wait_ge(rv, tt + 1)
                ecols = slice(8 * tt, 8 * (tt + 1))
                c.inc(sc.activation(out=e_s[:, ecols], in_=lgt_s[:, ecols],
                                    func=AF.Exp), ssx)

        @blk.vector
        def _(v):
            for tt in range(8):
                v.wait_ge(ssx, tt + 1)
                ecols = slice(8 * tt, 8 * (tt + 1))
                i1 = c.inc(v.tensor_reduce(out=m1[:], in_=e_s[:, ecols],
                                           axis=AX.X, op=ALU.max), svx)
                v.wait_ge(svx, i1)
                i2 = c.inc(v.tensor_scalar(out=msk[:], in0=e_s[:, ecols],
                                           scalar1=m1[:, :], scalar2=None,
                                           op0=ALU.is_ge), svx)
                v.wait_ge(svx, i2)
                i3 = c.inc(v.scalar_tensor_tensor(out=wts[:, ecols], in0=msk[:],
                                                  scalar=-1000.0, in1=e_s[:, ecols],
                                                  op0=ALU.mult, op1=ALU.add), svx)
                v.wait_ge(svx, i3)
                i4 = c.inc(v.tensor_reduce(out=m2[:], in_=wts[:, ecols],
                                           axis=AX.X, op=ALU.max), svx)
                v.wait_ge(svx, i4)
                i5 = c.inc(v.tensor_tensor(out=rcpr[:], in0=m1[:], in1=m2[:],
                                           op=ALU.add), svx)
                v.wait_ge(svx, i5)
                i6 = c.inc(v.reciprocal(out=rcpr[:], in_=rcpr[:]), svx)
                i7 = c.inc(v.tensor_scalar(out=msk[:], in0=e_s[:, ecols],
                                           scalar1=m2[:, :], scalar2=None,
                                           op0=ALU.is_ge), svx)
                v.wait_ge(svx, i7)
                i8 = c.inc(v.tensor_tensor(out=msk[:], in0=msk[:],
                                           in1=e_s[:, ecols], op=ALU.mult), svx)
                v.wait_ge(svx, max(i6, i8))
                c.inc(v.tensor_scalar(out=wts[:, ecols], in0=msk[:],
                                      scalar1=rcpr[:, :], scalar2=None,
                                      op0=ALU.mult), rv)
            for tt in range(8):
                v.wait_ge(rt, 8 + tt + 1)
                bank = [ps_a, ps_b][tt % 2]
                c.inc(v.tensor_copy(out=wt_s[0:8, 128 * tt:128 * (tt + 1)],
                                    in_=bank[0:8, 0:128]), rv)
            for tb in range(2):
                v.wait_ge(rt, 16 + tb + 1)
                bank = [ps_a, ps_b][tb]
                c.inc(v.tensor_copy(out=wb_s[:, 512 * tb:512 * (tb + 1)],
                                    in_=bank[:]), rv)
            for tb in range(2):
                v.wait_ge(rt, 18 + tb + 1)
                v.wait_ge(rv, 26)
                bank = [ps_a, ps_b][tb]
                c.inc(v.tensor_tensor(out=g_s[tb][:], in0=bank[:],
                                      in1=wb_s[:, 512 * tb:512 * (tb + 1)],
                                      op=ALU.mult), rv)

        @blk.scalar
        def _(sc):
            sc.wait_ge(pev, 64)
            for tb in range(2):
                for ic in range(NCH):
                    k = NCH * tb + ic
                    if k >= 2:
                        sc.wait_ge(sm2, k - 1)
                    c.inc(sc.activation(out=sq[ic % 2][:],
                                        in_=xr[:, ic * TPC + 512 * tb:
                                               ic * TPC + 512 * tb + 512],
                                        func=AF.Square), sqd)
                sc.wait_ge(rowv, tb + 1)
                c.inc(sc.activation(out=r3[:], in_=r2[:], func=AF.Sqrt,
                                    bias=eps_s[0:1, :], scale=1.0), ssx)

        @blk.vector
        def _(v):
            for oc in range(32):
                for tb in range(2):
                    g = 2 * oc + tb
                    v.wait_ge(mmd, g + 1)
                    bank = ps[g % 4]
                    cols = slice(oc * TPC + 512 * tb, oc * TPC + 512 * tb + 512)
                    if debug and oc == 0 and tb == 0:
                        i0 = c.inc(v.tensor_copy(out=uo[1][:, 0:512], in_=bank[:]), svx)
                        v.wait_ge(svx, i0)
                    i1 = c.inc(v.tensor_scalar(
                        out=u[:], in0=bank[:], scalar1=fb_s[:, oc:oc + 1],
                        scalar2=None, op0=ALU.add), svx)
                    v.wait_ge(svx, i1)
                    c.inc(v.tensor_tensor(out=xr[:, cols], in0=u[:],
                                          in1=xr[:, cols], op=ALU.add), pev)
            for tb in range(2):
                v.wait_ge(sm2, NCH * (tb + 1))
                i1 = c.inc(v.tensor_scalar(out=r1[:], in0=ps_mu[0:1, :],
                                           scalar1=-1.0 / D, scalar2=None,
                                           op0=ALU.mult), svx)
                i2 = c.inc(v.tensor_scalar(out=r2[:], in0=ps_m2[0:1, :],
                                           scalar1=1.0 / D, scalar2=None,
                                           op0=ALU.mult), svx)
                v.wait_ge(svx, i2)
                i3 = c.inc(v.tensor_tensor(out=r3[:], in0=r1[:], in1=r1[:],
                                           op=ALU.mult), svx)
                v.wait_ge(svx, i3)
                i4 = c.inc(v.scalar_tensor_tensor(out=r2[:], in0=r3[:],
                                                  scalar=-1.0, in1=r2[:],
                                                  op0=ALU.mult, op1=ALU.add), svx)
                v.wait_ge(svx, i4)
                c.inc(v.tensor_copy(out=r2[:], in_=r2[:]), rowv)
                v.wait_ge(ssx, 8 + tb + 1)
                i5 = c.inc(v.reciprocal(out=r3[:], in_=r3[:]), svx)
                v.wait_ge(svx, i5)
                c.inc(v.tensor_copy(out=r1[:], in_=r1[:]), rowd)
                c.inc(v.tensor_copy(out=r3[:], in_=r3[:]), rowd)
            v.wait_ge(bcd, 2)
            for ic in range(NCH):
                if ic >= 2:
                    v.wait_ge(sto, 16 * (ic - 1))
                for tb in range(2):
                    cols = slice(ic * TPC + 512 * tb, ic * TPC + 512 * tb + 512)
                    j1 = c.inc(v.tensor_tensor(out=u[:], in0=xr[:, cols],
                                               in1=ps[2 * tb][:], op=ALU.add), svx)
                    v.wait_ge(svx, j1)
                    last = v.tensor_tensor(out=uo[ic % 2][:, 512 * tb:512 * tb + 512],
                                           in0=u[:], in1=ps[2 * tb + 1][:],
                                           op=ALU.mult)
                    if tb == 1:
                        c.inc(last, appd)

        @blk.gpsimd
        def _(gp):
            nst = 0
            if debug:
                gp.wait_ge(pev, 1)
                gp.dma_start(out=dbg_o["d_lo"], in_=uo[1][:, 0:512]).then_inc(sto, 16)
                nst += 1
            for ic in range(NCH):
                gp.wait_ge(appd, ic + 1)
                gp.dma_start(out=x_o[ic], in_=uo[ic % 2][:]).then_inc(sto, 16)
            nst += NCH
            if debug:
                gp.wait_ge(rv, 28)
                gp.dma_start(out=dbg_o["d_e"], in_=e_s[:]).then_inc(sto, 16)
                gp.dma_start(out=dbg_o["d_wts"], in_=wts[:]).then_inc(sto, 16)
                gp.dma_start(out=dbg_o["d_wt"], in_=wt_s[:]).then_inc(sto, 16)
                gp.dma_start(out=dbg_o["d_wb"], in_=wb_s[:]).then_inc(sto, 16)
                gp.dma_start(out=dbg_o["d_g"][:, 0:512], in_=g_s[0][:]).then_inc(sto, 16)
                gp.dma_start(out=dbg_o["d_g"][:, 512:1024], in_=g_s[1][:]).then_inc(sto, 16)
                pass
                gp.dma_start(out=dbg_o["d_bb"], in_=b_bf[:].bitcast(F32)[:, 0:2048]).then_inc(sto, 16)
                nst += 7
            gp.wait_ge(sto, 16 * nst)

    return nc



_PROGS = {}
_DBG = {}


def _prog(key, fn, *a):
    if key not in _PROGS:
        _PROGS[key] = fn(*a)
    return _PROGS[key]


def _stack(arrs):
    return np.ascontiguousarray(arrs.reshape(NCH, 128, TPC))


def _tm_from_chunks(out_chunks):
    return out_chunks.reshape(D, TPC)


def kernel_unfused(L0_attn_x, L0_attn_Wq, L0_attn_Wk, L0_attn_Wv, L0_attn_Wo,
           L0_ffn_quantized_weights, L0_ffn_scales, L0_ffn_bias,
           L1_attn_wq, L1_attn_wk, L1_attn_wv, L1_attn_wo,
           L1_ffn_W_base, L1_ffn_b_base, L1_ffn_W_router,
           L1_ffn_lora_A, L1_ffn_lora_B):
    cores = list(range(NCORES))
    toks = [core_tokens(j) for j in cores]

    ident_bf = np.ascontiguousarray(np.eye(128, dtype=np.float32).astype(bfloat16))
    ident_f = np.ascontiguousarray(np.eye(128, dtype=np.float32))
    ones_bf = np.ones((128, 128), bfloat16)
    ones_col = np.ones((128, 1), bfloat16)
    ones_row = np.ones((1, 128), np.float32)
    epsv = np.full((1, 1), LN_EPS, np.float32)
    dpat = diag_patterns()
    cos0, sin0 = rope_tables_l0(S)
    cos1, sin1 = rope_tables_l1(S)
    rope0 = [rope_fm(cos0, sin0, toks[j]) for j in cores]
    rope1 = [rope_fm(cos1, sin1, toks[j]) for j in cores]
    ebias = {}
    for j in cores:
        _, bias = pack_kv(j, np.zeros((1, 128, S), bfloat16))
        ebias[j] = np.ascontiguousarray(
            np.broadcast_to(bias[None, :], (128, NT)).astype(np.float32))

    wq0 = colchunk(np.ascontiguousarray(L0_attn_Wq.T))
    wk0 = colchunk(np.ascontiguousarray(L0_attn_Wk.T))
    wv0 = colchunk(np.ascontiguousarray(L0_attn_Wv.T))
    wo0 = colchunk(np.ascontiguousarray(L0_attn_Wo.T))
    qw0 = colchunk(np.ascontiguousarray(L0_ffn_quantized_weights.T))
    sc0 = np.ascontiguousarray(L0_ffn_scales.reshape(32, 128).T)
    bi0 = np.ascontiguousarray(L0_ffn_bias.reshape(32, 128).T)
    wq1 = colchunk(np.ascontiguousarray(L1_attn_wq))
    wk1 = colchunk(np.ascontiguousarray(L1_attn_wk))
    wv1 = colchunk(np.ascontiguousarray(L1_attn_wv))
    wo1 = colchunk(np.ascontiguousarray(L1_attn_wo))
    wb1 = colchunk(np.ascontiguousarray(L1_ffn_W_base.T))
    fb1 = np.ascontiguousarray(L1_ffn_b_base.reshape(32, 128).T)
    wr1 = np.ascontiguousarray(
        L1_ffn_W_router.reshape(8, 32, 128).transpose(2, 1, 0).reshape(128, 256))
    la1 = np.ascontiguousarray(
        L1_ffn_lora_A.reshape(128, D).T.reshape(32, 128, 128)
        .transpose(1, 0, 2).reshape(128, 4096))
    lb1 = np.ascontiguousarray(
        L1_ffn_lora_B.transpose(0, 2, 1).reshape(128, D))
    d2 = np.zeros((8, 128), np.float32)
    for e in range(8):
        d2[e, 16 * e:16 * (e + 1)] = 2.0

    x_fm = [np.ascontiguousarray(L0_attn_x[j // 4].T[:, toks[j]]) for j in cores]

    def launch(nc, maps):
        res = run_bass_kernel_spmd(nc, maps, core_ids=cores)
        return res.results

    def gather_kv(q_r, k_r, v_r, nkv):
        kfull = [np.zeros((nkv, 128, S), bfloat16) for _ in range(B)]
        vfull = [np.zeros((nkv, 128, S), bfloat16) for _ in range(B)]
        for j in cores:
            b = j // 4
            karr = k_r[j]["k"]
            varr = v_r[j]["v"]
            va = varr.reshape(nkv, 128, 8, 128)
            c0, c1 = core_chunks(j)
            for li, ch in ((0, c0), (1, c1)):
                gsl = slice(512 * ch, 512 * (ch + 1))
                lsl = slice(512 * li, 512 * (li + 1))
                kfull[b][:, :, gsl] = karr[:, :, lsl]
            for tt in range(8):
                ch = c0 if tt < 4 else c1
                ktg = 4 * ch + (tt % 4)
                vfull[b][:, :, 128 * ktg:128 * (ktg + 1)] = va[:, :, tt, :]
        kp, vp = {}, {}
        for j in cores:
            b = j // 4
            kp[j], _ = pack_kv(j, kfull[b])
            vp[j], _ = pack_kv(j, vfull[b])
        return kp, vp

    _DBG.clear()
    nc_a = _prog("qkv0", build_qkv, 0)
    maps = [{"x": _stack(x_fm[j]), "wq": wq0, "wk": wk0, "wv": wv0,
             "cos": rope0[j][0], "sinS": rope0[j][1], "ident": ident_bf}
            for j in cores]
    r_a = launch(nc_a, maps)
    xs_cur = [r_a[j]["xs"] for j in cores]
    kp, vp = gather_kv(None, r_a, r_a, 32)

    nc_b = _prog("attn0", build_attn, 0)
    maps = [{"q": r_a[j]["q"], "kp": kp[j], "vp": vp[j], "ebias": ebias[j],
             "dpat": dpat, "onesb": ones_bf} for j in cores]
    r_b = launch(nc_b, maps)

    nc_c = _prog("oln", build_proj_ln, "plain", False)
    maps = [{"xin": r_b[j]["attn"], "xres": xs_cur[j], "w": wo0,
             "onesb": ones_col, "onesr": ones_row, "epsv": epsv}
            for j in cores]
    r_c = launch(nc_c, maps)
    xs_cur = [r_c[j]["xout"] for j in cores]
    _DBG["x1"] = list(xs_cur)

    nc_d = _prog("qffn", build_proj_ln, "qffn", False)
    maps = [{"xin": xs_cur[j], "xres": xs_cur[j], "w": qw0, "scales": sc0,
             "fbias": bi0, "onesb": ones_col, "onesr": ones_row, "epsv": epsv}
            for j in cores]
    r_d = launch(nc_d, maps)
    xs_cur = [r_d[j]["xout"] for j in cores]
    _DBG["x2"] = list(xs_cur)

    nc_e = _prog("qkv1", build_qkv, 1)
    maps = [{"x": xs_cur[j], "wq": wq1, "wk": wk1, "wv": wv1,
             "cos": rope1[j][0], "sinS": rope1[j][1], "ident": ident_bf}
            for j in cores]
    r_e = launch(nc_e, maps)
    kp, vp = gather_kv(None, r_e, r_e, 8)

    nc_f = _prog("attn1", build_attn, 1)
    maps = [{"q": r_e[j]["q"], "kp": kp[j], "vp": vp[j], "ebias": ebias[j],
             "dpat": dpat, "onesb": ones_bf} for j in cores]
    r_f = launch(nc_f, maps)

    maps = [{"xin": r_f[j]["attn"], "xres": xs_cur[j], "w": wo1,
             "onesb": ones_col, "onesr": ones_row, "epsv": epsv}
            for j in cores]
    r_g = launch(nc_c, maps)
    xs_cur = [r_g[j]["xout"] for j in cores]
    _DBG["x3"] = list(xs_cur)

    nc_h = _prog("ffn1", build_ffn1)
    maps = [{"xin": xs_cur[j], "w": wb1, "fbias": fb1, "wr": wr1, "la": la1,
             "lb": lb1, "d2": d2, "idf": ident_f, "onesb": ones_col,
             "onesr": ones_row, "epsv": epsv} for j in cores]
    r_h = launch(nc_h, maps)

    out = np.empty((B, S, D), np.float32)
    for j in cores:
        b = j // 4
        xo = _tm_from_chunks(r_h[j]["xout"])
        out[b, toks[j], :] = xo.T
    return out



NKTM = (16, 28)
CHUNK_RANK = [min(cg, 7 - cg) for cg in range(8)]
CHUNK_HALF = [0 if cg == CHUNK_RANK[cg] else 1 for cg in range(8)]


def fused_ebias(core_j):
    c0, c1 = core_chunks(core_j)
    b = np.full(52, PAD_BIAS, np.float32)
    b[0:4 * c0] = EXP_SHIFT
    b[16:20] = EXP_SHIFT
    b[20:20 + 4 * (c1 - 4) + 16] = EXP_SHIFT
    b[48:52] = EXP_SHIFT
    b[20:48] = np.where(np.arange(28) < 4 * c1, EXP_SHIFT, PAD_BIAS)
    return np.ascontiguousarray(np.broadcast_to(b[None, :], (128, 52)).astype(np.float32))


def build_fused():
    c = Ctx()
    nc = c.nc
    ap = lambda t: t.ap()

    x_in = nc.dram_tensor("x", [NCH, 128, TPC], F32, kind="ExternalInput").ap()
    w_l0 = {n: nc.dram_tensor(n, [noc, 128, D], F32, kind="ExternalInput").ap()
            for n, noc in [("wq0", 32), ("wk0", 32), ("wv0", 32), ("wo0", 32)]}
    qw_d = nc.dram_tensor("qw", [32, 128, D], I32, kind="ExternalInput").ap()
    sc_d = nc.dram_tensor("scales", [128, 32], F32, kind="ExternalInput").ap()
    bi_d = nc.dram_tensor("qbias", [128, 32], F32, kind="ExternalInput").ap()
    w_l1 = {n: nc.dram_tensor(n, [noc, 128, D], F32, kind="ExternalInput").ap()
            for n, noc in [("wq1", 32), ("wk1", 8), ("wv1", 8), ("wo1", 32)]}
    wb_d = nc.dram_tensor("wbase", [32, 128, D], F32, kind="ExternalInput").ap()
    fb_d = nc.dram_tensor("fbias", [128, 32], F32, kind="ExternalInput").ap()
    wr_d = nc.dram_tensor("wr", [128, 256], F32, kind="ExternalInput").ap()
    a_d = nc.dram_tensor("la", [128, 4096], F32, kind="ExternalInput").ap()
    b_d = nc.dram_tensor("lb", [128, 4096], F32, kind="ExternalInput").ap()
    d2_d = nc.dram_tensor("d2", [8, 128], F32, kind="ExternalInput").ap()
    rope_d = {n: nc.dram_tensor(n, [128, TPC], F32, kind="ExternalInput").ap()
              for n in ["cos0", "sin0", "cos1", "sin1"]}
    id_d = nc.dram_tensor("ident", [128, 128], BF16, kind="ExternalInput").ap()
    idf_d = nc.dram_tensor("idf", [128, 128], F32, kind="ExternalInput").ap()
    ob_d = nc.dram_tensor("onesb", [128, 1], BF16, kind="ExternalInput").ap()
    obm_d = nc.dram_tensor("onesbm", [128, 128], BF16, kind="ExternalInput").ap()
    or_d = nc.dram_tensor("onesr", [1, 128], F32, kind="ExternalInput").ap()
    eps_d = nc.dram_tensor("epsv", [1, 1], F32, kind="ExternalInput").ap()
    dpat_d = nc.dram_tensor("dpat", [4, 128, 512], BF16, kind="ExternalInput").ap()
    eb_d = nc.dram_tensor("ebias", [128, 52], F32, kind="ExternalInput").ap()
    x_o = nc.dram_tensor("xout", [NCH, 128, TPC], F32, kind="ExternalOutput").ap()

    q_h = {L: nc.dram_tensor("qh%d" % L, [32, 128, TPC], BF16).ap() for L in (0, 1)}
    k_h = {L: nc.dram_tensor("kh%d" % L, [(32 if L == 0 else 8) * 128, TPC], BF16).ap()
           for L in (0, 1)}
    v_h = {L: nc.dram_tensor("vh%d" % L, [(32 if L == 0 else 8) * 128, TPC], BF16).ap()
           for L in (0, 1)}
    kg_h = {L: nc.dram_tensor("kg%d" % L, [4 * (32 if L == 0 else 8) * 128, TPC], BF16)
            for L in (0, 1)}
    vg_h = {L: nc.dram_tensor("vg%d" % L, [4 * (32 if L == 0 else 8) * 128, TPC], BF16)
            for L in (0, 1)}
    at_h = {L: nc.dram_tensor("at%d" % L, [32, 128, TPC], BF16).ap() for L in (0, 1)}
    xs0_h = nc.dram_tensor("xs0", [NCH, 128, TPC], BF16).ap()
    xh = {i: nc.dram_tensor("xh%d" % i, [NCH, 128, TPC], BF16).ap() for i in (1, 2, 3)}

    with c.es:
        S_XS = c.sbuf("S_XS", [128, 32768], BF16)
        S_XR = c.sbuf("S_XR", [128, 32768], BF16)
        S_W = c.sbuf("S_W", [128, 8192], F32)
        AB = c.sbuf("AB", [128, 8192], BF16)
        cos_s = c.sbuf("cos_s", [128, TPC], F32)
        sin_s = c.sbuf("sin_s", [128, TPC], F32)
        id_s = c.sbuf("id_s", [128, 128], BF16)
        idf_s = c.sbuf("idf_s", [128, 128], F32)
        ones_b = c.sbuf("ones_b", [128, 1], BF16)
        ones_bm = c.sbuf("ones_bm", [128, 128], BF16)
        ones_r = c.sbuf("ones_r", [1, 128], F32)
        eps_s = c.sbuf("eps_s", [1, 1], F32)
        eb_s = c.sbuf("eb_s", [128, 52], F32)
        vsc_s = c.sbuf("vsc_s", [128, 32], F32)
        vbi_s = c.sbuf("vbi_s", [128, 32], F32)
        fb_s = c.sbuf("fb_s", [128, 32], F32)
        wr_s = c.sbuf("wr_s", [128, 256], F32)
        u_s = c.sbuf("u_s", [128, 512], F32)
        r1 = c.sbuf("r1", [1, 512], F32)
        r2 = c.sbuf("r2", [1, 512], F32)
        r3 = c.sbuf("r3", [1, 512], F32)
        lgt_s = c.sbuf("lgt_s", [128, 64], F32)
        e_s = c.sbuf("e_s", [128, 64], F32)
        wts_s = c.sbuf("wts_s", [128, 64], F32)
        m1_s = c.sbuf("m1_s", [128, 1], F32)
        mneg_s = c.sbuf("mneg_s", [128, 1], F32)
        m2_s = c.sbuf("m2_s", [128, 1], F32)
        msk_s = c.sbuf("msk_s", [128, 8], F32)
        rcpr_s = c.sbuf("rcpr_s", [128, 1], F32)
        wt_s = c.sbuf("wt_s", [8, TPC], F32)
        wb_s = c.sbuf("wb_s", [128, TPC], F32)
        g_s = [c.sbuf("g%d" % i, [128, 512], BF16) for i in range(2)]
        ps = [c.psum("ps%d" % i, [128, 512], F32) for i in range(4)]
        ps_mu = c.psum("ps_mu", [128, 512], F32)
        ps_m2 = c.psum("ps_m2", [128, 512], F32)
        ps_a = c.psum("ps_a", [128, 512], F32)
        ps_b = c.psum("ps_b", [128, 512], F32)
        ps_trA = ps_a[:].bitcast(BF16)
        ps_trB = ps_b[:].bitcast(BF16)
        d2_s2 = c.sbuf("d2_s2", [8, 128], F32)

        pb = c.sem("pb")
        gsvx = c.sem("gsvx")
        blk = c.es.enter_context(nc.Block())
        phase_n = [0]

        def barrier_wait(eng):
            eng.wait_ge(pb, 5 * phase_n[0])

        def barrier_done(incs):
            phase_n[0] += 1

        XSf = S_XS[:].bitcast(F32)
        XRf = S_XR[:].bitcast(F32)
        ABf = AB[:].bitcast(F32)
        uo = [XSf[:, 1024 * i:1024 * (i + 1)] for i in range(2)]
        sq = [S_XS[:, 4096 + 512 * i:4096 + 512 * (i + 1)] for i in range(2)]
        uo_w = [S_W[:, 2048 + 1024 * i:2048 + 1024 * (i + 1)] for i in range(2)]
        sq_w = [S_W[:].bitcast(BF16)[:, 512 * i:512 * (i + 1)] for i in range(2)]
        dpat_s = S_XS[:, 24576:26624]
        rcp_s = XSf[:, 13312:13824]


        def emit_const_loads():
            p = "cl"
            ld = c.sem(p + "ld")
            ms = c.sem(p + "ms")

            @blk.gpsimd
            def _(gp):
                gp.memset(S_W[:], 0.0)
                gp.memset(S_XS[:], 0.0)
                c.inc(gp.tensor_copy(out=S_XR[:, 0:1], in_=S_XS[:, 0:1]), ms)
                gp.memset(S_XR[:], 0.0)
                c.inc(gp.tensor_copy(out=S_W[:, 0:1], in_=S_W[:, 0:1]), ms)

            @blk.sync
            def _(sy):
                sy.wait_ge(ms, 2)
                for src, dst in [(id_d, id_s), (idf_d, idf_s), (ob_d, ones_b),
                                 (obm_d, ones_bm), (or_d, ones_r), (eps_d, eps_s),
                                 (eb_d, eb_s), (sc_d, vsc_s), (bi_d, vbi_s),
                                 (fb_d, fb_s), (wr_d, wr_s), (d2_d, d2_s2)]:
                    sy.dma_start(out=dst[:], in_=src).then_inc(ld, 16)
                sy.wait_ge(ld, 16 * 12)
                sy.sem_inc(pb, 5)
            phase_n[0] += 1

        def emit_qkv(L):
            p = "qk%d" % L
            nkv = 32 if L == 0 else 8
            ldx = c.sem(p + "ldx")
            ldxs = [c.sem(p + "ldxa"), c.sem(p + "ldxb")]
            wld = c.sem(p + "wld")
            mmd = c.sem(p + "mmd")
            pev = c.sem(p + "pev")
            xca = c.sem(p + "xca")
            trd = c.sem(p + "trd")
            tre = c.sem(p + "tre")
            sto = c.sem(p + "sto")
            svx = gsvx
            ccs = c.sem(p + "ccs")
            projs = [("v", w_l0["wv0"] if L == 0 else w_l1["wv1"], nkv),
                     ("k", w_l0["wk0"] if L == 0 else w_l1["wk1"], nkv),
                     ("q", w_l0["wq0"] if L == 0 else w_l1["wq1"], 32)]
            xs = S_XR
            xl = S_W[:, 4096:6144]
            qk_out = [S_XS[:, 1024 * i:1024 * (i + 1)] for i in range(2)]
            vfm = [S_XS[:, 2048 + 1024 * i:2048 + 1024 * (i + 1)] for i in range(2)]
            vtm = [S_XS[:, 4096 + 1024 * i:4096 + 1024 * (i + 1)] for i in range(2)]
            tmp = XSf[:, 3072:3584]
            u = XSf[:, 3584:4096]
            wbuf = [S_W[:, 0:4096], S_W[:, 4096:8192]]
            ps_tr = [ps_trA, ps_trB]
            n_xs_st = 0
            store_base = {}
            _si = n_xs_st
            for _pn, _wd, _noc in projs:
                store_base[_pn] = _si
                _si += _noc
            x_src = x_in if L == 0 else xh[2]
            cosd = rope_d["cos0"] if L == 0 else rope_d["cos1"]
            sind = rope_d["sin0"] if L == 0 else rope_d["sin1"]

            @blk.sync
            def _(sy):
                barrier_wait(sy)
                sy.dma_start(out=cos_s[:], in_=cosd).then_inc(ldx, 16)
                sy.dma_start(out=sin_s[:], in_=sind).then_inc(ldx, 16)
                if L == 0:
                    for ic in range(NCH):
                        if ic >= 2:
                            sy.wait_ge(xca, ic - 1)
                        sy.dma_start(out=xl[:, (ic % 2) * TPC:(ic % 2 + 1) * TPC],
                                     in_=x_src[ic]).then_inc(ldxs[ic % 2], 16)
                si = 0
                for _pn, wd, noc in projs:
                    for oc in range(noc):
                        if si == 1 and L == 0:
                            sy.wait_ge(xca, NCH)
                        if si >= 2:
                            sy.wait_ge(pev, 2 * si - 2)
                        sy.dma_start(out=wbuf[si % 2][:],
                                     in_=wd[oc]).then_inc(wld, 16)
                        si += 1
                sy.sem_inc(pb, 1)

            if L == 0:
                @blk.vector
                def _(v):
                    barrier_wait(v)
                    for ic in range(NCH):
                        v.wait_ge(ldxs[ic % 2], 16 * (ic // 2 + 1))
                        c.inc(v.tensor_copy(out=xs[:, ic * TPC:(ic + 1) * TPC],
                                            in_=xl[:, (ic % 2) * TPC:(ic % 2 + 1) * TPC]),
                              xca)

            @blk.tensor
            def _(t):
                barrier_wait(t)
                if L == 0:
                    t.wait_ge(xca, NCH)
                t.wait_ge(ldx, 32)
                si = 0
                g = 0

                def v_transpose(oc):
                    t.wait_ge(pev, 2 * oc + 2)
                    if oc >= 2:
                        t.wait_ge(tre, oc - 1)
                    tr = None
                    for tt in range(8):
                        tr = t.transpose(out=ps_tr[oc % 2][:, 128 * tt:128 * (tt + 1)],
                                         in_=vfm[oc % 2][:, 128 * tt:128 * (tt + 1)],
                                         identity=id_s[:])
                    c.inc(tr, trd)

                for _pn, wd, noc in projs:
                    for oc in range(noc):
                        t.wait_ge(wld, 16 * (si + 1))
                        wv_ = wbuf[si % 2][:].bitcast(BF16)[:, 1:2 * D:2]
                        for tb in range(2):
                            if g >= 4:
                                t.wait_ge(pev, g - 3)
                            bank = ps[g % 4]
                            mm = None
                            for ic in range(NCH):
                                mm = t.matmul(out=bank[:],
                                              lhsT=wv_[:, 128 * ic:128 * (ic + 1)],
                                              rhs=xs[:, ic * TPC + 512 * tb:
                                                     ic * TPC + 512 * tb + 512],
                                              start=(ic == 0), stop=(ic == NCH - 1))
                            c.inc(mm, mmd)
                            g += 1
                        si += 1
                        if _pn == "v" and oc >= 2:
                            v_transpose(oc - 2)
                    if _pn == "v":
                        v_transpose(nkv - 2)
                        v_transpose(nkv - 1)
                c.inc(t.nop() if False else mm, pb) if False else None
                t.sem_inc(pb, 1)

            @blk.vector
            def _(v):
                v.wait_ge(ldx, 32)
                g = 0

                def vtm_evac(oc):
                    v.wait_ge(trd, oc + 1)
                    if oc >= 2:
                        v.wait_ge(sto, 16 * (store_base["v"] + oc - 1))
                    c.inc(v.tensor_copy(out=vtm[oc % 2][:],
                                        in_=ps_tr[oc % 2][:, 0:1024]), tre)

                for oc in range(nkv):
                    for tb in range(2):
                        v.wait_ge(mmd, g + 1)
                        bank = ps[g % 4]
                        cols = slice(512 * tb, 512 * tb + 512)
                        if oc >= 2 and tb == 0:
                            v.wait_ge(trd, oc - 1)
                        c.inc(v.tensor_copy(out=vfm[oc % 2][:, cols], in_=bank[:]), pev)
                        g += 1
                    if oc >= 2:
                        vtm_evac(oc - 2)
                vtm_evac(nkv - 2)
                vtm_evac(nkv - 1)
                for pn, noc in (("k", nkv), ("q", 32)):
                    for oc in range(noc):
                        for tb in range(2):
                            v.wait_ge(mmd, g + 1)
                            bank = ps[g % 4]
                            cols = slice(512 * tb, 512 * tb + 512)
                            if tb == 0 and oc >= 2:
                                v.wait_ge(sto, 16 * (store_base[pn] + oc - 1))
                            i1 = c.inc(v.tensor_tensor(
                                out=tmp[0:64, :], in0=bank[64:128, :],
                                in1=sin_s[0:64, cols], op=ALU.mult), svx)
                            i2 = c.inc(v.tensor_tensor(
                                out=tmp[64:128, :], in0=bank[0:64, :],
                                in1=sin_s[64:128, cols], op=ALU.mult), svx)
                            i3 = c.inc(v.tensor_tensor(
                                out=u[:], in0=bank[:], in1=cos_s[:, cols],
                                op=ALU.mult), svx)
                            v.wait_ge(svx, i3)
                            last = v.tensor_tensor(out=qk_out[oc % 2][:, cols],
                                                   in0=u[:], in1=tmp[:], op=ALU.add)
                            c.inc(last, pev)
                            g += 1
                last.then_inc(pb, 1) if False else None
                v.sem_inc(pb, 1)

            @blk.scalar
            def _(s):
                barrier_wait(s)
                s.sem_inc(pb, 1)

            @blk.gpsimd
            def _(gp):
                barrier_wait(gp)
                nst = 0
                base_g = 0
                for pn, _wd, noc in projs:
                    if pn == "v":
                        for oc in range(nkv):
                            gp.wait_ge(tre, oc + 1)
                            gp.dma_start(out=v_h[L][128 * oc:128 * (oc + 1), :],
                                         in_=vtm[oc % 2][:]).then_inc(sto, 16)
                            nst += 1
                        gp.wait_ge(sto, 16 * nst)
                        gp.collective_compute(
                            "AllGather", ALU.bypass,
                            replica_groups=[[0, 1, 2, 3], [4, 5, 6, 7]],
                            ins=[v_h[L].opt()],
                            outs=[vg_h[L].ap().opt()]).then_inc(ccs)
                    else:
                        for oc in range(noc):
                            gp.wait_ge(pev, base_g + 2 * oc + 2)
                            out_d = (k_h[L][128 * oc:128 * (oc + 1), :] if pn == "k"
                                     else q_h[L][oc])
                            gp.dma_start(out=out_d, in_=qk_out[oc % 2][:]).then_inc(sto, 16)
                            nst += 1
                        if pn == "k":
                            gp.wait_ge(sto, 16 * nst)
                            gp.collective_compute(
                                "AllGather", ALU.bypass,
                                replica_groups=[[0, 1, 2, 3], [4, 5, 6, 7]],
                                ins=[k_h[L].opt()],
                                outs=[kg_h[L].ap().opt()]).then_inc(ccs)
                    base_g += 2 * noc
                gp.wait_ge(sto, 16 * nst)
                gp.wait_ge(ccs, 2)
                gp.sem_inc(pb, 1)
            phase_n[0] += 1

        def emit_attn(L):
            p = "at%d" % L
            nkv = 32 if L == 0 else 8
            rep = 32 // nkv
            qld = c.sem(p + "qld")
            kvld = c.sem(p + "kvld")
            smm = c.sem(p + "smm")
            sexp = c.sem(p + "sexp")
            sdg = c.sem(p + "sdg")
            spv = c.sem(p + "spv")
            sev = c.sem(p + "sev")
            sdone = c.sem(p + "sdone")
            svx = gsvx
            q_s = [S_XS[:, 1024 * i:1024 * (i + 1)] for i in range(2)]
            k_s = [S_XS[:, 2048 + 4608 * i:2048 + 4608 * (i + 1)] for i in range(2)]
            v_s = [S_XS[:, 11264 + 4608 * i:11264 + 4608 * (i + 1)] for i in range(2)]
            pt = [S_XS[:, 20480 + 512 * i:20480 + 512 * (i + 1)] for i in range(4)] + \
                 [S_XS[:, 27648 + 512 * i:27648 + 512 * (i + 1)] for i in range(2)]
            ao_s = [S_XS[:, 22528 + 1024 * i:22528 + 1024 * (i + 1)] for i in range(2)]
            ps_s = [ps[0], ps[1], ps_a, ps_b]
            ps_o = [ps[2], ps[3]]
            ps_z = [ps_mu, ps_m2]
            kgr = kg_h[L].ap()
            vgr = vg_h[L].ap()

            def slot_col(qb, kt):
                if kt < NKTM[qb]:
                    return 128 * kt
                return (3584 + 128 * (kt - NKTM[qb])) if qb == 0 else \
                       (4096 + 128 * (kt - NKTM[qb]))

            def bias_slot(qb, kt):
                return kt if qb == 0 else 20 + kt

            NTOT = (NKTM[0] + 4) + (NKTM[1] + 4)

            dpl = c.sem(p + "dpl")

            @blk.sync
            def _(sy):
                barrier_wait(sy)
                for a_ in range(4):
                    sy.dma_start(out=dpat_s[:, 512 * a_:512 * (a_ + 1)],
                                 in_=dpat_d[a_]).then_inc(dpl, 16)
                for h in range(32):
                    if h >= 2:
                        sy.wait_ge(sev, 2 * (h - 1))
                    sy.dma_start(out=q_s[h % 2][:], in_=q_h[L][h]).then_inc(qld, 16)
                    if h % rep == 0:
                        kv = h // rep
                        if kv >= 2:
                            sy.wait_ge(sev, 2 * rep * (kv - 1))
                        for cg in range(7):
                            rk, hf = CHUNK_RANK[cg], CHUNK_HALF[cg]
                            row = (rk * nkv + kv) * 128
                            sy.dma_start(
                                out=k_s[kv % 2][:, 512 * cg:512 * (cg + 1)],
                                in_=kgr[row:row + 128, 512 * hf:512 * (hf + 1)]
                            ).then_inc(kvld, 16)
                            sy.dma_start(
                                out=v_s[kv % 2][:, 512 * cg:512 * (cg + 1)],
                                in_=vgr[row:row + 128, 512 * hf:512 * (hf + 1)]
                            ).then_inc(kvld, 16)
                        for qb in range(2):
                            lc = slice(512 * qb, 512 * (qb + 1))
                            dc = slice(3584, 4096) if qb == 0 else slice(4096, 4608)
                            sy.dma_start(out=k_s[kv % 2][:, dc],
                                         in_=k_h[L][128 * kv:128 * (kv + 1), lc]
                                         ).then_inc(kvld, 16)
                            sy.dma_start(out=v_s[kv % 2][:, dc],
                                         in_=v_h[L][128 * kv:128 * (kv + 1), lc]
                                         ).then_inc(kvld, 16)
                sy.sem_inc(pb, 1)

            n_kv_dma = 18

            def slot_amt(qb, kt, h, nreal0, nreal1, npads0, npads1):
                a = 1
                nr = nreal0 if qb == 0 else nreal1
                np_ = npads0 if qb == 0 else npads1
                if nr > 0 and kt == nr - 1:
                    a += np_
                if qb == 1 and kt == NKTM[1] + 3 and nreal0 == 0 and h < 31:
                    a += npads0
                return a

            @blk.tensor
            def _(t):
                barrier_wait(t)
                idx_t = t.partition_id() % 4
                snap = dict(c.cnt)
                finals = []
                for c0v in t.Switch(idx_t, 4):
                    c.cnt.clear()
                    c.cnt.update(snap)
                    c1v = 7 - c0v
                    nreal0, nreal1 = 4 * c0v, 4 * c1v
                    npads0, npads1 = 16 - nreal0, 28 - nreal1
                    if nreal0 == 0:
                        t.sem_inc(smm, npads0)
                        c.cnt[id(smm)] += npads0
                        t.sem_inc(spv, npads0)
                        c.cnt[id(spv)] += npads0
                    for h in range(32):
                        t.wait_ge(qld, 16 * (h + 1))
                        t.wait_ge(kvld, 16 * n_kv_dma * (h // rep + 1))
                        kbuf = k_s[(h // rep) % 2]
                        vbuf = v_s[(h // rep) % 2]
                        for qb in range(2):
                            nkt = NKTM[qb] + 4
                            base = 0 if qb == 0 else NKTM[0] + 4
                            g2 = 2 * h + qb
                            gkt0 = NTOT * h + base
                            qcols = slice(512 * qb, 512 * qb + 512)
                            nreal = nreal0 if qb == 0 else nreal1
                            first_real = 0 if nreal > 0 else nkt - 4
                            if g2 >= 2:
                                t.wait_ge(sev, g2 - 1)

                            def is_pad(kt):
                                return kt < nkt - 4 and kt >= nreal

                            def amt(kt):
                                return slot_amt(qb, kt, h, nreal0, nreal1,
                                                npads0, npads1)

                            def scores(kt):
                                if is_pad(kt):
                                    return
                                gkt = gkt0 + kt
                                if gkt >= 4:
                                    t.wait_ge(sexp, gkt - 3)
                                cc = slot_col(qb, kt)
                                c.inc(t.matmul(out=ps_s[gkt % 4][:],
                                               lhsT=kbuf[:, cc:cc + 128],
                                               rhs=q_s[h % 2][:, qcols],
                                               start=True, stop=True),
                                      smm, amt(kt))

                            def pv(kt):
                                if is_pad(kt):
                                    return
                                gkt = gkt0 + kt
                                if kt >= nkt - 4:
                                    t.wait_ge(sdg, 4 * g2 + (kt - (nkt - 4)) + 1)
                                else:
                                    t.wait_ge(sexp, gkt + 1)
                                ptile = pt[gkt % 6]
                                cc = slot_col(qb, kt)
                                t.matmul(out=ps_o[g2 % 2][:], lhsT=vbuf[:, cc:cc + 128],
                                         rhs=ptile[:], start=(kt == first_real),
                                         stop=(kt == nkt - 1))

                            def psum_mm(kt):
                                if is_pad(kt):
                                    return
                                gkt = gkt0 + kt
                                ptile = pt[gkt % 6]
                                c.inc(t.matmul(out=ps_z[g2 % 2][:], lhsT=ones_bm[:],
                                               rhs=ptile[:], start=(kt == first_real),
                                               stop=(kt == nkt - 1)),
                                      spv, amt(kt))

                            for kt_ in range(4):
                                scores(kt_)
                            for kt2 in range(0, nkt, 2):
                                if kt2 + 4 < nkt:
                                    scores(kt2 + 4)
                                if kt2 + 5 < nkt:
                                    scores(kt2 + 5)
                                pv(kt2)
                                pv(kt2 + 1)
                                psum_mm(kt2)
                                psum_mm(kt2 + 1)
                    finals.append(dict(c.cnt))
                assert all(f == finals[0] for f in finals[1:])
                t.sem_inc(pb, 1)

            @blk.scalar
            def _(sc):
                barrier_wait(sc)
                idx_s = sc.partition_id() % 4
                snap = dict(c.cnt)
                finals = []
                for c0v in sc.Switch(idx_s, 4):
                    c.cnt.clear()
                    c.cnt.update(snap)
                    c1v = 7 - c0v
                    nreal0, nreal1 = 4 * c0v, 4 * c1v
                    npads0, npads1 = 16 - nreal0, 28 - nreal1
                    if nreal0 == 0:
                        sc.sem_inc(sexp, npads0)
                        c.cnt[id(sexp)] += npads0
                    for h in range(32):
                        for qb in range(2):
                            nkt = NKTM[qb] + 4
                            base = 0 if qb == 0 else NKTM[0] + 4
                            gkt0 = NTOT * h + base
                            nreal = nreal0 if qb == 0 else nreal1
                            for kt in range(nkt):
                                if kt < nkt - 4 and kt >= nreal:
                                    continue
                                gkt = gkt0 + kt
                                sc.wait_ge(smm, gkt + 1)
                                if gkt >= 6:
                                    sc.wait_ge(spv, gkt - 5)
                                bs = bias_slot(qb, kt)
                                c.inc(sc.activation(out=pt[gkt % 6][:],
                                                    in_=ps_s[gkt % 4][:], func=AF.Exp,
                                                    bias=eb_s[:, bs:bs + 1],
                                                    scale=float(HD) ** -0.5),
                                      sexp, slot_amt(qb, kt, h, nreal0, nreal1,
                                                     npads0, npads1))
                    finals.append(dict(c.cnt))
                assert all(f == finals[0] for f in finals[1:])
                sc.sem_inc(pb, 1)

            @blk.vector
            def _(v):
                barrier_wait(v)
                v.wait_ge(dpl, 64)
                for h in range(32):
                    if h >= 2:
                        v.wait_ge(sdone, 16 * (h - 1))
                    for qb in range(2):
                        nkt = NKTM[qb] + 4
                        base = 0 if qb == 0 else NKTM[0] + 4
                        g2 = 2 * h + qb
                        gkt0 = NTOT * h + base
                        for kt in range(nkt - 4, nkt):
                            vv = kt - (nkt - 4)
                            v.wait_ge(sexp, gkt0 + kt + 1)
                            ptile = pt[(gkt0 + kt) % 6]
                            c.inc(v.tensor_tensor(out=ptile[:], in0=ptile[:],
                                                  in1=dpat_s[:, 512 * vv:512 * (vv + 1)],
                                                  op=ALU.mult), sdg)
                        v.wait_ge(spv, gkt0 + nkt)
                        i1 = c.inc(v.reciprocal(out=rcp_s[:], in_=ps_z[g2 % 2][:]), svx)
                        v.wait_ge(svx, i1)
                        c.inc(v.tensor_tensor(
                            out=ao_s[h % 2][:, 512 * qb:512 * qb + 512],
                            in0=ps_o[g2 % 2][:], in1=rcp_s[:], op=ALU.mult), sev)
                v.sem_inc(pb, 1)

            @blk.gpsimd
            def _(gp):
                barrier_wait(gp)
                for h in range(32):
                    gp.wait_ge(sev, 2 * h + 2)
                    gp.dma_start(out=at_h[L][h], in_=ao_s[h % 2][:]).then_inc(sdone, 16)
                gp.wait_ge(sdone, 16 * 32)
                gp.sem_inc(pb, 1)
            phase_n[0] += 1

        def emit_projln(name, mode, x_src, res_src, w_src, out_dst, final=False):
            p = name
            ld = c.sem(p + "ld")
            wld = c.sem(p + "wld")
            wcst = c.sem(p + "wcst")
            mmd = c.sem(p + "mmd")
            pev = c.sem(p + "pev")
            sqd = c.sem(p + "sqd")
            sm2 = c.sem(p + "sm2")
            rowv = c.sem(p + "rowv")
            ssx = c.sem(p + "ssx")
            rowd = c.sem(p + "rowd")
            bcd = c.sem(p + "bcd")
            appd = c.sem(p + "appd")
            sto = c.sem(p + "sto")
            svx = gsvx
            plain = (mode != "qffn")
            mub = [S_W[:].bitcast(BF16)[:, 1024 + 512 * i:1024 + 512 * (i + 1)]
                   for i in range(2)]
            rsb = [S_W[:].bitcast(BF16)[:, 2048 + 512 * i:2048 + 512 * (i + 1)]
                   for i in range(2)]
            xs = S_XS if plain else S_XR
            xr = S_XR if plain else S_XS
            strm = S_XR
            sqv = sq if plain else sq_w
            uov = uo if plain else uo_w
            wbuf = [S_W[:, 0:4096], S_W[:, 4096:8192]]
            wb16 = [AB[:, 4096 * i:4096 * (i + 1)] for i in range(2)]
            out_dt = F32 if final else BF16

            @blk.sync
            def _(sy):
                barrier_wait(sy)
                if plain:
                    for ic in range(NCH):
                        sy.dma_start(out=xs[:, ic * TPC:(ic + 1) * TPC],
                                     in_=x_src[ic]).then_inc(ld, 16)
                for oc in range(32):
                    if oc >= 2:
                        if mode == "qffn":
                            sy.wait_ge(wcst, oc - 1)
                        else:
                            sy.wait_ge(pev, 2 * oc - 2)
                    sy.dma_start(out=wbuf[oc % 2][:].bitcast(I32 if mode == "qffn" else F32),
                                 in_=w_src[oc]).then_inc(wld, 16)
                sy.sem_inc(pb, 1)

            ld_all = 16 * NCH if plain else 0

            @blk.tensor
            def _(t):
                barrier_wait(t)
                t.wait_ge(ld, ld_all)
                for oc in range(32):
                    if mode == "qffn":
                        t.wait_ge(wcst, oc + 1)
                        wv_ = wb16[oc % 2][:]
                    else:
                        t.wait_ge(wld, 16 * (oc + 1))
                        wv_ = wbuf[oc % 2][:].bitcast(BF16)[:, 1:2 * D:2]
                    for tb in range(2):
                        g = 2 * oc + tb
                        if g >= 4:
                            t.wait_ge(pev, g - 3)
                        bank = ps[g % 4]
                        mm = None
                        for ic in range(NCH):
                            mm = t.matmul(out=bank[:],
                                          lhsT=wv_[:, 128 * ic:128 * (ic + 1)],
                                          rhs=xs[:, ic * TPC + 512 * tb:
                                                 ic * TPC + 512 * tb + 512],
                                          start=(ic == 0), stop=(ic == NCH - 1))
                        c.inc(mm, mmd)
                t.wait_ge(pev, 64)
                for tb in range(2):
                    if tb == 1:
                        t.wait_ge(rowd, 2)
                    for ic in range(NCH):
                        t.matmul(out=ps_mu[0:1, :], lhsT=ones_b[:],
                                 rhs=xr[:, ic * TPC + 512 * tb:ic * TPC + 512 * tb + 512],
                                 start=(ic == 0), stop=(ic == NCH - 1))
                    for ic in range(NCH):
                        k = NCH * tb + ic
                        t.wait_ge(sqd, k + 1)
                        c.inc(t.matmul(out=ps_m2[0:1, :], lhsT=ones_b[:],
                                       rhs=sqv[ic % 2][:], start=(ic == 0),
                                       stop=(ic == NCH - 1)), sm2)
                    t.wait_ge(rowd, 2 * tb + 2)
                    t.matmul(out=ps[2 * tb][:], lhsT=ones_r[:], rhs=r1[:],
                             start=True, stop=True)
                    c.inc(t.matmul(out=ps[2 * tb + 1][:], lhsT=ones_r[:], rhs=r3[:],
                                   start=True, stop=True), bcd)
                t.sem_inc(pb, 1)

            @blk.scalar
            def _(sc):
                barrier_wait(sc)
                sc.wait_ge(pev, 64)
                for tb in range(2):
                    for ic in range(NCH):
                        k = NCH * tb + ic
                        if k >= 2:
                            sc.wait_ge(sm2, k - 1)
                        c.inc(sc.activation(out=sqv[ic % 2][:],
                                            in_=xr[:, ic * TPC + 512 * tb:
                                                   ic * TPC + 512 * tb + 512],
                                            func=AF.Square), sqd)
                    sc.wait_ge(rowv, tb + 1)
                    c.inc(sc.activation(out=r3[:], in_=r2[:], func=AF.Sqrt,
                                        bias=eps_s[0:1, :], scale=1.0), ssx)
                sc.sem_inc(pb, 1)

            @blk.vector
            def _(v):
                barrier_wait(v)
                if mode == "qffn":
                    for oc in range(2):
                        v.wait_ge(wld, 16 * (oc + 1))
                        c.inc(v.tensor_copy(out=wb16[oc % 2][:],
                                            in_=wbuf[oc % 2][:].bitcast(I32)), wcst)
                for oc in range(32):
                    for tb in range(2):
                        g = 2 * oc + tb
                        v.wait_ge(mmd, g + 1)
                        bank = ps[g % 4]
                        cols = slice(oc * TPC + 512 * tb, oc * TPC + 512 * tb + 512)
                        if mode == "qffn":
                            i1 = c.inc(v.tensor_scalar(
                                out=u_s[:], in0=bank[:], scalar1=vsc_s[:, oc:oc + 1],
                                scalar2=vbi_s[:, oc:oc + 1], op0=ALU.mult,
                                op1=ALU.add), svx)
                            v.wait_ge(svx, i1)
                            c.inc(v.tensor_tensor(out=xr[:, cols], in0=u_s[:],
                                                  in1=strm[:, cols], op=ALU.add), pev)
                        else:
                            c.inc(v.tensor_tensor(out=xr[:, cols], in0=bank[:],
                                                  in1=strm[:, cols], op=ALU.add), pev)
                    if mode == "qffn" and oc + 2 < 32:
                        v.wait_ge(wld, 16 * (oc + 3))
                        c.inc(v.tensor_copy(out=wb16[oc % 2][:],
                                            in_=wbuf[oc % 2][:].bitcast(I32)), wcst)
                for tb in range(2):
                    v.wait_ge(sm2, NCH * (tb + 1))
                    i1 = c.inc(v.tensor_scalar(out=r1[:], in0=ps_mu[0:1, :],
                                               scalar1=-1.0 / D, scalar2=None,
                                               op0=ALU.mult), svx)
                    i2 = c.inc(v.tensor_scalar(out=r2[:], in0=ps_m2[0:1, :],
                                               scalar1=1.0 / D, scalar2=None,
                                               op0=ALU.mult), svx)
                    v.wait_ge(svx, i2)
                    i3 = c.inc(v.tensor_tensor(out=r3[:], in0=r1[:], in1=r1[:],
                                               op=ALU.mult), svx)
                    v.wait_ge(svx, i3)
                    i4 = c.inc(v.scalar_tensor_tensor(out=r2[:], in0=r3[:],
                                                      scalar=-1.0, in1=r2[:],
                                                      op0=ALU.mult, op1=ALU.add), svx)
                    v.wait_ge(svx, i4)
                    c.inc(v.tensor_copy(out=r2[:], in_=r2[:]), rowv)
                    v.wait_ge(ssx, tb + 1)
                    i5 = c.inc(v.reciprocal(out=r3[:], in_=r3[:]), svx)
                    v.wait_ge(svx, i5)
                    c.inc(v.tensor_copy(out=r1[:], in_=r1[:]), rowd)
                    c.inc(v.tensor_copy(out=r3[:], in_=r3[:]), rowd)
                v.wait_ge(bcd, 2)
                for tb in range(2):
                    c.inc(v.tensor_copy(out=mub[tb][:], in_=ps[2 * tb][:]), svx)
                    jb = c.inc(v.tensor_copy(out=rsb[tb][:], in_=ps[2 * tb + 1][:]), svx)
                v.wait_ge(svx, jb)
                for ic in range(NCH):
                    if final and ic >= 2:
                        v.wait_ge(sto, 16 * (ic - 1))
                    for tb in range(2):
                        cols = slice(ic * TPC + 512 * tb, ic * TPC + 512 * tb + 512)
                        j1 = c.inc(v.tensor_tensor(out=u_s[:].bitcast(BF16)[:, 0:512],
                                                   in0=xr[:, cols],
                                                   in1=mub[tb][:], op=ALU.add), svx)
                        v.wait_ge(svx, j1)
                        lastv = v.tensor_tensor(
                            out=(uov[ic % 2][:, 512 * tb:512 * tb + 512] if final
                                 else strm[:, cols]),
                            in0=u_s[:].bitcast(BF16)[:, 0:512],
                            in1=rsb[tb][:], op=ALU.mult)
                        if tb == 1:
                            c.inc(lastv, appd)
                v.sem_inc(pb, 1)

            @blk.gpsimd
            def _(gp):
                barrier_wait(gp)
                if final:
                    for ic in range(NCH):
                        gp.wait_ge(appd, ic + 1)
                        gp.dma_start(out=out_dst[ic], in_=uov[ic % 2][:]).then_inc(sto, 16)
                    gp.wait_ge(sto, 16 * NCH)
                else:
                    gp.wait_ge(appd, NCH)
                gp.sem_inc(pb, 1)
            phase_n[0] += 1

        def emit_ffn1(x_src, out_dst):
            p = "f1"
            ld = c.sem(p + "ld")
            ldab = c.sem(p + "ldab")
            abct = c.sem(p + "abct")
            rt = c.sem(p + "rt")
            rv = c.sem(p + "rv")
            wld = c.sem(p + "wld")
            mmd = c.sem(p + "mmd")
            pev = c.sem(p + "pev")
            sqd = c.sem(p + "sqd")
            sm2 = c.sem(p + "sm2")
            rowv = c.sem(p + "rowv")
            ssx = c.sem(p + "ssx")
            rowd = c.sem(p + "rowd")
            bcd = c.sem(p + "bcd")
            appd = c.sem(p + "appd")
            sto = c.sem(p + "sto")
            svx = gsvx
            xs = S_XR
            xr = S_XS
            strm = S_XR
            wbuf = [S_W[:, 0:4096], S_W[:, 4096:8192]]
            a_bf = AB[:, 0:4096]
            b_bf = AB[:, 4096:8192]

            @blk.sync
            def _(sy):
                barrier_wait(sy)
                sy.dma_start(out=wbuf[0][:], in_=a_d).then_inc(ldab, 16)
                sy.dma_start(out=wbuf[1][:], in_=b_d).then_inc(ldab, 16)
                sy.wait_ge(abct, 2)
                for oc in range(32):
                    if oc >= 2:
                        sy.wait_ge(pev, 2 * oc - 2)
                    sy.dma_start(out=wbuf[oc % 2][:], in_=wb_d[oc]).then_inc(wld, 16)
                sy.sem_inc(pb, 1)

            @blk.tensor
            def _(t):
                barrier_wait(t)
                t.wait_ge(abct, 2)
                for tt in range(8):
                    bank = [ps_a, ps_b][tt % 2]
                    if tt >= 2:
                        t.wait_ge(rv, tt - 1)
                    mm = None
                    for ic in range(NCH):
                        mm = t.matmul(out=bank[:, 0:8],
                                      lhsT=xs[:, ic * TPC + 128 * tt:ic * TPC + 128 * (tt + 1)],
                                      rhs=wr_s[:].bitcast(BF16)[:, 1:512:2][:, 8 * ic:8 * (ic + 1)],
                                      start=(ic == 0), stop=(ic == NCH - 1))
                    c.inc(mm, rt)
                for tt in range(8):
                    t.wait_ge(rv, 8 + tt + 1)
                    bank = [ps_a, ps_b][tt % 2]
                    if tt >= 2:
                        t.wait_ge(rv, 16 + tt - 1)
                    c.inc(t.transpose(out=bank[0:8, 0:128],
                                      in_=wts_s[:, 8 * tt:8 * (tt + 1)],
                                      identity=idf_s[:]), rt)
                for tb in range(2):
                    t.wait_ge(rv, 24)
                    bank = [ps_a, ps_b][tb]
                    c.inc(t.matmul(out=bank[:], lhsT=d2_s2[:],
                                   rhs=wt_s[0:8, 512 * tb:512 * (tb + 1)],
                                   start=True, stop=True), rt)
                for tb in range(2):
                    t.wait_ge(rv, 24 + tb + 1)
                    bank = [ps_a, ps_b][tb]
                    mm = None
                    for ic in range(NCH):
                        mm = t.matmul(out=bank[:],
                                      lhsT=a_bf[:, 128 * ic:128 * (ic + 1)],
                                      rhs=xs[:, ic * TPC + 512 * tb:ic * TPC + 512 * tb + 512],
                                      start=(ic == 0), stop=(ic == NCH - 1))
                    c.inc(mm, rt)
                for oc in range(32):
                    t.wait_ge(wld, 16 * (oc + 1))
                    wv_ = wbuf[oc % 2][:].bitcast(BF16)[:, 1:2 * D:2]
                    for tb in range(2):
                        g = 2 * oc + tb
                        if g >= 4:
                            t.wait_ge(pev, g - 3)
                        if oc == 0:
                            t.wait_ge(rv, 26 + tb + 1)
                        bank = ps[g % 4]
                        for ic in range(NCH):
                            t.matmul(out=bank[:],
                                     lhsT=wv_[:, 128 * ic:128 * (ic + 1)],
                                     rhs=xs[:, ic * TPC + 512 * tb:
                                            ic * TPC + 512 * tb + 512],
                                     start=(ic == 0), stop=False)
                        c.inc(t.matmul(out=bank[:],
                                       lhsT=b_bf[:, 128 * oc:128 * (oc + 1)],
                                       rhs=g_s[tb][:], start=False, stop=True), mmd)
                t.wait_ge(pev, 64)
                for tb in range(2):
                    if tb == 1:
                        t.wait_ge(rowd, 2)
                    for ic in range(NCH):
                        t.matmul(out=ps_mu[0:1, :], lhsT=ones_b[:],
                                 rhs=xr[:, ic * TPC + 512 * tb:ic * TPC + 512 * tb + 512],
                                 start=(ic == 0), stop=(ic == NCH - 1))
                    for ic in range(NCH):
                        k = NCH * tb + ic
                        t.wait_ge(sqd, k + 1)
                        c.inc(t.matmul(out=ps_m2[0:1, :], lhsT=ones_b[:],
                                       rhs=sq_w[ic % 2][:], start=(ic == 0),
                                       stop=(ic == NCH - 1)), sm2)
                    t.wait_ge(rowd, 2 * tb + 2)
                    t.matmul(out=ps[2 * tb][:], lhsT=ones_r[:], rhs=r1[:],
                             start=True, stop=True)
                    c.inc(t.matmul(out=ps[2 * tb + 1][:], lhsT=ones_r[:], rhs=r3[:],
                                   start=True, stop=True), bcd)
                t.sem_inc(pb, 1)

            @blk.vector
            def _(v):
                barrier_wait(v)
                v.wait_ge(ldab, 32)
                c.inc(v.tensor_copy(out=a_bf[:], in_=wbuf[0][:]), abct)
                c.inc(v.tensor_copy(out=b_bf[:], in_=wbuf[1][:]), abct)
                for tt in range(8):
                    v.wait_ge(rt, tt + 1)
                    bank = [ps_a, ps_b][tt % 2]
                    ecols = slice(8 * tt, 8 * (tt + 1))
                    i1 = c.inc(v.tensor_reduce(out=m1_s[:], in_=bank[:, 0:8],
                                               axis=AX.X, op=ALU.max), svx)
                    v.wait_ge(svx, i1)
                    i2 = c.inc(v.tensor_scalar(out=mneg_s[:], in0=m1_s[:],
                                               scalar1=-1.0, scalar2=None,
                                               op0=ALU.mult), svx)
                    v.wait_ge(svx, i2)
                    i3 = c.inc(v.tensor_scalar(out=lgt_s[:, ecols], in0=bank[:, 0:8],
                                               scalar1=mneg_s[:, :], scalar2=None,
                                               op0=ALU.add), svx)
                    v.wait_ge(svx, i3)
                    c.inc(v.tensor_copy(out=lgt_s[:, ecols], in_=lgt_s[:, ecols]), rv)
                for tt in range(8):
                    v.wait_ge(ssx, tt + 1)
                    ecols = slice(8 * tt, 8 * (tt + 1))
                    i1 = c.inc(v.tensor_reduce(out=m1_s[:], in_=e_s[:, ecols],
                                               axis=AX.X, op=ALU.max), svx)
                    v.wait_ge(svx, i1)
                    i2 = c.inc(v.tensor_scalar(out=msk_s[:], in0=e_s[:, ecols],
                                               scalar1=m1_s[:, :], scalar2=None,
                                               op0=ALU.is_ge), svx)
                    v.wait_ge(svx, i2)
                    i3 = c.inc(v.scalar_tensor_tensor(out=wts_s[:, ecols], in0=msk_s[:],
                                                      scalar=-1000.0, in1=e_s[:, ecols],
                                                      op0=ALU.mult, op1=ALU.add), svx)
                    v.wait_ge(svx, i3)
                    i4 = c.inc(v.tensor_reduce(out=m2_s[:], in_=wts_s[:, ecols],
                                               axis=AX.X, op=ALU.max), svx)
                    v.wait_ge(svx, i4)
                    i5 = c.inc(v.tensor_tensor(out=rcpr_s[:], in0=m1_s[:], in1=m2_s[:],
                                               op=ALU.add), svx)
                    v.wait_ge(svx, i5)
                    i6 = c.inc(v.reciprocal(out=rcpr_s[:], in_=rcpr_s[:]), svx)
                    i7 = c.inc(v.tensor_scalar(out=msk_s[:], in0=e_s[:, ecols],
                                               scalar1=m2_s[:, :], scalar2=None,
                                               op0=ALU.is_ge), svx)
                    v.wait_ge(svx, i7)
                    i8 = c.inc(v.tensor_tensor(out=msk_s[:], in0=msk_s[:],
                                               in1=e_s[:, ecols], op=ALU.mult), svx)
                    v.wait_ge(svx, max(i6, i8))
                    c.inc(v.tensor_scalar(out=wts_s[:, ecols], in0=msk_s[:],
                                          scalar1=rcpr_s[:, :], scalar2=None,
                                          op0=ALU.mult), rv)
                for tt in range(8):
                    v.wait_ge(rt, 8 + tt + 1)
                    bank = [ps_a, ps_b][tt % 2]
                    c.inc(v.tensor_copy(out=wt_s[0:8, 128 * tt:128 * (tt + 1)],
                                        in_=bank[0:8, 0:128]), rv)
                for tb in range(2):
                    v.wait_ge(rt, 16 + tb + 1)
                    bank = [ps_a, ps_b][tb]
                    c.inc(v.tensor_copy(out=wb_s[:, 512 * tb:512 * (tb + 1)],
                                        in_=bank[:]), rv)
                for tb in range(2):
                    v.wait_ge(rt, 18 + tb + 1)
                    v.wait_ge(rv, 26)
                    bank = [ps_a, ps_b][tb]
                    c.inc(v.tensor_tensor(out=g_s[tb][:], in0=bank[:],
                                          in1=wb_s[:, 512 * tb:512 * (tb + 1)],
                                          op=ALU.mult), rv)
                for oc in range(32):
                    for tb in range(2):
                        g = 2 * oc + tb
                        v.wait_ge(mmd, g + 1)
                        bank = ps[g % 4]
                        cols = slice(oc * TPC + 512 * tb, oc * TPC + 512 * tb + 512)
                        i1 = c.inc(v.tensor_scalar(
                            out=u_s[:], in0=bank[:], scalar1=fb_s[:, oc:oc + 1],
                            scalar2=None, op0=ALU.add), svx)
                        v.wait_ge(svx, i1)
                        c.inc(v.tensor_tensor(out=xr[:, cols], in0=u_s[:],
                                              in1=strm[:, cols], op=ALU.add), pev)
                for tb in range(2):
                    v.wait_ge(sm2, NCH * (tb + 1))
                    i1 = c.inc(v.tensor_scalar(out=r1[:], in0=ps_mu[0:1, :],
                                               scalar1=-1.0 / D, scalar2=None,
                                               op0=ALU.mult), svx)
                    i2 = c.inc(v.tensor_scalar(out=r2[:], in0=ps_m2[0:1, :],
                                               scalar1=1.0 / D, scalar2=None,
                                               op0=ALU.mult), svx)
                    v.wait_ge(svx, i2)
                    i3 = c.inc(v.tensor_tensor(out=r3[:], in0=r1[:], in1=r1[:],
                                               op=ALU.mult), svx)
                    v.wait_ge(svx, i3)
                    i4 = c.inc(v.scalar_tensor_tensor(out=r2[:], in0=r3[:],
                                                      scalar=-1.0, in1=r2[:],
                                                      op0=ALU.mult, op1=ALU.add), svx)
                    v.wait_ge(svx, i4)
                    c.inc(v.tensor_copy(out=r2[:], in_=r2[:]), rowv)
                    v.wait_ge(ssx, 8 + tb + 1)
                    i5 = c.inc(v.reciprocal(out=r3[:], in_=r3[:]), svx)
                    v.wait_ge(svx, i5)
                    c.inc(v.tensor_copy(out=r1[:], in_=r1[:]), rowd)
                    c.inc(v.tensor_copy(out=r3[:], in_=r3[:]), rowd)
                v.wait_ge(bcd, 2)
                for ic in range(NCH):
                    if ic >= 2:
                        v.wait_ge(sto, 16 * (ic - 1))
                    for tb in range(2):
                        cols = slice(ic * TPC + 512 * tb, ic * TPC + 512 * tb + 512)
                        j1 = c.inc(v.tensor_tensor(out=u_s[:], in0=xr[:, cols],
                                                   in1=ps[2 * tb][:], op=ALU.add), svx)
                        v.wait_ge(svx, j1)
                        lastv = v.tensor_tensor(out=uo_w[ic % 2][:, 512 * tb:512 * tb + 512],
                                                in0=u_s[:], in1=ps[2 * tb + 1][:],
                                                op=ALU.mult)
                        if tb == 1:
                            c.inc(lastv, appd)
                v.sem_inc(pb, 1)

            @blk.scalar
            def _(sc):
                barrier_wait(sc)
                for tt in range(8):
                    sc.wait_ge(rv, tt + 1)
                    ecols = slice(8 * tt, 8 * (tt + 1))
                    c.inc(sc.activation(out=e_s[:, ecols], in_=lgt_s[:, ecols],
                                        func=AF.Exp), ssx)
                sc.wait_ge(pev, 64)
                for tb in range(2):
                    for ic in range(NCH):
                        k = NCH * tb + ic
                        if k >= 2:
                            sc.wait_ge(sm2, k - 1)
                        c.inc(sc.activation(out=sq_w[ic % 2][:],
                                            in_=xr[:, ic * TPC + 512 * tb:
                                                   ic * TPC + 512 * tb + 512],
                                            func=AF.Square), sqd)
                    sc.wait_ge(rowv, tb + 1)
                    c.inc(sc.activation(out=r3[:], in_=r2[:], func=AF.Sqrt,
                                        bias=eps_s[0:1, :], scale=1.0), ssx)
                sc.sem_inc(pb, 1)

            @blk.gpsimd
            def _(gp):
                barrier_wait(gp)
                for ic in range(NCH):
                    gp.wait_ge(appd, ic + 1)
                    gp.dma_start(out=out_dst[ic], in_=uo_w[ic % 2][:]).then_inc(sto, 16)
                gp.wait_ge(sto, 16 * NCH)
            phase_n[0] += 1

        emit_const_loads()
        emit_qkv(0)
        emit_attn(0)
        emit_projln("o0", "plain", at_h[0], xs0_h, w_l0["wo0"], xh[1])
        emit_projln("qf", "qffn", xh[1], xh[1], qw_d, xh[2])
        emit_qkv(1)
        emit_attn(1)
        emit_projln("o1", "plain", at_h[1], xh[2], w_l1["wo1"], xh[3])
        emit_ffn1(xh[3], x_o)

    return nc


def kernel_fused(L0_attn_x, L0_attn_Wq, L0_attn_Wk, L0_attn_Wv, L0_attn_Wo,
                 L0_ffn_quantized_weights, L0_ffn_scales, L0_ffn_bias,
                 L1_attn_wq, L1_attn_wk, L1_attn_wv, L1_attn_wo,
                 L1_ffn_W_base, L1_ffn_b_base, L1_ffn_W_router,
                 L1_ffn_lora_A, L1_ffn_lora_B):
    cores = list(range(NCORES))
    toks = [core_tokens(j) for j in cores]
    ident_bf = np.ascontiguousarray(np.eye(128, dtype=np.float32).astype(bfloat16))
    ident_f = np.ascontiguousarray(np.eye(128, dtype=np.float32))
    ones_bm = np.ones((128, 128), bfloat16)
    ones_col = np.ones((128, 1), bfloat16)
    ones_row = np.ones((1, 128), np.float32)
    epsv = np.full((1, 1), LN_EPS, np.float32)
    dpat = diag_patterns()
    cos0, sin0 = rope_tables_l0(S)
    cos1, sin1 = rope_tables_l1(S)
    rope0 = [rope_fm(cos0, sin0, toks[j]) for j in cores]
    rope1 = [rope_fm(cos1, sin1, toks[j]) for j in cores]

    wq0 = colchunk(np.ascontiguousarray(L0_attn_Wq.T))
    wk0 = colchunk(np.ascontiguousarray(L0_attn_Wk.T))
    wv0 = colchunk(np.ascontiguousarray(L0_attn_Wv.T))
    wo0 = colchunk(np.ascontiguousarray(L0_attn_Wo.T))
    qw0 = colchunk(np.ascontiguousarray(L0_ffn_quantized_weights.T))
    sc0 = np.ascontiguousarray(L0_ffn_scales.reshape(32, 128).T)
    bi0 = np.ascontiguousarray(L0_ffn_bias.reshape(32, 128).T)
    wq1 = colchunk(np.ascontiguousarray(L1_attn_wq))
    wk1 = colchunk(np.ascontiguousarray(L1_attn_wk))
    wv1 = colchunk(np.ascontiguousarray(L1_attn_wv))
    wo1 = colchunk(np.ascontiguousarray(L1_attn_wo))
    wb1 = colchunk(np.ascontiguousarray(L1_ffn_W_base.T))
    fb1 = np.ascontiguousarray(L1_ffn_b_base.reshape(32, 128).T)
    wr1 = np.ascontiguousarray(
        L1_ffn_W_router.reshape(8, 32, 128).transpose(2, 1, 0).reshape(128, 256))
    la1 = np.ascontiguousarray(
        L1_ffn_lora_A.reshape(128, D).T.reshape(32, 128, 128)
        .transpose(1, 0, 2).reshape(128, 4096))
    lb1 = np.ascontiguousarray(L1_ffn_lora_B.transpose(0, 2, 1).reshape(128, D))
    d2 = np.zeros((8, 128), np.float32)
    for e in range(8):
        d2[e, 16 * e:16 * (e + 1)] = 2.0
    x_fm = [np.ascontiguousarray(L0_attn_x[j // 4].T[:, toks[j]]) for j in cores]

    nc = _prog("fused", build_fused)
    maps = []
    for j in cores:
        maps.append({
            "x": _stack(x_fm[j]), "wq0": wq0, "wk0": wk0, "wv0": wv0, "wo0": wo0,
            "qw": qw0, "scales": sc0, "qbias": bi0,
            "wq1": wq1, "wk1": wk1, "wv1": wv1, "wo1": wo1,
            "wbase": wb1, "fbias": fb1, "wr": wr1, "la": la1, "lb": lb1, "d2": d2,
            "cos0": rope0[j][0], "sin0": rope0[j][1],
            "cos1": rope1[j][0], "sin1": rope1[j][1],
            "ident": ident_bf, "idf": ident_f, "onesb": ones_col,
            "onesbm": ones_bm, "onesr": ones_row, "epsv": epsv,
            "dpat": dpat, "ebias": fused_ebias(j),
        })
    res = run_bass_kernel_spmd(nc, maps, core_ids=cores)
    out = np.empty((B, S, D), np.float32)
    for j in cores:
        b = j // 4
        out[b, toks[j], :] = res.results[j]["xout"].reshape(D, TPC).T
    return out


def kernel(**inputs):
    return kernel_fused(**inputs)
